# version 10
# baseline (speedup 1.0000x reference)
import contextlib
import numpy as np
import ml_dtypes
import concourse.bass as bass
import concourse.mybir as mybir
from concourse.bass_utils import run_bass_kernel_spmd
from concourse.alu_op_type import AluOpType as OP

F32 = mybir.dt.float32
BF16 = mybir.dt.bfloat16
AF = mybir.ActivationFunctionType
AX = mybir.AxisListType

NCORE = 8
TLOC = 2048
D = 1024
DIN = 2048
DST = 16
DTR = 64
NEG = -1.0e30

BF = ml_dtypes.bfloat16


def _ap_name(x):
    try:
        return x.tensor.name
    except AttributeError:
        return None


def _is_ap(x):
    return hasattr(x, "tensor")


class Prog:

    def __init__(self, same_engine_edges=False):
        self.ops = []
        self.const_tile = None
        self.psum_names = set()
        self.same_engine_edges = same_engine_edges

    def _rec(self, eng, method, args, kwargs, dma, reads, writes):
        for a in list(reads) + list(writes):
            if not isinstance(a, str) and hasattr(a, "space"):
                try:
                    if "PSUM" in str(a.space):
                        self.psum_names.add(a.tensor.name)
                except Exception:
                    pass
        rd = set(a if isinstance(a, str) else _ap_name(a) for a in reads)
        wr = set(a if isinstance(a, str) else _ap_name(a) for a in writes)
        rd.discard(None); wr.discard(None)
        self.ops.append((eng, method, args, kwargs, dma, rd, wr))

    def _const_read(self, reads, scalars):
        if self.const_tile is not None and any(
                isinstance(s, float) for s in scalars):
            reads = list(reads) + [self.const_tile]
        return reads

    def dma(self, eng, out, in_, **kw):
        self._rec(eng, "dma_start", (), {"out": out, "in_": in_, **kw},
                  True, [in_], [out])

    def mm(self, out, lhsT, rhs, **kw):
        self._rec("tensor", "matmul", (out, lhsT, rhs), kw, False,
                  [lhsT, rhs, out], [out])

    def transpose(self, out, in_, ident):
        self._rec("tensor", "transpose", (out, in_, ident), {}, False,
                  [in_, ident], [out])

    def act(self, out, in_, func, bias=None, scale=None, accum_out=None):
        kw = {}
        reads = [in_]
        writes = [out]
        if bias is not None:
            kw["bias"] = bias
            (reads if _is_ap(bias) else []).append(bias)
        if scale is not None:
            kw["scale"] = scale
            (reads if _is_ap(scale) else []).append(scale)
        if accum_out is not None:
            kw["accum_out"] = accum_out
            writes.append(accum_out)
        if self.const_tile is not None:
            reads = list(reads) + [self.const_tile]
        self._rec("scalar", "activation", (out, in_, func), kw, False,
                  reads, writes)

    def tt(self, out, a, b, op, eng="vector"):
        self._rec(eng, "tensor_tensor", (out, a, b), {"op": op}, False,
                  [a, b], [out])

    def ts(self, out, in0, s1, s2=None, op0=OP.mult, op1=None, eng="vector",
           accum_out=None):
        reads = [in0] + [s for s in (s1, s2) if _is_ap(s)]
        reads = self._const_read(reads, [s for s in (s1, s2)
                                         if not _is_ap(s) and s is not None])
        writes = [out] + ([accum_out] if accum_out is not None else [])
        if op1 is None:
            self._rec(eng, "tensor_scalar", (out, in0, s1, None, op0),
                      {"accum_out": accum_out}, False, reads, writes)
        else:
            self._rec(eng, "tensor_scalar", (out, in0, s1, s2, op0, op1),
                      {"accum_out": accum_out}, False, reads, writes)

    def stt(self, out, in0, scalar, in1, op0, op1, eng="vector"):
        reads = [in0, in1] + ([scalar] if _is_ap(scalar) else [])
        reads = self._const_read(reads, [scalar] if not _is_ap(scalar) else [])
        self._rec(eng, "scalar_tensor_tensor", (out, in0, scalar, in1, op0, op1),
                  {}, False, reads, writes=[out])

    def copy(self, out, in_, eng="vector"):
        self._rec(eng, "tensor_copy", (out, in_), {}, False, [in_], [out])

    def ttr(self, out, in0, in1, scalar, op0, op1, accum_out):
        reads = [in0, in1]
        reads = self._const_read(reads, [scalar] if not _is_ap(scalar) else [])
        self._rec("vector", "tensor_tensor_reduce", (out, in0, in1, 1.0, scalar,
                  op0, op1, accum_out), {}, False, reads, [out, accum_out])

    def red(self, out, in_, op, negate=None):
        self._rec("vector", "tensor_reduce", (out, in_, AX.X, op),
                  {"negate": negate}, False, [in_], [out])

    def recip(self, out, in_):
        self._rec("vector", "reciprocal", (out, in_), {}, False, [in_], [out])

    def scan(self, out, a, u, initial):
        reads = [a, u] + ([initial] if _is_ap(initial) else [])
        self._rec("vector", "tensor_tensor_scan",
                  (out, a, u, initial, OP.mult, OP.add), {}, False,
                  reads, [out])

    def memset(self, ap, val, eng="vector"):
        self._rec(eng, "memset", (ap, val), {}, False, [], [ap])

    TRUNC = None

    def emit(self, block, sems):
        if Prog.TRUNC is not None:
            self.ops = self.ops[:Prog.TRUNC]
        n = len(self.ops)

        last_writer = {}
        readers = {}
        deps = [set() for _ in range(n)]
        psum = self.psum_names
        for i, (eng, _m, _a, _k, dma, rd, wr) in enumerate(self.ops):
            for t in rd:
                if t in last_writer:
                    deps[i].add(last_writer[t])
            for t in wr:
                for r in readers.get(t, ()):
                    deps[i].add(r)
                if t in last_writer:
                    deps[i].add(last_writer[t])
            for t in wr:
                last_writer[t] = i
                readers[t] = []
            for t in rd:
                readers.setdefault(t, []).append(i)
                if t in psum:
                    last_writer[t] = i
                    readers[t] = []
        prev_dma = {}
        for i, (eng, _m, _a, _k, dma, _r, _w) in enumerate(self.ops):
            if dma:
                if eng in prev_dma:
                    deps[i].add(prev_dma[eng])
                prev_dma[eng] = i

        kept = [[] for _ in range(n)]
        need_inc = [False] * n
        for i, (eng, _m, _a, _k, dma, _r, _w) in enumerate(self.ops):
            if dma:
                need_inc[i] = True
            for j in deps[i]:
                jeng, jdma = self.ops[j][0], self.ops[j][4]
                if (not jdma) and jeng == eng and not (
                        self.same_engine_edges and eng != "tensor"):
                    continue
                kept[i].append(j)
                need_inc[j] = True

        sem_key = [None] * n
        cumval = [0] * n
        counters = {}
        for i, (eng, _m, _a, _k, dma, _r, _w) in enumerate(self.ops):
            key = ("dma_" + eng) if dma else eng
            sem_key[i] = key
            if need_inc[i]:
                counters[key] = counters.get(key, 0) + (16 if dma else 1)
                cumval[i] = counters[key]
        totals = dict(counters)

        reqs = [dict() for _ in range(n)]
        for i in range(n):
            for j in kept[i]:
                k, v = sem_key[j], cumval[j]
                if reqs[i].get(k, 0) < v:
                    reqs[i][k] = v

        ops = self.ops

        def engine_body(name):
            def body(eng_obj):
                guaranteed = {}
                for i, (ename, method, args, kwargs, dma, _r, _w) in enumerate(ops):
                    if ename != name:
                        continue
                    for k, v in sorted(reqs[i].items()):
                        if guaranteed.get(k, 0) >= v:
                            continue
                        eng_obj.wait_ge(sems[k], v)
                        guaranteed[k] = v
                    kwargs2 = {k: v for k, v in kwargs.items() if v is not None} \
                        if method in ("activation", "tensor_scalar") else kwargs
                    inst = getattr(eng_obj, method)(*args, **kwargs2)
                    if need_inc[i]:
                        inst.then_inc(sems[sem_key[i]], 16 if dma else 1)
                for k, v in totals.items():
                    eng_obj.wait_ge(sems[k], v)
            return body

        block.sync(engine_body("sync"))
        block.gpsimd(engine_body("gpsimd"))
        block.tensor(engine_body("tensor"))
        block.vector(engine_body("vector"))
        block.scalar(engine_body("scalar"))


def make_sems(nc, E):
    return {k: E(nc.semaphore(f"sem_{k}")) for k in
            ["tensor", "vector", "scalar", "gpsimd", "sync",
             "dma_sync", "dma_gpsimd", "dma_scalar"]}


def col_chunks(cols, nch=512):
    return [(c, min(nch, cols - c)) for c in range(0, cols, nch)]


class Consts:

    def __init__(self, nc, p, tile, dtype=F32):
        self.nc, self.p, self.tile, self.dtype = nc, p, tile, dtype
        self.n = 0
        self.seen = {}
        p.const_tile = tile[:].tensor.name

    def reg(self, *vals):
        for v in vals:
            v = float(v)
            if v in self.seen:
                continue
            ap = self.tile[:, self.n:self.n + 1]
            self.nc.const_aps.register(self.dtype, v, ap)
            self.p.memset(ap, v)
            self.seen[v] = ap
            self.n += 1


HALO1 = 128
XCOLS = HALO1 + TLOC


def build_mamba_pre():
    nc = bass.Bass("TRN2", target_bir_lowering=False, detect_race_conditions=False)
    xh = nc.dram_tensor("xh", [D, XCOLS], F32, kind="ExternalInput").ap()
    rms_w = nc.dram_tensor("rms_w", [D], F32, kind="ExternalInput").ap()
    in_proj = nc.dram_tensor("in_proj", [D, 2 * DIN], F32, kind="ExternalInput").ap()
    conv_w = nc.dram_tensor("conv_w", [DIN, 4], F32, kind="ExternalInput").ap()
    conv_b = nc.dram_tensor("conv_b", [DIN], F32, kind="ExternalInput").ap()
    x_proj = nc.dram_tensor("x_proj", [DIN, 96], F32, kind="ExternalInput").ap()
    dt_w = nc.dram_tensor("dt_w", [DTR, DIN], F32, kind="ExternalInput").ap()
    dt_b = nc.dram_tensor("dt_b", [DIN], F32, kind="ExternalInput").ap()

    xn_f = nc.dram_tensor("xn_f", [D, TLOC], F32, kind="ExternalOutput").ap()
    z_f = nc.dram_tensor("z_f", [DIN, TLOC], F32, kind="ExternalOutput").ap()
    xi_f = nc.dram_tensor("xi_f", [DIN, TLOC], F32, kind="ExternalOutput").ap()
    delta_f = nc.dram_tensor("delta_f", [DIN, TLOC], F32, kind="ExternalOutput").ap()
    bT = nc.dram_tensor("bT", [DST, TLOC], F32, kind="ExternalOutput").ap()
    cT = nc.dram_tensor("cT", [DST, TLOC], F32, kind="ExternalOutput").ap()

    with contextlib.ExitStack() as st:
        E = st.enter_context
        xc = [E(nc.sbuf_tensor(f"xc{i}", [128, XCOLS], F32)) for i in range(8)]
        xmb = [E(nc.sbuf_tensor(f"xmb{i}", [128, XCOLS], BF16)) for i in range(8)]
        sq = E(nc.sbuf_tensor("sq", [128, 512], F32))
        rows = E(nc.sbuf_tensor("rows", [1, XCOLS], F32))
        row2 = E(nc.sbuf_tensor("row2", [1, XCOLS], F32))
        crow = E(nc.sbuf_tensor("crow", [1, XCOLS], F32))
        ones1 = E(nc.sbuf_tensor("ones1", [1, 128], F32))
        ones1c = E(nc.sbuf_tensor("ones1c", [128, 1], F32))
        bc1 = E(nc.sbuf_tensor("bc1", [128, XCOLS], F32))
        bc2 = E(nc.sbuf_tensor("bc2", [128, XCOLS], F32))
        rmsw_sb = E(nc.sbuf_tensor("rmsw_sb", [128, 8], F32))
        wstage = [E(nc.sbuf_tensor(f"wstage{i}", [128, 1024], BF16)) for i in range(2)]
        cw_sb = E(nc.sbuf_tensor("cw_sb", [128, 64], F32))
        cb_sb = E(nc.sbuf_tensor("cb_sb", [128, 16], F32))
        dtb_sb = E(nc.sbuf_tensor("dtb_sb", [128, 16], F32))
        xp_sb = E(nc.sbuf_tensor("xp_sb", [128, 16 * 96], BF16))
        dtw_sb = E(nc.sbuf_tensor("dtw_sb", [64, DIN], BF16))
        xi_raw = E(nc.sbuf_tensor("xi_raw", [128, XCOLS], F32))
        xi_acc = E(nc.sbuf_tensor("xi_acc", [128, TLOC], F32))
        xib = E(nc.sbuf_tensor("xib", [128, TLOC], BF16))
        dbc_sb = E(nc.sbuf_tensor("dbc_sb", [96, TLOC], F32))
        dtlb = E(nc.sbuf_tensor("dtlb", [64, TLOC], BF16))
        sp1 = E(nc.sbuf_tensor("sp1", [128, TLOC], F32))
        sp2 = E(nc.sbuf_tensor("sp2", [128, TLOC], F32))
        ps_a_r = [E(nc.psum_tensor(f"ps_a{i}", [128, 512], F32)) for i in range(2)]
        ps_r1 = E(nc.psum_tensor("ps_r1", [1, 512], F32))
        ps_r2 = ps_r1
        ps_bc = E(nc.psum_tensor("ps_bc", [128, 512], F32))
        ps_dbc = E(nc.psum_tensor("ps_dbc", [96, TLOC], F32))
        cpool = E(nc.sbuf_tensor("cpool", [128, 8], F32))
        sems = make_sems(nc, E)
        block = E(nc.Block())

        p = Prog()
        cst = Consts(nc, p, cpool)
        cst.reg(1e-5, 1.0 / D, -0.5, -1.0, 1.0, 0.0)
        for i in range(8):
            p.dma("sync", xc[i][:], xh[128 * i:128 * (i + 1), :])
        p.dma("sync", rmsw_sb[:], rms_w.rearrange("(i p) -> p i", p=128),
              allow_slow_non_contiguous=True)
        p.dma("sync", cw_sb[:].rearrange("p (i k) -> p i k", k=4),
              conv_w.rearrange("(i p) k -> p i k", p=128))
        p.dma("sync", cb_sb[:], conv_b.rearrange("(i p) -> p i", p=128),
              allow_slow_non_contiguous=True)
        p.dma("sync", dtb_sb[:], dt_b.rearrange("(i p) -> p i", p=128),
              allow_slow_non_contiguous=True)
        p.dma("gpsimd", xp_sb[:].rearrange("p (i j) -> p i j", j=96),
              x_proj.rearrange("(i p) j -> p i j", p=128))
        p.dma("gpsimd", dtw_sb[:], dt_w)
        p.memset(ones1[:], 1.0)
        p.memset(ones1c[:], 1.0)

        for c0, cw in col_chunks(XCOLS):
            for i in range(8):
                p.mm(ps_r1[:, 0:cw], ones1c[:], xc[i][:, c0:c0 + cw],
                     start=(i == 0), stop=(i == 7))
            p.copy(rows[:, c0:c0 + cw], ps_r1[:, 0:cw])
            for i in range(8):
                p.act(sq[:, 0:cw], xc[i][:, c0:c0 + cw], AF.Square)
                p.mm(ps_r2[:, 0:cw], ones1c[:], sq[:, 0:cw],
                     start=(i == 0), stop=(i == 7))
            p.copy(row2[:, c0:c0 + cw], ps_r2[:, 0:cw])
        p.ts(rows[:], rows[:], 1.0 / D, op0=OP.mult)
        p.act(crow[:], rows[:], AF.Square)
        p.ts(row2[:], row2[:], 1.0 / D, op0=OP.mult)
        p.tt(crow[:], row2[:], crow[:], OP.subtract)
        p.act(crow[:], crow[:], AF.Ln, bias=1e-5)
        p.act(crow[:], crow[:], AF.Exp, scale=-0.5)
        for c0, cw in col_chunks(XCOLS):
            p.mm(ps_bc[:, 0:cw], ones1[:], crow[:, c0:c0 + cw], start=True, stop=True)
            p.copy(bc1[:, c0:c0 + cw], ps_bc[:, 0:cw])
        p.tt(crow[:], rows[:], crow[:], OP.mult)
        p.ts(crow[:], crow[:], -1.0, op0=OP.mult)
        for c0, cw in col_chunks(XCOLS):
            p.mm(ps_bc[:, 0:cw], ones1[:], crow[:, c0:c0 + cw], start=True, stop=True)
            p.copy(bc2[:, c0:c0 + cw], ps_bc[:, 0:cw])
        for i in range(8):
            p.tt(xc[i][:], xc[i][:], bc1[:], OP.mult)
            p.tt(xc[i][:], xc[i][:], bc2[:], OP.add)
            p.dma("sync", xn_f[128 * i:128 * (i + 1), :], xc[i][:, HALO1:XCOLS])

        for c0, cw in col_chunks(XCOLS):
            for i in range(8):
                p.act(sq[:, 0:cw], xc[i][:, c0:c0 + cw], AF.Square)
                p.mm(ps_r1[:, 0:cw], ones1c[:], sq[:, 0:cw],
                     start=(i == 0), stop=(i == 7))
            p.copy(rows[:, c0:c0 + cw], ps_r1[:, 0:cw])
        p.act(rows[:], rows[:], AF.Ln, scale=1.0 / D, bias=1e-5)
        p.act(rows[:], rows[:], AF.Exp, scale=-0.5)
        for c0, cw in col_chunks(XCOLS):
            p.mm(ps_bc[:, 0:cw], ones1[:], rows[:, c0:c0 + cw], start=True, stop=True)
            p.copy(bc1[:, c0:c0 + cw], ps_bc[:, 0:cw])
        for i in range(8):
            p.stt(xmb[i][:], xc[i][:], rmsw_sb[:, i:i + 1], bc1[:], OP.mult, OP.mult)

        for m in range(16):
            w = wstage[m % 2]
            p.dma("gpsimd", w[:],
                  in_proj[:, 128 * m:128 * (m + 1)].rearrange("(k p) m -> p k m", p=128))
            for ci, (c0, cw) in enumerate(col_chunks(XCOLS)):
                ps = ps_a_r[ci % 2]
                for k in range(8):
                    p.mm(ps[:, 0:cw], w[:, 128 * k:128 * (k + 1)],
                         xmb[k][:, c0:c0 + cw], start=(k == 0), stop=(k == 7))
                p.act(xi_raw[:, c0:c0 + cw], ps[:, 0:cw], AF.Copy)
            p.ts(xi_acc[:], xi_raw[:, HALO1 - 3:HALO1 - 3 + TLOC],
                 cw_sb[:, 4 * m:4 * m + 1], op0=OP.mult)
            for kk in range(1, 4):
                p.stt(xi_acc[:], xi_raw[:, HALO1 - 3 + kk:HALO1 - 3 + kk + TLOC],
                      cw_sb[:, 4 * m + kk:4 * m + kk + 1], xi_acc[:],
                      OP.mult, OP.add)
            p.act(xi_acc[:], xi_acc[:], AF.Silu, bias=cb_sb[:, m:m + 1])
            p.dma("sync", xi_f[128 * m:128 * (m + 1), :], xi_acc[:])
            p.copy(xib[:], xi_acc[:])
            for c0, cw in col_chunks(TLOC):
                p.mm(ps_dbc[:, c0:c0 + cw], xp_sb[:, 96 * m:96 * (m + 1)],
                     xib[:, c0:c0 + cw], start=(m == 0), stop=(m == 15))

        for m in range(16, 32):
            w = wstage[m % 2]
            p.dma("gpsimd", w[:],
                  in_proj[:, 128 * m:128 * (m + 1)].rearrange("(k p) m -> p k m", p=128))
            for ci, (c0, cw) in enumerate(col_chunks(TLOC)):
                ps = ps_a_r[ci % 2]
                for k in range(8):
                    p.mm(ps[:, 0:cw], w[:, 128 * k:128 * (k + 1)],
                         xmb[k][:, HALO1 + c0:HALO1 + c0 + cw],
                         start=(k == 0), stop=(k == 7))
                p.act(sp1[:, c0:c0 + cw], ps[:, 0:cw], AF.Copy)
            p.dma("sync", z_f[128 * (m - 16):128 * (m - 15), :], sp1[:])

        p.act(dbc_sb[:], ps_dbc[:], AF.Copy)
        p.dma("sync", bT, dbc_sb[64:80, :])
        p.dma("sync", cT, dbc_sb[80:96, :])
        p.copy(dtlb[:], dbc_sb[0:64, :])
        for dm in range(16):
            for ci, (c0, cw) in enumerate(col_chunks(TLOC)):
                ps = ps_a_r[ci % 2]
                p.mm(ps[:, 0:cw], dtw_sb[:, 128 * dm:128 * (dm + 1)],
                     dtlb[:, c0:c0 + cw], start=True, stop=True)
                p.act(sp1[:, c0:c0 + cw], ps[:, 0:cw], AF.Abs,
                      bias=dtb_sb[:, dm:dm + 1])
                p.act(sp1[:, c0:c0 + cw], sp1[:, c0:c0 + cw], AF.Exp, scale=-1.0)
                p.act(sp1[:, c0:c0 + cw], sp1[:, c0:c0 + cw], AF.Ln, bias=1.0)
                p.ts(sp2[:, c0:c0 + cw], ps[:, 0:cw], dtb_sb[:, dm:dm + 1], 0.0,
                     op0=OP.add, op1=OP.max)
                p.tt(sp2[:, c0:c0 + cw], sp2[:, c0:c0 + cw], sp1[:, c0:c0 + cw],
                     OP.add)
            p.dma("sync", delta_f[128 * dm:128 * (dm + 1), :], sp2[:])

        p.emit(block, sems)
    return nc


def shard_tokens(a_t):
    return [a_t[c * TLOC:(c + 1) * TLOC] for c in range(NCORE)]


def make_xh(xT):
    outs = []
    for c in range(NCORE):
        t0 = c * TLOC
        if c % 4 == 0:
            halo = np.zeros((D, HALO1), np.float32)
        else:
            halo = xT[:, t0 - HALO1:t0]
        outs.append(np.ascontiguousarray(
            np.concatenate([halo, xT[:, t0:t0 + TLOC]], axis=1)))
    return outs


def run_mamba_pre(x, w):
    xT = np.ascontiguousarray(x.reshape(16384, D).T)
    xhs = make_xh(xT)
    nc = build_mamba_pre()
    in_maps = []
    for c in range(NCORE):
        in_maps.append({
            "xh": xhs[c],
            "rms_w": w["rms_w"], "in_proj": w["in_proj"],
            "conv_w": w["conv_w"], "conv_b": w["conv_b"],
            "x_proj": w["x_proj"], "dt_w": w["dt_w"], "dt_b": w["dt_b"],
        })
    res = run_bass_kernel_spmd(nc, in_maps, core_ids=list(range(NCORE)))
    return res.results


TFULL = 16384
TCH = 4096


def build_scan():
    nc = bass.Bass("TRN2", target_bir_lowering=False, detect_race_conditions=False)
    deltaS = nc.dram_tensor("deltaS", [256, TFULL], F32, kind="ExternalInput").ap()
    xiS = nc.dram_tensor("xiS", [256, TFULL], F32, kind="ExternalInput").ap()
    bTf = nc.dram_tensor("bTf", [DST, TFULL], F32, kind="ExternalInput").ap()
    cTf = nc.dram_tensor("cTf", [DST, TFULL], F32, kind="ExternalInput").ap()
    a_log = nc.dram_tensor("a_log", [256, DST], F32, kind="ExternalInput").ap()
    d_vec = nc.dram_tensor("d_vec", [256], F32, kind="ExternalInput").ap()
    yS = nc.dram_tensor("yS", [256, TFULL], F32, kind="ExternalOutput").ap()

    with contextlib.ExitStack() as st:
        E = st.enter_context
        dl = E(nc.sbuf_tensor("dl", [128, TCH], F32))
        xi = E(nc.sbuf_tensor("xi", [128, TCH], F32))
        wv = E(nc.sbuf_tensor("wv", [128, TCH], F32))
        av = E(nc.sbuf_tensor("av", [128, TCH], F32))
        uv = E(nc.sbuf_tensor("uv", [128, TCH], F32))
        hv = E(nc.sbuf_tensor("hv", [128, TCH], F32))
        yv = E(nc.sbuf_tensor("yv", [128, TCH], F32))
        bbc = [E(nc.sbuf_tensor(f"bbc{i}", [128, TCH], F32)) for i in range(2)]
        cbc = [E(nc.sbuf_tensor(f"cbc{i}", [128, TCH], F32)) for i in range(2)]
        a_sb = E(nc.sbuf_tensor("a_sb", [128, DST], F32))
        d_sb = E(nc.sbuf_tensor("d_sb", [128, 2], F32))
        carry = E(nc.sbuf_tensor("carry", [128, DST], F32))
        cpool = E(nc.sbuf_tensor("cpool", [128, 4], F32))
        sems = make_sems(nc, E)
        block = E(nc.Block())

        p = Prog(same_engine_edges=True)
        cst = Consts(nc, p, cpool)
        cst.reg(-1.0, 1.0, 0.0)
        p.dma("sync", d_sb[:], d_vec.rearrange("(i p) -> p i", p=128),
              allow_slow_non_contiguous=True)
        for blk in range(2):
            p.dma("sync", a_sb[:], a_log[128 * blk:128 * (blk + 1), :])
            p.act(a_sb[:], a_sb[:], AF.Exp)
            p.ts(a_sb[:], a_sb[:], -1.0, op0=OP.mult)
            for tc in range(4):
                t0 = tc * TCH
                p.dma("sync", dl[:], deltaS[128 * blk:128 * (blk + 1), t0:t0 + TCH])
                p.dma("sync", xi[:], xiS[128 * blk:128 * (blk + 1), t0:t0 + TCH])
                p.tt(wv[:], dl[:], xi[:], OP.mult)
                p.ts(yv[:], xi[:], d_sb[:, blk:blk + 1], op0=OP.mult)
                for s in range(DST):
                    bb, cb = bbc[s % 2], cbc[s % 2]
                    p.dma("gpsimd", bb[:],
                          bTf[s:s + 1, t0:t0 + TCH].broadcast_to([128, TCH]))
                    p.dma("sync", cb[:],
                          cTf[s:s + 1, t0:t0 + TCH].broadcast_to([128, TCH]))
                    p.act(av[:], dl[:], AF.Exp, scale=a_sb[:, s:s + 1])
                    p.tt(uv[:], wv[:], bb[:], OP.mult)
                    init = 0.0 if tc % 2 == 0 else carry[:, s:s + 1]
                    p.scan(hv[:], av[:], uv[:], init)
                    p.copy(carry[:, s:s + 1], hv[:, TCH - 1:TCH])
                    p.tt(av[:], hv[:], cb[:], OP.mult)
                    p.tt(yv[:], yv[:], av[:], OP.add)
                p.dma("sync", yS[128 * blk:128 * (blk + 1), t0:t0 + TCH], yv[:])
        p.emit(block, sems)
    return nc


def run_scan(pre_results, a_log_full, d_full):
    delta_fm = np.concatenate([np.asarray(r["delta_f"]) for r in pre_results], axis=1)
    xi_fm = np.concatenate([np.asarray(r["xi_f"]) for r in pre_results], axis=1)
    bTf = np.concatenate([np.asarray(r["bT"]) for r in pre_results], axis=1)
    cTf = np.concatenate([np.asarray(r["cT"]) for r in pre_results], axis=1)
    nc = build_scan()
    in_maps = []
    for c in range(NCORE):
        r0 = c * 256
        in_maps.append({
            "deltaS": np.ascontiguousarray(delta_fm[r0:r0 + 256]),
            "xiS": np.ascontiguousarray(xi_fm[r0:r0 + 256]),
            "bTf": bTf, "cTf": cTf,
            "a_log": np.ascontiguousarray(a_log_full[r0:r0 + 256]),
            "d_vec": np.ascontiguousarray(d_full[r0:r0 + 256]),
        })
    res = run_bass_kernel_spmd(nc, in_maps, core_ids=list(range(NCORE)))
    return np.concatenate([np.asarray(r["yS"]) for r in res.results], axis=0)


def build_mgate():
    nc = bass.Bass("TRN2", target_bir_lowering=False, detect_race_conditions=False)
    yT = nc.dram_tensor("yT", [DIN, TLOC], F32, kind="ExternalInput").ap()
    z_f = nc.dram_tensor("z_f", [DIN, TLOC], F32, kind="ExternalInput").ap()
    xn_f = nc.dram_tensor("xn_f", [D, TLOC], F32, kind="ExternalInput").ap()
    out_proj = nc.dram_tensor("out_proj", [DIN, D], F32, kind="ExternalInput").ap()
    xT = nc.dram_tensor("xT", [D, TLOC], F32, kind="ExternalInput").ap()
    g_f = nc.dram_tensor("g_f", [D, TLOC], F32, kind="ExternalInput").ap()
    x1_f = nc.dram_tensor("x1_f", [D, TLOC], F32, kind="ExternalOutput").ap()

    with contextlib.ExitStack() as st:
        E = st.enter_context
        gb = [E(nc.sbuf_tensor(f"gb{i}", [128, TLOC], BF16)) for i in range(16)]
        zt = E(nc.sbuf_tensor("zt", [128, TLOC], F32))
        yt = E(nc.sbuf_tensor("yt", [128, TLOC], F32))
        xnt = E(nc.sbuf_tensor("xnt", [128, TLOC], F32))
        st_out = E(nc.sbuf_tensor("st_out", [128, TLOC], F32))
        gch = E(nc.sbuf_tensor("gch", [128, TLOC], F32))
        xch = E(nc.sbuf_tensor("xch", [128, TLOC], F32))
        op_sb = E(nc.sbuf_tensor("op_sb", [128, 16 * D], BF16))
        ps = [E(nc.psum_tensor(f"ps{i}", [128, 512], F32)) for i in range(2)]
        sems = make_sems(nc, E)
        block = E(nc.Block())

        p = Prog(same_engine_edges=True)
        p.dma("gpsimd", op_sb[:].rearrange("p (k m) -> p k m", m=D),
              out_proj.rearrange("(k p) m -> p k m", p=128))
        for i in range(16):
            p.dma("sync", zt[:], z_f[128 * i:128 * (i + 1), :])
            p.dma("sync", yt[:], yT[128 * i:128 * (i + 1), :])
            p.act(zt[:], zt[:], AF.Silu)
            p.tt(zt[:], zt[:], yt[:], OP.mult)
            p.copy(gb[i][:], zt[:])
        for m in range(8):
            p.dma("sync", xnt[:], xn_f[128 * m:128 * (m + 1), :])
            p.dma("sync", gch[:], g_f[128 * m:128 * (m + 1), :])
            p.dma("sync", xch[:], xT[128 * m:128 * (m + 1), :])
            for ci, (c0, cw) in enumerate(col_chunks(TLOC)):
                pp = ps[ci % 2]
                for k in range(16):
                    p.mm(pp[:, 0:cw], op_sb[:, k * D + 128 * m:k * D + 128 * (m + 1)],
                         gb[k][:, c0:c0 + cw], start=(k == 0), stop=(k == 15))
                p.tt(st_out[:, c0:c0 + cw], pp[:, 0:cw], xnt[:, c0:c0 + cw], OP.add)
            p.act(gch[:], gch[:], AF.Silu)
            p.tt(gch[:], gch[:], st_out[:], OP.mult)
            p.tt(gch[:], gch[:], xch[:], OP.add)
            p.dma("sync", x1_f[128 * m:128 * (m + 1), :], gch[:])
        p.emit(block, sems)
    return nc


def run_mgate(y_fm, pre_results, out_proj):
    nc = build_mgate()
    in_maps = []
    for c in range(NCORE):
        in_maps.append({
            "yT": np.ascontiguousarray(y_fm[:, c * TLOC:(c + 1) * TLOC]),
            "z_f": np.asarray(pre_results[c]["z_f"]),
            "xn_f": np.asarray(pre_results[c]["xn_f"]),
            "out_proj": out_proj,
        })
    res = run_bass_kernel_spmd(nc, in_maps, core_ids=list(range(NCORE)))
    return [np.asarray(r["s_f"]) for r in res.results]


HATT = 257
AXC = HATT + TLOC
QKC = 2304
NW = 8
WSZ = 256


def build_attn():
    nc = bass.Bass("TRN2", target_bir_lowering=False, detect_race_conditions=False)
    xh = nc.dram_tensor("xh", [D, AXC], F32, kind="ExternalInput").ap()
    wqkv = nc.dram_tensor("wqkv", [D, 3 * D], F32, kind="ExternalInput").ap()
    wo = nc.dram_tensor("wo", [D, D], F32, kind="ExternalInput").ap()
    qsE = nc.dram_tensor("qsE", [128, 1], F32, kind="ExternalInput").ap()
    ksE = nc.dram_tensor("ksE", [128, 1], F32, kind="ExternalInput").ap()
    csq = nc.dram_tensor("csq", [128, QKC], F32, kind="ExternalInput").ap()
    snq = nc.dram_tensor("snq", [128, QKC], F32, kind="ExternalInput").ap()
    csk = nc.dram_tensor("csk", [128, QKC], F32, kind="ExternalInput").ap()
    snk = nc.dram_tensor("snk", [128, QKC], F32, kind="ExternalInput").ap()
    maskQ = nc.dram_tensor("maskQ", [2, 128, 512], F32, kind="ExternalInput").ap()
    maskQ0 = nc.dram_tensor("maskQ0", [2, 128, 512], F32, kind="ExternalInput").ap()
    maskTd = nc.dram_tensor("maskTd", [4, 128, 512], F32, kind="ExternalInput").ap()
    maskT0d = nc.dram_tensor("maskT0d", [4, 128, 512], F32, kind="ExternalInput").ap()
    bcast2 = nc.dram_tensor("bcast2", [2, 128], F32, kind="ExternalInput").ap()
    perm = nc.dram_tensor("perm", [128, 128], F32, kind="ExternalInput").ap()
    ident = nc.dram_tensor("ident", [128, 128], F32, kind="ExternalInput").ap()
    x_out = nc.dram_tensor("x_out", [D, TLOC], F32, kind="ExternalOutput").ap()

    with contextlib.ExitStack() as st:
        E = st.enter_context
        xsb = [E(nc.sbuf_tensor(f"xsb{i}", [128, AXC], BF16)) for i in range(8)]
        vsb = [E(nc.sbuf_tensor(f"vsb{i}", [128, 16 * 65], BF16)) for i in range(18)]
        attn_sb = [E(nc.sbuf_tensor(f"attn{i}", [128, TLOC], BF16)) for i in range(8)]
        wv_all = E(nc.sbuf_tensor("wv_all", [128, 8 * D], BF16))
        csqb = E(nc.sbuf_tensor("csqb", [128, QKC], BF16))
        snqb = E(nc.sbuf_tensor("snqb", [128, QKC], BF16))
        cskb = E(nc.sbuf_tensor("cskb", [128, QKC], BF16))
        snkb = E(nc.sbuf_tensor("snkb", [128, QKC], BF16))
        mQ = E(nc.sbuf_tensor("mQ", [128, 2 * 512], F32))
        mQ0 = E(nc.sbuf_tensor("mQ0", [128, 2 * 512], F32))
        mT = E(nc.sbuf_tensor("mT", [128, 4 * 512], F32))
        mT0 = E(nc.sbuf_tensor("mT0", [128, 2 * 512], F32))
        bc2f = E(nc.sbuf_tensor("bc2f", [2, 128], F32))
        permb = E(nc.sbuf_tensor("permb", [128, 128], BF16))
        idb = E(nc.sbuf_tensor("idb", [128, 128], BF16))
        idf = E(nc.sbuf_tensor("idf", [128, 128], F32))
        ones2 = E(nc.sbuf_tensor("ones2", [128, 2], F32))
        ones1b = E(nc.sbuf_tensor("ones1b", [1, 128], BF16))
        onesKb = E(nc.sbuf_tensor("onesKb", [128, 1], BF16))
        qse_sb = E(nc.sbuf_tensor("qse_sb", [128, 1], F32))
        kse_sb = E(nc.sbuf_tensor("kse_sb", [128, 1], F32))
        wqk = [E(nc.sbuf_tensor("wqk0", [128, 2 * 1024], BF16))] * 2
        wost = E(nc.sbuf_tensor("wost", [128, 1024], BF16))
        qt = E(nc.sbuf_tensor("qt", [128, QKC], F32))
        qnb = E(nc.sbuf_tensor("qnb", [128, QKC], BF16))
        knb = E(nc.sbuf_tensor("knb", [128, QKC], BF16))
        ubb = [E(nc.sbuf_tensor(f"ubb{i}", [128, 512], BF16)) for i in range(2)]
        rbc = [E(nc.sbuf_tensor(f"rbc{i}", [128, 512], F32)) for i in range(2)]
        nrm = E(nc.sbuf_tensor("nrm", [2, 512], F32))
        nrmb = E(nc.sbuf_tensor("nrmb", [1, 512], BF16))
        scr = [E(nc.sbuf_tensor(f"scr{i}", [128, 512], F32)) for i in range(2)]
        rotb = E(nc.sbuf_tensor("rotb", [128, 512], BF16))
        t2b = E(nc.sbuf_tensor("t2b", [128, 512], BF16))
        scrT = [E(nc.sbuf_tensor(f"scrT{i}", [128, 512], F32)) for i in range(2)]
        ptb = [E(nc.sbuf_tensor(f"ptb{i}", [128, 512], BF16)) for i in range(2)]
        mstk = E(nc.sbuf_tensor("mstk", [128, 4], F32))
        negrow = [E(nc.sbuf_tensor(f"negrow{i}", [1, 512], BF16)) for i in range(2)]
        cpool = E(nc.sbuf_tensor("cpool", [128, 8], F32))
        psA = [E(nc.psum_tensor(f"psA{i}", [128, 512], F32)) for i in range(2)]
        psB = [E(nc.psum_tensor(f"psB{i}", [128, 512], F32)) for i in range(2)]
        psPV = [E(nc.psum_tensor(f"psPV{i}", [65, 512], F32)) for i in range(2)]
        psL = [E(nc.psum_tensor(f"psL{i}", [1, 512], F32)) for i in range(2)]
        sems = make_sems(nc, E)
        block = E(nc.Block())

        p = Prog()
        cst = Consts(nc, p, cpool)
        cst.reg(-1.0, 1.0, 0.0, 1e-12, 8.0)

        for i in range(8):
            p.dma("gpsimd", xsb[i][:], xh[128 * i:128 * (i + 1), :])
        p.dma("gpsimd", csqb[:], csq)
        p.dma("gpsimd", snqb[:], snq)
        p.dma("gpsimd", cskb[:], csk)
        p.dma("gpsimd", snkb[:], snk)
        p.dma("sync", mQ[:].rearrange("p (i c) -> p i c", c=512),
              maskQ.rearrange("i p c -> p i c"))
        p.dma("sync", mQ0[:].rearrange("p (i c) -> p i c", c=512),
              maskQ0.rearrange("i p c -> p i c"))
        p.dma("sync", mT[:].rearrange("p (i c) -> p i c", c=512),
              maskTd.rearrange("i p c -> p i c"))
        p.dma("sync", mT0[:].rearrange("p (i c) -> p i c", c=512),
              maskT0d[0:2].rearrange("i p c -> p i c"))
        p.dma("sync", bc2f[:], bcast2)
        p.dma("gpsimd", permb[:], perm)
        p.dma("gpsimd", idb[:], ident)
        p.dma("sync", idf[:], ident)
        p.dma("sync", qse_sb[:], qsE)
        p.dma("sync", kse_sb[:], ksE)
        for k in range(8):
            p.dma("gpsimd", wv_all[:, D * k:D * (k + 1)],
                  wqkv[128 * k:128 * (k + 1), 2 * D:3 * D])
        p.memset(ones2[:], 0.0)
        p.memset(ones2[0:64, 0:1], 1.0)
        p.memset(ones2[64:128, 1:2], 1.0)
        p.memset(ones1b[:], 1.0)
        p.memset(onesKb[:], 1.0)

        def xs_cols(i, c0, cwidth):
            off = 1 if i < 4 else 0
            return xsb[i][:, c0 + off:c0 + off + cwidth]

        for tc in range(18):
            p.memset(vsb[tc][:].rearrange("p (h c) -> p h c", c=65)[:, :, 64:65],
                     1.0)
            for half in range(2):
                ps = psA[(2 * tc + half) % 2]
                for k in range(8):
                    p.mm(ps[:, 0:512], xs_cols(k, 128 * tc, 128),
                         wv_all[:, D * k + 512 * half:D * k + 512 * (half + 1)],
                         start=(k == 0), stop=(k == 7))
                p.copy(vsb[tc][:].rearrange("p (h c) -> p h c", c=65)
                       [:, 8 * half:8 * (half + 1), 0:64],
                       ps[:, 0:512].rearrange("p (h c) -> p h c", c=64))

        def qk_make(dst_b, cs_b, sn_b, scale_col, lo=0):
            for ci, (c0, cw) in enumerate(col_chunks(QKC - lo)):
                c0 += lo
                s0 = scr[ci % 2]
                p.act(s0[:, 0:cw], qt[:, c0:c0 + cw], AF.Square)
                p.mm(psB[0][0:2, 0:cw], ones2[:], s0[:, 0:cw], start=True, stop=True)
                p.act(nrm[0:2, 0:cw], psB[0][0:2, 0:cw], AF.Sqrt)
                p.recip(nrm[0:2, 0:cw], nrm[0:2, 0:cw])
                p.ts(nrm[0:2, 0:cw], nrm[0:2, 0:cw], 1e12, op0=OP.min)
                p.mm(psB[0][:, 0:cw], bc2f[:], nrm[0:2, 0:cw],
                     start=True, stop=True)
                p.act(rbc[ci % 2][:, 0:cw], psB[0][:, 0:cw], AF.Copy)
                p.stt(qt[:, c0:c0 + cw], qt[:, c0:c0 + cw], scale_col,
                      rbc[ci % 2][:, 0:cw], OP.mult, OP.mult)
            for ci, (c0, cw) in enumerate(col_chunks(QKC - lo)):
                c0 += lo
                ub = ubb[ci % 2]
                ps = psB[ci % 2]
                p.copy(ub[:, 0:cw], qt[:, c0:c0 + cw])
                p.mm(ps[:, 0:cw], permb[:], ub[:, 0:cw],
                     start=True, stop=True)
                p.act(rotb[:, 0:cw], ps[:, 0:cw], AF.Copy)
                p.tt(rotb[:, 0:cw], rotb[:, 0:cw], sn_b[:, c0:c0 + cw], OP.mult)
                p.tt(t2b[:, 0:cw], ub[:, 0:cw], cs_b[:, c0:c0 + cw], OP.mult)
                p.tt(dst_b[:, c0:c0 + cw], t2b[:, 0:cw], rotb[:, 0:cw], OP.add)

        for hp in range(8):
            par = hp % 2
            wq = wqk[par]
            p.dma("gpsimd", wq[:, 0:1024].rearrange("p (k m) -> p k m", m=128),
                  wqkv[:, 128 * hp:128 * (hp + 1)].rearrange("(k p) m -> p k m", p=128))
            p.dma("gpsimd", wq[:, 1024:2048].rearrange("p (k m) -> p k m", m=128),
                  wqkv[:, D + 128 * hp:D + 128 * (hp + 1)].rearrange("(k p) m -> p k m", p=128))
            for c0, cw in col_chunks(TLOC):
                c0 += 256
                ps = psA[(c0 // 512) % 2]
                for k in range(8):
                    p.mm(ps[:, 0:cw], wq[:, 128 * k:128 * (k + 1)],
                         xs_cols(k, c0, cw), start=(k == 0), stop=(k == 7))
                p.act(qt[:, c0:c0 + cw], ps[:, 0:cw], AF.Copy)
            qk_make(qnb, csqb, snqb, qse_sb[:, 0:1], lo=256)
            for c0, cw in col_chunks(QKC):
                ps = psA[(c0 // 512) % 2]
                for k in range(8):
                    p.mm(ps[:, 0:cw], wq[:, 1024 + 128 * k:1024 + 128 * (k + 1)],
                         xs_cols(k, c0, cw), start=(k == 0), stop=(k == 7))
                p.act(qt[:, c0:c0 + cw], ps[:, 0:cw], AF.Copy)
            qk_make(knb, cskb, snkb, kse_sb[:, 0:1])
            qn, kn = qnb, knb

            for w in range(NW):
                wp = w % 2
                qcol0 = 256 + 256 * w
                kcol0 = 256 * w
                mq = mQ0 if w == 0 else mQ
                for hh in range(2):
                    base = 64 * hh
                    for qb in range(2):
                        ps = psA[(2 * hh + qb) % 2]
                        s0 = scr[(2 * hh + qb) % 2]
                        p.mm(ps[:, 0:512],
                             qn[base:base + 64, qcol0 + 128 * qb:qcol0 + 128 * (qb + 1)],
                             kn[base:base + 64, kcol0:kcol0 + 512],
                             start=True, stop=True)
                        p.tt(s0[:, 0:512], ps[:, 0:512],
                             mq[:, 512 * qb:512 * (qb + 1)], OP.add)
                        p.red(mstk[:, 2 * hh + qb:2 * hh + qb + 1], s0[:, 0:512],
                              OP.max)
                for col in range(4):
                    hh, qb = col // 2, col % 2
                    p.transpose(psL[0][0:1, 0:128], mstk[:, col:col + 1], idf[:])
                    p.ts(negrow[wp][0:1, 256 * hh + 128 * qb:256 * hh + 128 * (qb + 1)],
                         psL[0][0:1, 0:128], -1.0, op0=OP.mult)
                for kc in range(4):
                    ps = psB[kc % 2]
                    sT = scrT[kc % 2]
                    pt = ptb[kc % 2]
                    for hh in range(2):
                        base = 64 * hh
                        p.mm(ps[:, 256 * hh:256 * (hh + 1)],
                             kn[base:base + 64, kcol0 + 128 * kc:kcol0 + 128 * (kc + 1)],
                             qn[base:base + 64, qcol0:qcol0 + WSZ],
                             start=(hh == 0), stop=False,
                             tile_position=(base, 0))
                        p.mm(ps[:, 256 * hh:256 * (hh + 1)], ones1b[:],
                             negrow[wp][0:1, 256 * hh:256 * hh + WSZ],
                             start=False, stop=(hh == 1))
                    mt_ap = (mT0[:, 512 * kc:512 * (kc + 1)]
                             if (w == 0 and kc < 2)
                             else mT[:, 512 * kc:512 * (kc + 1)])
                    p.tt(sT[:, 0:512], ps[:, 0:512], mt_ap, OP.add)
                    p.act(pt[:, 0:512], sT[:, 0:512], AF.Exp, scale=8.0)
                    for hh in range(2):
                        h_glob = 2 * hp + hh
                        p.mm(psPV[wp][0:65, 256 * hh:256 * (hh + 1)],
                             vsb[2 * w + kc][:, 65 * h_glob:65 * (h_glob + 1)],
                             pt[:, 256 * hh:256 * (hh + 1)],
                             start=(kc == 0 and hh == 0),
                             stop=(kc == 3 and hh == 1))
                p.recip(nrm[0:1, 0:512], psPV[wp][64:65, 0:512])
                p.copy(nrmb[0:1, 0:512], nrm[0:1, 0:512])
                p.mm(psB[0][0:64, 0:512], ones1b[0:1, 0:64], nrmb[0:1, 0:512],
                     start=True, stop=True)
                p.act(scr[1][0:64, 0:512], psB[0][0:64, 0:512], AF.Copy)
                for hh in range(2):
                    p.tt(attn_sb[hp][64 * hh:64 * (hh + 1), 256 * w:256 * (w + 1)],
                         psPV[wp][0:64, 256 * hh:256 * (hh + 1)],
                         scr[1][0:64, 256 * hh:256 * (hh + 1)], OP.mult)

        out_sb = qt
        for m in range(8):
            p.dma("gpsimd", wost[:].rearrange("p (k m) -> p k m", m=128),
                  wo[:, 128 * m:128 * (m + 1)].rearrange("(k p) m -> p k m", p=128))
            for c0, cw in col_chunks(TLOC):
                par2 = (c0 // 512) % 2
                ps = psA[par2]
                xs_ = scr[par2]
                p.dma("sync", xs_[:, 0:cw],
                      xh[128 * m:128 * (m + 1), HATT + c0:HATT + c0 + cw])
                for k in range(8):
                    p.mm(ps[:, 0:cw], wost[:, 128 * k:128 * (k + 1)],
                         attn_sb[k][:, c0:c0 + cw], start=(k == 0), stop=(k == 7))
                p.tt(out_sb[:, c0:c0 + cw], ps[:, 0:cw], xs_[:, 0:cw], OP.add)
            p.dma("sync", x_out[128 * m:128 * (m + 1), :], out_sb[:, 0:TLOC])
        p.emit(block, sems)
    return nc


def attn_consts(core):
    DIM_HEAD, WINDOW = 64, 256
    n = 8192
    lt0 = (core % 4) * TLOC
    pos = np.arange(lt0 - 256, lt0 + TLOC, dtype=np.float64)
    half = np.arange(0, DIM_HEAD, 2, dtype=np.float64)
    inv_freq = 1.0 / 10000.0 ** (half / DIM_HEAD)
    freqs = pos[:, None] * inv_freq[None, :]
    freqs = np.concatenate([freqs, freqs], axis=1)
    base_scale = (half + 0.4 * DIM_HEAD) / (1.4 * DIM_HEAD)
    power = (pos - n // 2) / (WINDOW // 2)
    scale = base_scale[None, :] ** power[:, None]
    scale = np.concatenate([scale, scale], axis=1)
    cos, sin = np.cos(freqs), np.sin(freqs)
    csq = (cos * scale).T
    snq = (sin * scale).T
    csk = (cos / scale).T
    snk = (sin / scale).T
    two = lambda a: np.ascontiguousarray(
        np.concatenate([a, a], axis=0).astype(np.float32))
    csq, snq, csk, snk = map(two, (csq, snq, csk, snk))

    i_ = np.arange(256)[:, None]
    j_ = np.arange(512)[None, :]
    bad = ~((i_ <= j_) & (j_ <= i_ + 256))
    mq = np.where(bad, NEG, 0.0).astype(np.float32)
    maskQ = np.stack([mq[0:128], mq[128:256]])
    mt_full = np.where(bad.T, NEG, 0.0).astype(np.float32)
    dd = lambda m: np.concatenate([m, m], axis=1)
    maskT = np.stack([dd(mt_full[128 * kc:128 * (kc + 1)]) for kc in range(4)])
    if core % 4 == 0:
        mq0 = mq.copy()
        mq0[:, 0:256] = NEG
        maskQ0 = np.stack([mq0[0:128], mq0[128:256]])
        mt0 = mt_full.copy()
        mt0[0:256, :] = NEG
        maskT0 = np.stack([dd(mt0[128 * kc:128 * (kc + 1)]) for kc in range(4)])
    else:
        maskQ0, maskT0 = maskQ, maskT

    P = np.zeros((128, 128), np.float32)
    for blk in range(2):
        for dd in range(32):
            P[blk * 64 + dd + 32, blk * 64 + dd] = -1.0
            P[blk * 64 + dd, blk * 64 + dd + 32] = 1.0
    bcast2 = np.zeros((2, 128), np.float32)
    bcast2[0, 0:64] = 1.0
    bcast2[1, 64:128] = 1.0
    return {
        "csq": csq, "snq": snq, "csk": csk, "snk": snk,
        "maskQ": np.ascontiguousarray(maskQ), "maskQ0": np.ascontiguousarray(maskQ0),
        "maskTd": np.ascontiguousarray(maskT), "maskT0d": np.ascontiguousarray(maskT0),
        "bcast2": bcast2, "perm": P, "ident": np.eye(128, dtype=np.float32),
    }


def run_attn(nc_attn, x_fm, wqkv_i, qs_i, ks_i, wo_i, consts):
    in_maps = []
    for c in range(NCORE):
        t0 = c * TLOC
        if c % 4 == 0:
            halo = np.zeros((D, HATT), np.float32)
        else:
            halo = x_fm[:, t0 - HATT:t0]
        xh = np.ascontiguousarray(np.concatenate([halo, x_fm[:, t0:t0 + TLOC]], 1))
        m = {"xh": xh, "wqkv": wqkv_i, "wo": wo_i,
             "qsE": np.ascontiguousarray(np.tile(qs_i, 2)[:, None]),
             "ksE": np.ascontiguousarray(np.tile(ks_i, 2)[:, None])}
        m.update(consts[c])
        in_maps.append(m)
    res = run_bass_kernel_spmd(nc_attn, in_maps, core_ids=list(range(NCORE)))
    return np.concatenate([np.asarray(r["x_out"]) for r in res.results], axis=1)


DFF = 2730
NMC = 22


def build_ffn():
    nc = bass.Bass("TRN2", target_bir_lowering=False, detect_race_conditions=False)
    xh = nc.dram_tensor("xh", [D, 1 + TLOC], F32, kind="ExternalInput").ap()
    w1a = nc.dram_tensor("w1a", [D, DFF], F32, kind="ExternalInput").ap()
    w1g = nc.dram_tensor("w1g", [D, DFF], F32, kind="ExternalInput").ap()
    w2 = nc.dram_tensor("w2", [DFF, D], F32, kind="ExternalInput").ap()
    gE = nc.dram_tensor("gE", [128, 8], F32, kind="ExternalInput").ap()
    bE = nc.dram_tensor("bE", [128, 8], F32, kind="ExternalInput").ap()
    b1aE = nc.dram_tensor("b1aE", [128, NMC], F32, kind="ExternalInput").ap()
    b1gE = nc.dram_tensor("b1gE", [128, NMC], F32, kind="ExternalInput").ap()
    b2E = nc.dram_tensor("b2E", [128, 8], F32, kind="ExternalInput").ap()
    x_out = nc.dram_tensor("x_out", [D, TLOC], F32, kind="ExternalOutput").ap()

    with contextlib.ExitStack() as st:
        E = st.enter_context
        xnb = [E(nc.sbuf_tensor(f"xnb{i}", [128, TLOC], BF16)) for i in range(8)]
        xs32 = E(nc.sbuf_tensor("xs32", [128, 1 + TLOC], F32))
        sq = E(nc.sbuf_tensor("sq", [128, 512], F32))
        rows = E(nc.sbuf_tensor("rows", [1, TLOC], F32))
        row2 = E(nc.sbuf_tensor("row2", [1, TLOC], F32))
        crow = E(nc.sbuf_tensor("crow", [1, TLOC], F32))
        bc1b = E(nc.sbuf_tensor("bc1b", [128, TLOC], BF16))
        bc2b = E(nc.sbuf_tensor("bc2b", [128, TLOC], BF16))
        ones1 = E(nc.sbuf_tensor("ones1", [1, 128], F32))
        ones1c = E(nc.sbuf_tensor("ones1c", [128, 1], F32))
        ones1cb = E(nc.sbuf_tensor("ones1cb", [128, 1], BF16))
        gsb = E(nc.sbuf_tensor("gsb", [128, 8], F32))
        bsb = E(nc.sbuf_tensor("bsb", [128, 8], F32))
        b1as = E(nc.sbuf_tensor("b1as", [128, NMC], F32))
        b1gs = E(nc.sbuf_tensor("b1gs", [128, NMC], F32))
        b2s = E(nc.sbuf_tensor("b2s", [128, 8], F32))
        wst = E(nc.sbuf_tensor("wst", [128, 8 * 128], BF16))
        wst2 = E(nc.sbuf_tensor("wst2", [128, 8 * 128], BF16))
        w2st = E(nc.sbuf_tensor("w2st", [128, NMC * 128], BF16))
        haf = E(nc.sbuf_tensor("haf", [128, TLOC], F32))
        gel = E(nc.sbuf_tensor("gel", [128, TLOC], F32))
        cpool = E(nc.sbuf_tensor("cpool", [128, 8], F32))
        psA = E(nc.psum_tensor("psA", [128, 512], F32))
        psB = E(nc.psum_tensor("psB", [128, 512], F32))
        psR1 = E(nc.psum_tensor("psR1", [1, 512], F32))
        psR2 = E(nc.psum_tensor("psR2", [1, 512], F32))
        psBC = E(nc.psum_tensor("psBC", [128, 512], F32))
        sems = make_sems(nc, E)
        block = E(nc.Block())
        st2 = contextlib.ExitStack()
        xbr = [st2.enter_context(nc.sbuf_tensor(f"xbr{i}", [128, 1 + TLOC], BF16))
               for i in range(8)]

        p = Prog()
        cst = Consts(nc, p, cpool)
        cst.reg(-1.0, 1.0, 0.0, 1e-5, 1.0 / D)
        p.dma("sync", gsb[:], gE)
        p.dma("sync", bsb[:], bE)
        p.dma("sync", b1as[:], b1aE)
        p.dma("sync", b1gs[:], b1gE)
        p.dma("sync", b2s[:], b2E)
        p.memset(ones1[:], 1.0)
        p.memset(ones1c[:], 1.0)
        p.memset(ones1cb[:], 1.0)

        def shx(i, c0, cw):
            off = 1 if i < 4 else 0
            return xbr[i][:, c0 + off:c0 + off + cw]

        for i in range(8):
            p.dma("gpsimd", xbr[i][:], xh[128 * i:128 * (i + 1), :])
        for c0, cw in col_chunks(TLOC):
            for i in range(8):
                p.mm(psR1[:, 0:cw], ones1cb[:], shx(i, c0, cw),
                     start=(i == 0), stop=(i == 7))
            p.copy(rows[:, c0:c0 + cw], psR1[:, 0:cw])
            for i in range(8):
                p.act(sq[:, 0:cw], shx(i, c0, cw), AF.Square)
                p.mm(psR2[:, 0:cw], ones1c[:], sq[:, 0:cw],
                     start=(i == 0), stop=(i == 7))
            p.copy(row2[:, c0:c0 + cw], psR2[:, 0:cw])
        p.ts(rows[:], rows[:], 1.0 / D, op0=OP.mult)
        p.act(crow[:], rows[:], AF.Square)
        p.ts(row2[:], row2[:], 1.0 / D, op0=OP.mult)
        p.tt(crow[:], row2[:], crow[:], OP.subtract)
        p.act(crow[:], crow[:], AF.Ln, bias=1e-5)
        p.act(crow[:], crow[:], AF.Exp, scale=-0.5)
        for c0, cw in col_chunks(TLOC):
            p.mm(psBC[:, 0:cw], ones1[:], crow[:, c0:c0 + cw], start=True, stop=True)
            p.copy(bc1b[:, c0:c0 + cw], psBC[:, 0:cw])
        p.tt(crow[:], rows[:], crow[:], OP.mult)
        p.ts(crow[:], crow[:], -1.0, op0=OP.mult)
        for c0, cw in col_chunks(TLOC):
            p.mm(psBC[:, 0:cw], ones1[:], crow[:, c0:c0 + cw], start=True, stop=True)
            p.copy(bc2b[:, c0:c0 + cw], psBC[:, 0:cw])
        for i in range(8):
            for c0, cw in col_chunks(TLOC):
                p.tt(xnb[i][:, c0:c0 + cw], shx(i, c0, cw), bc1b[:, c0:c0 + cw],
                     OP.mult)
            p.tt(xnb[i][:], xnb[i][:], bc2b[:], OP.add)
            p.ts(xnb[i][:], xnb[i][:], gsb[:, i:i + 1], bsb[:, i:i + 1],
                 op0=OP.mult, op1=OP.add)

        st2.close()
        tb = [E(nc.sbuf_tensor(f"tb{i}", [128, TLOC], BF16)) for i in range(NMC)]

        for mc in range(NMC):
            mw = min(128, DFF - 128 * mc)
            p.dma("gpsimd", wst[:, 0:8 * mw].rearrange("p (k m) -> p k m", m=mw),
                  w1a[:, 128 * mc:128 * mc + mw].rearrange("(k p) m -> p k m", p=128))
            p.dma("gpsimd", wst2[:, 0:8 * mw].rearrange("p (k m) -> p k m", m=mw),
                  w1g[:, 128 * mc:128 * mc + mw].rearrange("(k p) m -> p k m", p=128))
            for c0, cw in col_chunks(TLOC):
                for k in range(8):
                    p.mm(psA[0:mw, 0:cw], wst[:, mw * k:mw * (k + 1)],
                         xnb[k][:, c0:c0 + cw], start=(k == 0), stop=(k == 7))
                for k in range(8):
                    p.mm(psB[0:mw, 0:cw], wst2[:, mw * k:mw * (k + 1)],
                         xnb[k][:, c0:c0 + cw], start=(k == 0), stop=(k == 7))
                p.ts(haf[0:mw, c0:c0 + cw], psA[0:mw, 0:cw],
                     b1as[0:mw, mc:mc + 1], op0=OP.add)
                p.act(gel[0:mw, c0:c0 + cw], psB[0:mw, 0:cw], AF.Gelu,
                      bias=b1gs[0:mw, mc:mc + 1])
            p.tt(tb[mc][0:mw, :], haf[0:mw, :], gel[0:mw, :], OP.mult)

        for m2 in range(8):
            p.dma("gpsimd",
                  w2st[:, 0:21 * 128].rearrange("p (k m) -> p k m", m=128),
                  w2[0:2688, 128 * m2:128 * (m2 + 1)].rearrange("(k p) m -> p k m", p=128))
            p.dma("gpsimd", w2st[0:42, 21 * 128:22 * 128],
                  w2[2688:2730, 128 * m2:128 * (m2 + 1)])
            p.dma("sync", xs32[:, 0:TLOC], xh[128 * m2:128 * (m2 + 1), 1:1 + TLOC])
            for ci, (c0, cw) in enumerate(col_chunks(TLOC)):
                ps = psA if ci % 2 == 0 else psB
                for mc in range(NMC):
                    mw = min(128, DFF - 128 * mc)
                    p.mm(ps[:, 0:cw], w2st[0:mw, 128 * mc:128 * mc + 128],
                         tb[mc][0:mw, c0:c0 + cw], start=(mc == 0),
                         stop=(mc == NMC - 1))
                p.stt(haf[:, c0:c0 + cw], ps[:, 0:cw], b2s[:, m2:m2 + 1],
                      xs32[:, c0:c0 + cw], OP.add, OP.add)
            p.dma("sync", x_out[128 * m2:128 * (m2 + 1), :], haf[:, 0:TLOC])
        p.emit(block, sems)
    return nc


def run_ffn(nc_ffn, x_fm, W1_i, b1_i, W2_i, b2_i, lng_i, lnb_i):
    rear = lambda v: np.ascontiguousarray(v.reshape(-1, 128).T)
    padE = lambda v, n: np.ascontiguousarray(
        np.pad(v, (0, n * 128 - len(v))).reshape(n, 128).T)
    in_maps = []
    for c in range(NCORE):
        t0 = c * TLOC
        if c % 4 == 0:
            halo = np.zeros((D, 1), np.float32)
        else:
            halo = x_fm[:, t0 - 1:t0]
        xh = np.ascontiguousarray(np.concatenate([halo, x_fm[:, t0:t0 + TLOC]], 1))
        in_maps.append({
            "xh": xh,
            "w1a": np.ascontiguousarray(W1_i[:, :DFF]),
            "w1g": np.ascontiguousarray(W1_i[:, DFF:]),
            "w2": W2_i,
            "gE": rear(lng_i), "bE": rear(lnb_i),
            "b1aE": padE(b1_i[:DFF], NMC), "b1gE": padE(b1_i[DFF:], NMC),
            "b2E": rear(b2_i),
        })
    res = run_bass_kernel_spmd(nc_ffn, in_maps, core_ids=list(range(NCORE)))
    return np.concatenate([np.asarray(r["x_out"]) for r in res.results], axis=1)


def build_combine1():
    nc = bass.Bass("TRN2", target_bir_lowering=False, detect_race_conditions=False)
    xT = nc.dram_tensor("xT", [D, TLOC], F32, kind="ExternalInput").ap()
    s_f = nc.dram_tensor("s_f", [D, TLOC], F32, kind="ExternalInput").ap()
    g_f = nc.dram_tensor("g_f", [D, TLOC], F32, kind="ExternalInput").ap()
    x1_f = nc.dram_tensor("x1_f", [D, TLOC], F32, kind="ExternalOutput").ap()

    with contextlib.ExitStack() as st:
        E = st.enter_context
        a = E(nc.sbuf_tensor("a", [128, TLOC], F32))
        b = E(nc.sbuf_tensor("b", [128, TLOC], F32))
        c = E(nc.sbuf_tensor("c", [128, TLOC], F32))
        sems = make_sems(nc, E)
        block = E(nc.Block())
        p = Prog()
        for i in range(8):
            p.dma("sync", a[:], g_f[128 * i:128 * (i + 1), :])
            p.dma("sync", b[:], s_f[128 * i:128 * (i + 1), :])
            p.dma("sync", c[:], xT[128 * i:128 * (i + 1), :])
            p.act(a[:], a[:], AF.Silu)
            p.tt(a[:], a[:], b[:], OP.mult)
            p.tt(a[:], a[:], c[:], OP.add)
            p.dma("sync", x1_f[128 * i:128 * (i + 1), :], a[:])
        p.emit(block, sems)
    return nc


QTERMS = [
    [(0, 0, 1), (1, 1, -1), (2, 2, -1), (3, 3, -1)],
    [(1, 0, 1), (0, 1, 1), (3, 2, 1), (2, 3, -1)],
    [(2, 0, 1), (3, 1, -1), (0, 2, 1), (1, 3, 1)],
    [(3, 0, 1), (2, 1, 1), (1, 2, -1), (0, 3, 1)],
]


def emit_quat_load(p, Wd, bd, wpos, wneg, bE_sb, scrcol):
    for w in range(4):
        for kc in range(2):
            p.dma("gpsimd", wpos[:, (w * 2 + kc) * 256:(w * 2 + kc + 1) * 256],
                  Wd[w, 128 * kc:128 * (kc + 1), :])
    p.ts(wneg[:], wpos[:], -1.0, op0=OP.mult)
    p.dma("sync", scrcol[:].rearrange("p (w h) -> p w h", h=2),
          bd.rearrange("w (h p) -> p w h", p=128),
          allow_slow_non_contiguous=True)
    for o in range(4):
        for h in range(2):
            terms = QTERMS[o]
            (c0_, w0_, s0_) = terms[0]
            p.ts(bE_sb[:, o * 2 + h:o * 2 + h + 1],
                 scrcol[:, w0_ * 2 + h:w0_ * 2 + h + 1], float(s0_), op0=OP.mult)
            for (c_, w_, s_) in terms[1:]:
                p.stt(bE_sb[:, o * 2 + h:o * 2 + h + 1],
                      scrcol[:, w_ * 2 + h:w_ * 2 + h + 1], float(s_),
                      bE_sb[:, o * 2 + h:o * 2 + h + 1], OP.mult, OP.add)


def emit_quat_mm_fm(p, xb, wpos, wneg, bE_sb, psA, out_cb, quatnorm, scrq, nsq,
                    cols=TLOC):
    for oc in range(8):
        o, half = oc // 2, oc % 2
        for c0, cw in col_chunks(cols):
            first = True
            for (c_, w_, s_) in QTERMS[o]:
                src = wpos if s_ > 0 else wneg
                for kc in range(2):
                    p.mm(psA[:, 0:cw],
                         src[:, (w_ * 2 + kc) * 256 + 128 * half:
                             (w_ * 2 + kc) * 256 + 128 * half + 128],
                         xb[c_ * 2 + kc][:, c0:c0 + cw],
                         start=first, stop=(kc == 1 and (c_, w_, s_) == QTERMS[o][3]))
                    first = False
            p.ts(scrq[oc][:, c0:c0 + cw], psA[:, 0:cw],
                 bE_sb[:, oc:oc + 1], op0=OP.add)


def build_quat():
    nc = bass.Bass("TRN2", target_bir_lowering=False, detect_race_conditions=False)
    x1 = nc.dram_tensor("x1", [D, TLOC], F32, kind="ExternalInput").ap()
    qW = nc.dram_tensor("qW", [4, 256, 256], F32, kind="ExternalInput").ap()
    qb = nc.dram_tensor("qb", [4, 256], F32, kind="ExternalInput").ap()
    kW = nc.dram_tensor("kW", [4, 256, 256], F32, kind="ExternalInput").ap()
    kb = nc.dram_tensor("kb", [4, 256], F32, kind="ExternalInput").ap()
    vW = nc.dram_tensor("vW", [4, 256, 256], F32, kind="ExternalInput").ap()
    vb = nc.dram_tensor("vb", [4, 256], F32, kind="ExternalInput").ap()
    xn2b_o = nc.dram_tensor("xn2b_o", [D, TLOC], BF16, kind="ExternalOutput").ap()
    q_f = nc.dram_tensor("q_f", [D, TLOC], BF16, kind="ExternalOutput").ap()
    k_f = nc.dram_tensor("k_f", [D, TLOC], BF16, kind="ExternalOutput").ap()
    v_t = nc.dram_tensor("v_t", [TLOC, D], BF16, kind="ExternalOutput").ap()
    bscr = nc.dram_tensor("bscr", [128, 8], F32, kind="Internal").ap()

    with contextlib.ExitStack() as st:
        E = st.enter_context
        xb = [E(nc.sbuf_tensor(f"xb{i}", [128, TLOC], BF16)) for i in range(8)]
        sq = E(nc.sbuf_tensor("sq", [128, 512], F32))
        rows = E(nc.sbuf_tensor("rows", [1, TLOC], F32))
        row2 = E(nc.sbuf_tensor("row2", [1, TLOC], F32))
        crow = E(nc.sbuf_tensor("crow", [1, TLOC], F32))
        bc1 = E(nc.sbuf_tensor("bc1", [128, TLOC], F32))
        bc2 = E(nc.sbuf_tensor("bc2", [128, TLOC], F32))
        ones1 = E(nc.sbuf_tensor("ones1", [1, 128], F32))
        ones1c = E(nc.sbuf_tensor("ones1c", [128, 1], F32))
        onesb = E(nc.sbuf_tensor("onesb", [1, 128], BF16))
        wpos = E(nc.sbuf_tensor("wpos", [128, 2048], BF16))
        wneg = E(nc.sbuf_tensor("wneg", [128, 2048], BF16))
        bE_sb = E(nc.sbuf_tensor("bE_sb", [128, 8], F32))
        scrcol = E(nc.sbuf_tensor("scrcol", [128, 8], F32))
        bErow = E(nc.sbuf_tensor("bErow", [1, 1024], BF16))
        vout = E(nc.sbuf_tensor("vout", [128, 512], BF16))
        cpool = E(nc.sbuf_tensor("cpool", [128, 8], F32))
        psA = E(nc.psum_tensor("psA", [128, 512], F32))
        psV = E(nc.psum_tensor("psV", [128, 512], F32))
        psR1 = E(nc.psum_tensor("psR1", [1, 512], F32))
        psR2 = E(nc.psum_tensor("psR2", [1, 512], F32))
        psBC = E(nc.psum_tensor("psBC", [128, 512], F32))
        sems = make_sems(nc, E)
        block = E(nc.Block())
        st2 = contextlib.ExitStack()
        xc = [st2.enter_context(nc.sbuf_tensor(f"xc{i}", [128, TLOC], F32))
              for i in range(8)]

        p = Prog(same_engine_edges=True)
        cst = Consts(nc, p, cpool)
        cst.reg(-1.0, 1.0, 0.0, 1e-5, 1e-6, 1.0 / D)
        p.memset(ones1[:], 1.0)
        p.memset(ones1c[:], 1.0)
        p.memset(onesb[:], 1.0)
        for i in range(8):
            p.dma("sync", xc[i][:], x1[128 * i:128 * (i + 1), :])
        for c0, cw in col_chunks(TLOC):
            for i in range(8):
                p.mm(psR1[:, 0:cw], ones1c[:], xc[i][:, c0:c0 + cw],
                     start=(i == 0), stop=(i == 7))
            p.copy(rows[:, c0:c0 + cw], psR1[:, 0:cw])
            for i in range(8):
                p.act(sq[:, 0:cw], xc[i][:, c0:c0 + cw], AF.Square)
                p.mm(psR2[:, 0:cw], ones1c[:], sq[:, 0:cw],
                     start=(i == 0), stop=(i == 7))
            p.copy(row2[:, c0:c0 + cw], psR2[:, 0:cw])
        p.ts(rows[:], rows[:], 1.0 / D, op0=OP.mult)
        p.act(crow[:], rows[:], AF.Square)
        p.ts(row2[:], row2[:], 1.0 / D, op0=OP.mult)
        p.tt(crow[:], row2[:], crow[:], OP.subtract)
        p.act(crow[:], crow[:], AF.Ln, bias=1e-5)
        p.act(crow[:], crow[:], AF.Exp, scale=-0.5)
        for c0, cw in col_chunks(TLOC):
            p.mm(psBC[:, 0:cw], ones1[:], crow[:, c0:c0 + cw], start=True, stop=True)
            p.copy(bc1[:, c0:c0 + cw], psBC[:, 0:cw])
        p.tt(crow[:], rows[:], crow[:], OP.mult)
        p.ts(crow[:], crow[:], -1.0, op0=OP.mult)
        for c0, cw in col_chunks(TLOC):
            p.mm(psBC[:, 0:cw], ones1[:], crow[:, c0:c0 + cw], start=True, stop=True)
            p.copy(bc2[:, c0:c0 + cw], psBC[:, 0:cw])
        for i in range(8):
            p.tt(xc[i][:], xc[i][:], bc1[:], OP.mult)
            p.tt(xc[i][:], xc[i][:], bc2[:], OP.add)
            p.copy(xb[i][:], xc[i][:])
            p.dma("sync", xn2b_o[128 * i:128 * (i + 1), :], xb[i][:])

        st2.close()
        xbo = [E(nc.sbuf_tensor(f"xbo{i}", [128, TLOC], BF16)) for i in range(8)]
        outq = [E(nc.sbuf_tensor(f"outq{i}", [128, TLOC], F32)) for i in range(8)]
        scr0 = E(nc.sbuf_tensor("scr0", [128, TLOC], F32))
        nsq = [E(nc.sbuf_tensor(f"nsq{i}", [128, TLOC], F32)) for i in range(2)]

        def qk_one(Wd, bd, out_dram):
            emit_quat_load(p, Wd, bd, wpos, wneg, bE_sb, scrcol)
            for oc in range(8):
                o, half = oc // 2, oc % 2
                for c0, cw in col_chunks(TLOC):
                    first = True
                    for ti, (c_, w_, s_) in enumerate(QTERMS[o]):
                        src = wpos if s_ > 0 else wneg
                        for kc in range(2):
                            p.mm(psA[:, 0:cw],
                                 src[:, (w_ * 2 + kc) * 256 + 128 * half:
                                     (w_ * 2 + kc) * 256 + 128 * half + 128],
                                 xb[c_ * 2 + kc][:, c0:c0 + cw],
                                 start=first, stop=(ti == 3 and kc == 1))
                            first = False
                    p.ts(outq[oc][:, c0:c0 + cw], psA[:, 0:cw],
                         bE_sb[:, oc:oc + 1], op0=OP.add)
            for half in range(2):
                p.act(nsq[half][:], outq[half][:], AF.Square)
                for o in range(1, 4):
                    p.act(scr0[:], outq[2 * o + half][:], AF.Square)
                    p.tt(nsq[half][:], nsq[half][:], scr0[:], OP.add)
                p.act(nsq[half][:], nsq[half][:], AF.Ln, bias=1e-6)
                p.act(nsq[half][:], nsq[half][:], AF.Exp, scale=-0.5)
            for oc in range(8):
                p.tt(outq[oc][:], outq[oc][:], nsq[oc % 2][:], OP.mult)
                p.copy(xbo[oc][:], outq[oc][:])
                p.dma("sync", out_dram[128 * oc:128 * (oc + 1), :], xbo[oc][:])

        qk_one(qW, qb, q_f)
        qk_one(kW, kb, k_f)

        emit_quat_load(p, vW, vb, wpos, wneg, bE_sb, scrcol)
        p.dma("sync", bscr, bE_sb[:])
        p.dma("gpsimd",
              bErow[:].rearrange("a (o h pp) -> a o h pp", o=4, h=2),
              bscr.rearrange("p (o h) -> o h p", o=4).unsqueeze(0),
              allow_slow_non_contiguous=True)
        for tc in range(16):
            for half2 in range(2):
                first = True
                for oo in range(2):
                    o = 2 * half2 + oo
                    for (c_, w_, s_) in QTERMS[o]:
                        src = wpos if s_ > 0 else wneg
                        for kc in range(2):
                            p.mm(psV[:, 256 * oo:256 * (oo + 1)],
                                 xb[c_ * 2 + kc][:, 128 * tc:128 * (tc + 1)],
                                 src[:, (w_ * 2 + kc) * 256:(w_ * 2 + kc + 1) * 256],
                                 start=first, stop=False)
                            first = False
                p.mm(psV[:, 0:512], onesb[:], bErow[0:1, 512 * half2:512 * (half2 + 1)],
                     start=False, stop=True)
                p.copy(vout[:], psV[:, 0:512])
                p.dma("sync", v_t[128 * tc:128 * (tc + 1), 512 * half2:512 * (half2 + 1)],
                      vout[:])
        p.emit(block, sems)
    return nc


def run_quat(nc_quat, x1_fm, inp):
    in_maps = []
    for c in range(NCORE):
        in_maps.append({
            "x1": np.ascontiguousarray(x1_fm[:, c * TLOC:(c + 1) * TLOC]),
            "qW": np.asarray(inp["L1_attn_qW"]), "qb": np.asarray(inp["L1_attn_qb"]),
            "kW": np.asarray(inp["L1_attn_kW"]), "kb": np.asarray(inp["L1_attn_kb"]),
            "vW": np.asarray(inp["L1_attn_vW"]), "vb": np.asarray(inp["L1_attn_vb"]),
        })
    res = run_bass_kernel_spmd(nc_quat, in_maps, core_ids=list(range(NCORE)))
    return res.results


TBAT = 8192


def build_attn2():
    nc = bass.Bass("TRN2", target_bir_lowering=False, detect_race_conditions=False)
    q_fd = nc.dram_tensor("q_fd", [D, TLOC], BF16, kind="ExternalInput").ap()
    kf = nc.dram_tensor("kf", [D, TBAT], BF16, kind="ExternalInput").ap()
    vt = nc.dram_tensor("vt", [TBAT, D], BF16, kind="ExternalInput").ap()
    jv = nc.dram_tensor("jv", [1, 1024], F32, kind="ExternalInput").ap()
    oW = nc.dram_tensor("oW", [4, 256, 256], F32, kind="ExternalInput").ap()
    ob = nc.dram_tensor("ob", [4, 256], F32, kind="ExternalInput").ap()
    s2_f = nc.dram_tensor("s2_f", [D, TLOC], F32, kind="ExternalOutput").ap()

    with contextlib.ExitStack() as st:
        E = st.enter_context
        qsb = [E(nc.sbuf_tensor(f"qsb{i}", [128, TLOC], BF16)) for i in range(8)]
        osb = [E(nc.sbuf_tensor(f"osb{i}", [128, TLOC], BF16)) for i in range(8)]
        ksb = [E(nc.sbuf_tensor(f"ksb{i}", [64, TBAT], BF16)) for i in range(2)]
        vsb = [E(nc.sbuf_tensor(f"vsb{i}", [128, 64 * 65], BF16)) for i in range(2)]
        qh = [E(nc.sbuf_tensor(f"qh{i}", [64, TLOC], BF16)) for i in range(2)]
        ptb = [E(nc.sbuf_tensor(f"ptb{i}", [128, 1024], BF16)) for i in range(2)]
        jrow = E(nc.sbuf_tensor("jrow", [1, 1024], F32))
        jcol = E(nc.sbuf_tensor("jcol", [1, 1], F32))
        jb = E(nc.sbuf_tensor("jb", [128, 1], F32))
        rl = E(nc.sbuf_tensor("rl", [1, 512], F32))
        rlb = E(nc.sbuf_tensor("rlb", [1, 512], BF16))
        rb_sb = E(nc.sbuf_tensor("rb_sb", [64, 512], F32))
        ones1 = E(nc.sbuf_tensor("ones1", [1, 128], F32))
        onesb = E(nc.sbuf_tensor("onesb", [1, 128], BF16))
        wpos = E(nc.sbuf_tensor("wpos", [128, 2048], BF16))
        wneg = E(nc.sbuf_tensor("wneg", [128, 2048], BF16))
        bE_sb = E(nc.sbuf_tensor("bE_sb", [128, 8], F32))
        scrcol = E(nc.sbuf_tensor("scrcol", [128, 8], F32))
        scrF = E(nc.sbuf_tensor("scrF", [128, TLOC], F32))
        cpool = E(nc.sbuf_tensor("cpool", [128, 8], F32))
        psS = [E(nc.psum_tensor(f"psS{i}", [128, 1024], F32)) for i in range(2)]
        psO = E(nc.psum_tensor("psO", [65, 512], F32))
        psB = E(nc.psum_tensor("psB", [128, 512], F32))
        sems = make_sems(nc, E)
        block = E(nc.Block())

        p = Prog(same_engine_edges=True)
        cst = Consts(nc, p, cpool)
        cst.reg(-1.0, 1.0, 0.0, 0.125, 0.1 / 1024.0)
        p.memset(ones1[:], 1.0)
        p.memset(onesb[:], 1.0)
        for i in range(8):
            p.dma("sync", qsb[i][:], q_fd[128 * i:128 * (i + 1), :])
        p.dma("sync", jrow[:], jv)
        p.act(jrow[:], jrow[:], AF.Sin)
        p.red(jcol[:], jrow[:], OP.add)
        p.ts(jcol[:], jcol[:], 0.1 / 1024.0, op0=OP.mult)
        p.mm(psB[:, 0:1], ones1[:], jcol[:], start=True, stop=True)
        p.copy(jb[:], psB[:, 0:1])

        for h in range(16):
            par = h % 2
            ks_, vs_, qh_ = ksb[par], vsb[par], qh[par]
            p.dma("sync", ks_[:], kf[64 * h:64 * (h + 1), :])
            p.memset(vs_[:].rearrange("p (kc c) -> p kc c", c=65)[:, :, 64:65], 1.0)
            p.dma("sync", vs_[:].rearrange("p (kc c) -> p kc c", c=65)[:, :, 0:64],
                  vt[:, 64 * h:64 * (h + 1)].rearrange("(kc p) m -> p kc m", p=128))
            hi, hr = h // 2, 64 * (h % 2)
            p.copy(qh_[:], qsb[hi][hr:hr + 64, :])
            for qc in range(4):
                for kk in range(32):
                    ps = psS[kk % 2]
                    pt = ptb[kk % 2]
                    for j in range(2):
                        kc = 2 * kk + j
                        p.mm(ps[:, 512 * j:512 * (j + 1)],
                             ks_[0:64, 128 * kc:128 * (kc + 1)],
                             qh_[:, 512 * qc:512 * (qc + 1)],
                             start=True, stop=True)
                    p.act(pt[:], ps[:], AF.Exp, scale=0.125, bias=jb[:])
                    for j in range(2):
                        kc = 2 * kk + j
                        p.mm(psO[:, 0:512],
                             vs_[:, 65 * kc:65 * (kc + 1)],
                             pt[:, 512 * j:512 * (j + 1)],
                             start=(kk == 0 and j == 0),
                             stop=(kk == 31 and j == 1))
                p.recip(rl[:], psO[64:65, 0:512])
                p.copy(rlb[:], rl[:])
                p.mm(psB[0:64, 0:512], onesb[0:1, 0:64], rlb[0:1, :],
                     start=True, stop=True)
                p.act(rb_sb[:], psB[0:64, 0:512], AF.Copy)
                p.tt(osb[hi][hr:hr + 64, 512 * qc:512 * (qc + 1)],
                     psO[0:64, 0:512], rb_sb[:], OP.mult)

        emit_quat_load(p, oW, ob, wpos, wneg, bE_sb, scrcol)
        for oc in range(8):
            o, half = oc // 2, oc % 2
            for c0, cw in col_chunks(TLOC):
                ps = psS[(c0 // 512) % 2]
                first = True
                for ti, (c_, w_, s_) in enumerate(QTERMS[o]):
                    src = wpos if s_ > 0 else wneg
                    for kc in range(2):
                        p.mm(ps[:, 0:cw],
                             src[:, (w_ * 2 + kc) * 256 + 128 * half:
                                 (w_ * 2 + kc) * 256 + 128 * half + 128],
                             osb[c_ * 2 + kc][:, c0:c0 + cw],
                             start=first, stop=(ti == 3 and kc == 1))
                        first = False
                p.ts(scrF[:, c0:c0 + cw], ps[:, 0:cw], bE_sb[:, oc:oc + 1],
                     op0=OP.add)
            p.dma("sync", s2_f[128 * oc:128 * (oc + 1), :], scrF[:])
        p.emit(block, sems)
    return nc


def run_attn2(nc_a2, quat_results, inp):
    kf_b, vt_b = [], []
    for b in range(2):
        kf_b.append(np.concatenate(
            [np.asarray(quat_results[4 * b + i]["k_f"]) for i in range(4)], axis=1))
        vt_b.append(np.concatenate(
            [np.asarray(quat_results[4 * b + i]["v_t"]) for i in range(4)], axis=0))
    in_maps = []
    for c in range(NCORE):
        b = c // 4
        in_maps.append({
            "q_fd": np.asarray(quat_results[c]["q_f"]),
            "kf": np.ascontiguousarray(kf_b[b]),
            "vt": np.ascontiguousarray(vt_b[b]),
            "jv": np.asarray(inp["L1_attn_J"])[None, :],
            "oW": np.asarray(inp["L1_attn_oW"]), "ob": np.asarray(inp["L1_attn_ob"]),
        })
    res = run_bass_kernel_spmd(nc_a2, in_maps, core_ids=list(range(NCORE)))
    return [np.asarray(r["s2_f"]) for r in res.results]


NEXP = 8


def build_moe():
    nc = bass.Bass("TRN2", target_bir_lowering=False, detect_race_conditions=False)
    xn2b = nc.dram_tensor("xn2b", [D, TLOC], BF16, kind="ExternalInput").ap()
    x1T = nc.dram_tensor("x1T", [TLOC, D], F32, kind="ExternalInput").ap()
    s2T = nc.dram_tensor("s2T", [TLOC, D], F32, kind="ExternalInput").ap()
    wr = nc.dram_tensor("wr", [D, NEXP], F32, kind="ExternalInput").ap()
    br = nc.dram_tensor("br", [NEXP, 1], F32, kind="ExternalInput").ap()
    w1 = nc.dram_tensor("w1", [NEXP, D, 2048], F32, kind="ExternalInput").ap()
    b1 = nc.dram_tensor("b1", [NEXP, 2048], F32, kind="ExternalInput").ap()
    w2 = nc.dram_tensor("w2", [NEXP, 2048, D], F32, kind="ExternalInput").ap()
    b2 = nc.dram_tensor("b2", [NEXP, D], F32, kind="ExternalInput").ap()
    ident = nc.dram_tensor("ident", [128, 128], F32, kind="ExternalInput").ap()
    out_t = nc.dram_tensor("out_t", [TLOC, D], F32, kind="ExternalOutput").ap()

    with contextlib.ExitStack() as st:
        E = st.enter_context
        xn = [E(nc.sbuf_tensor(f"xn{i}", [128, TLOC], BF16)) for i in range(8)]
        hb = [E(nc.sbuf_tensor(f"hb{i}", [128, TLOC], BF16)) for i in range(16)]
        acc = [E(nc.sbuf_tensor(f"acc{i}", [128, D], F32)) for i in range(16)]
        w2all = E(nc.sbuf_tensor("w2all", [128, 16 * D], BF16))
        w1st = [E(nc.sbuf_tensor(f"w1st{i}", [128, 8 * 128], BF16)) for i in range(2)]
        wr_sb = E(nc.sbuf_tensor("wr_sb", [128, 64], BF16))
        br_sb = E(nc.sbuf_tensor("br_sb", [8, 1], F32))
        b1E = E(nc.sbuf_tensor("b1E", [128, 16], F32))
        b2row = E(nc.sbuf_tensor("b2row", [1, D], BF16))
        lg_sb = E(nc.sbuf_tensor("lg_sb", [8, TLOC], BF16))
        idbB = E(nc.sbuf_tensor("idbB", [128, 128], BF16))
        lgT = E(nc.sbuf_tensor("lgT", [128, 8], F32))
        gate = E(nc.sbuf_tensor("gate", [128, 128], F32))
        m1c = E(nc.sbuf_tensor("m1c", [128, 1], F32))
        m2c = E(nc.sbuf_tensor("m2c", [128, 1], F32))
        w1c = E(nc.sbuf_tensor("w1c", [128, 1], F32))
        w2c = E(nc.sbuf_tensor("w2c", [128, 1], F32))
        eq1 = E(nc.sbuf_tensor("eq1", [128, 8], F32))
        eq2 = E(nc.sbuf_tensor("eq2", [128, 8], F32))
        mskd = E(nc.sbuf_tensor("mskd", [128, 8], F32))
        tmp = [E(nc.sbuf_tensor(f"tmp{i}", [128, 512], F32)) for i in range(2)]
        onesb = E(nc.sbuf_tensor("onesb", [1, 128], BF16))
        cpool = E(nc.sbuf_tensor("cpool", [128, 4], F32))
        psA = [E(nc.psum_tensor(f"psA{i}", [128, 512], F32)) for i in range(2)]
        psLG = E(nc.psum_tensor("psLG", [8, TLOC], F32))
        psTR = E(nc.psum_tensor("psTR", [128, 8], BF16))
        sems = make_sems(nc, E)
        block = E(nc.Block())

        p = Prog(same_engine_edges=True)
        cst = Consts(nc, p, cpool)
        cst.reg(-1.0, 1.0, 0.0, NEG)
        p.memset(onesb[:], 1.0)
        for i in range(8):
            p.dma("sync", xn[i][:], xn2b[128 * i:128 * (i + 1), :])
        p.dma("gpsimd", wr_sb[:].rearrange("p (k m) -> p k m", m=8),
              wr.rearrange("(k p) m -> p k m", p=128))
        p.dma("sync", br_sb[:], br)
        p.dma("gpsimd", idbB[:], ident)
        for c0, cw in col_chunks(TLOC):
            for k in range(8):
                p.mm(psLG[:, c0:c0 + cw], wr_sb[:, 8 * k:8 * (k + 1)],
                     xn[k][:, c0:c0 + cw], start=(k == 0), stop=(k == 7))
        p.ts(lg_sb[:], psLG[:], br_sb[:], op0=OP.add)
        for tc in range(16):
            p.transpose(psTR[:], lg_sb[0:8, 128 * tc:128 * (tc + 1)], idbB[0:8, 0:8])
            p.copy(lgT[:], psTR[:])
            p.red(m1c[:], lgT[:], OP.max)
            p.ts(eq1[:], lgT[:], m1c[:], op0=OP.is_equal)
            p.stt(mskd[:], eq1[:], NEG, lgT[:], OP.mult, OP.add)
            p.red(m2c[:], mskd[:], OP.max)
            p.tt(w1c[:], m1c[:], m2c[:], OP.subtract)
            p.act(w1c[:], w1c[:], AF.Sigmoid)
            p.ts(w2c[:], w1c[:], -1.0, 1.0, op0=OP.mult, op1=OP.add)
            p.ts(eq2[:], lgT[:], m2c[:], op0=OP.is_equal)
            p.ts(eq1[:], eq1[:], w1c[:], op0=OP.mult)
            p.ts(eq2[:], eq2[:], w2c[:], op0=OP.mult)
            p.tt(gate[:, 8 * tc:8 * (tc + 1)], eq1[:], eq2[:], OP.add)
        for e in range(NEXP):
            p.dma("sync", b1E[:].rearrange("p (i) -> p i"),
                  b1[e].rearrange("(i p) -> p i", p=128),
                  allow_slow_non_contiguous=True)
            p.dma("gpsimd", b2row[:], b2[e:e + 1, :])
            for mc in range(16):
                ws_ = w1st[mc % 2]
                p.dma("gpsimd", ws_[:].rearrange("p (k m) -> p k m", m=128),
                      w1[e, :, 128 * mc:128 * (mc + 1)].rearrange(
                          "(k p) m -> p k m", p=128))
                p.dma("gpsimd", w2all[:, D * mc:D * (mc + 1)],
                      w2[e, 128 * mc:128 * (mc + 1), :])
                for ci, (c0, cw) in enumerate(col_chunks(TLOC)):
                    ps = psA[ci % 2]
                    for k in range(8):
                        p.mm(ps[:, 0:cw], ws_[:, 128 * k:128 * (k + 1)],
                             xn[k][:, c0:c0 + cw], start=(k == 0), stop=(k == 7))
                    p.act(hb[mc][:, c0:c0 + cw], ps[:, 0:cw], AF.Relu,
                          bias=b1E[:, mc:mc + 1])
            for tc in range(16):
                gcol = gate[:, 8 * tc + e:8 * tc + e + 1]
                for half in range(2):
                    ps = psA[(2 * tc + half) % 2]
                    tm_ = tmp[(2 * tc + half) % 2]
                    for mc in range(16):
                        p.mm(ps[:, 0:512], hb[mc][:, 128 * tc:128 * (tc + 1)],
                             w2all[:, D * mc + 512 * half:D * mc + 512 * (half + 1)],
                             start=(mc == 0), stop=False)
                    p.mm(ps[:, 0:512], onesb[:],
                         b2row[0:1, 512 * half:512 * (half + 1)],
                         start=False, stop=True)
                    if e == 0:
                        p.ts(acc[tc][:, 512 * half:512 * (half + 1)], ps[:, 0:512],
                             gcol, op0=OP.mult)
                    else:
                        p.ts(tm_[:], ps[:, 0:512], gcol, op0=OP.mult)
                        p.tt(acc[tc][:, 512 * half:512 * (half + 1)],
                             acc[tc][:, 512 * half:512 * (half + 1)], tm_[:], OP.add)
        for tc in range(16):
            for half in range(2):
                hs = slice(512 * half, 512 * (half + 1))
                tm_ = tmp[half]
                sc_ = tmp[1 - half]
                p.act(sc_[:], acc[tc][:, hs], AF.Silu)
                p.dma("sync", tm_[:], s2T[128 * tc:128 * (tc + 1), hs])
                p.tt(sc_[:], sc_[:], tm_[:], OP.mult)
                p.dma("sync", tm_[:], x1T[128 * tc:128 * (tc + 1), hs])
                p.tt(sc_[:], sc_[:], tm_[:], OP.add)
                p.dma("sync", out_t[128 * tc:128 * (tc + 1), hs], sc_[:])
        p.emit(block, sems)
    return nc


def run_moe(nc_moe, quat_results, x1_fm, s2_list, inp):
    in_maps = []
    for c in range(NCORE):
        t0 = c * TLOC
        in_maps.append({
            "xn2b": np.asarray(quat_results[c]["xn2b_o"]),
            "x1T": np.ascontiguousarray(x1_fm[:, t0:t0 + TLOC].T),
            "s2T": np.ascontiguousarray(np.asarray(s2_list[c]).T),
            "wr": np.asarray(inp["L1_ffn_Wr"]),
            "br": np.asarray(inp["L1_ffn_br"])[:, None],
            "w1": np.asarray(inp["L1_ffn_W1"]), "b1": np.asarray(inp["L1_ffn_b1"]),
            "w2": np.asarray(inp["L1_ffn_W2"]), "b2": np.asarray(inp["L1_ffn_b2"]),
            "ident": np.eye(128, dtype=np.float32),
        })
    res = run_bass_kernel_spmd(nc_moe, in_maps, core_ids=list(range(NCORE)))
    return [np.asarray(r["out_t"]) for r in res.results]


_NC_CACHE = {}


def _get(name, builder):
    if name not in _NC_CACHE:
        _NC_CACHE[name] = builder()
    return _NC_CACHE[name]


def kernel(**inp):
    x = np.asarray(inp["L0_attn_x"], np.float32)
    xT = np.ascontiguousarray(x.reshape(16384, D).T)

    wm = {k: np.asarray(inp[f"L0_attn_{k}"], np.float32) for k in
          ["rms_w", "in_proj", "conv_w", "conv_b", "x_proj", "dt_w", "dt_b"]}
    nc_pre = _get("pre", build_mamba_pre)
    xhs = make_xh(xT)
    in_maps = [{"xh": xhs[c], **wm} for c in range(NCORE)]
    pre = run_bass_kernel_spmd(nc_pre, in_maps, core_ids=list(range(NCORE))).results

    y_fm = run_scan_cached(pre, np.asarray(inp["L0_attn_A_log"], np.float32),
                           np.asarray(inp["L0_attn_D"], np.float32))

    x_fm = np.concatenate([np.asarray(r["xn_f"]) for r in pre], axis=1)
    consts = [attn_consts(c) for c in range(NCORE)]
    nc_attn = _get("attn", build_attn)
    nc_ffn = _get("ffn", build_ffn)
    for i in range(6):
        x_fm = run_attn(nc_attn, x_fm,
                        np.asarray(inp["L0_ffn_Wqkv"][i], np.float32),
                        np.asarray(inp["L0_ffn_q_scale"][i], np.float32),
                        np.asarray(inp["L0_ffn_k_scale"][i], np.float32),
                        np.asarray(inp["L0_ffn_Wo"][i], np.float32), consts)
        x_fm = run_ffn(nc_ffn, x_fm,
                       np.asarray(inp["L0_ffn_W1"][i], np.float32),
                       np.asarray(inp["L0_ffn_b1"][i], np.float32),
                       np.asarray(inp["L0_ffn_W2"][i], np.float32),
                       np.asarray(inp["L0_ffn_b2"][i], np.float32),
                       np.asarray(inp["L0_ffn_ln_g"][i], np.float32),
                       np.asarray(inp["L0_ffn_ln_b"][i], np.float32))

    x1_fm = run_mgate_cached(y_fm, pre,
                             np.asarray(inp["L0_attn_out_proj"], np.float32),
                             xT, x_fm)

    nc_quat = _get("quat", build_quat)
    quat = run_quat(nc_quat, x1_fm, inp)
    nc_a2 = _get("a2", build_attn2)
    s2 = run_attn2(nc_a2, quat, inp)
    nc_moe = _get("moe", build_moe)
    outs = run_moe(nc_moe, quat, x1_fm, s2, inp)
    out = np.concatenate(outs, axis=0).reshape(2, 8192, D).astype(np.float32)
    return out


def run_scan_cached(pre_results, a_log_full, d_full):
    delta_fm = np.concatenate([np.asarray(r["delta_f"]) for r in pre_results], axis=1)
    xi_fm = np.concatenate([np.asarray(r["xi_f"]) for r in pre_results], axis=1)
    bTf = np.concatenate([np.asarray(r["bT"]) for r in pre_results], axis=1)
    cTf = np.concatenate([np.asarray(r["cT"]) for r in pre_results], axis=1)
    nc = _get("scan", build_scan)
    in_maps = []
    for c in range(NCORE):
        r0 = c * 256
        in_maps.append({
            "deltaS": np.ascontiguousarray(delta_fm[r0:r0 + 256]),
            "xiS": np.ascontiguousarray(xi_fm[r0:r0 + 256]),
            "bTf": bTf, "cTf": cTf,
            "a_log": np.ascontiguousarray(a_log_full[r0:r0 + 256]),
            "d_vec": np.ascontiguousarray(d_full[r0:r0 + 256]),
        })
    res = run_bass_kernel_spmd(nc, in_maps, core_ids=list(range(NCORE)))
    return np.concatenate([np.asarray(r["yS"]) for r in res.results], axis=0)


def run_mgate_cached(y_fm, pre_results, out_proj, xT, g_fm):
    nc = _get("mgate", build_mgate)
    in_maps = []
    for c in range(NCORE):
        t0 = c * TLOC
        in_maps.append({
            "yT": np.ascontiguousarray(y_fm[:, t0:t0 + TLOC]),
            "z_f": np.asarray(pre_results[c]["z_f"]),
            "xn_f": np.asarray(pre_results[c]["xn_f"]),
            "out_proj": out_proj,
            "xT": np.ascontiguousarray(xT[:, t0:t0 + TLOC]),
            "g_f": np.ascontiguousarray(g_fm[:, t0:t0 + TLOC]),
        })
    res = run_bass_kernel_spmd(nc, in_maps, core_ids=list(range(NCORE)))
    return np.concatenate([np.asarray(r["x1_f"]) for r in res.results], axis=1)


# version 11
# speedup vs baseline: 1.2031x; 1.2031x over previous
import contextlib
import numpy as np
import ml_dtypes
import concourse.bass as bass
import concourse.mybir as mybir
from concourse.bass_utils import run_bass_kernel_spmd
from concourse.alu_op_type import AluOpType as OP

F32 = mybir.dt.float32
BF16 = mybir.dt.bfloat16
AF = mybir.ActivationFunctionType
AX = mybir.AxisListType

NCORE = 8
TLOC = 2048
D = 1024
DIN = 2048
DST = 16
DTR = 64
NEG = -1.0e30

BF = ml_dtypes.bfloat16


def _ap_name(x):
    try:
        return x.tensor.name
    except AttributeError:
        return None


def _is_ap(x):
    return hasattr(x, "tensor")


class Prog:

    def __init__(self, same_engine_edges=False):
        self.ops = []
        self.const_tile = None
        self.psum_names = set()
        self.same_engine_edges = same_engine_edges

    def _rec(self, eng, method, args, kwargs, dma, reads, writes):
        for a in list(reads) + list(writes):
            if not isinstance(a, str) and hasattr(a, "space"):
                try:
                    if "PSUM" in str(a.space):
                        self.psum_names.add(a.tensor.name)
                except Exception:
                    pass
        rd = set(a if isinstance(a, str) else _ap_name(a) for a in reads)
        wr = set(a if isinstance(a, str) else _ap_name(a) for a in writes)
        rd.discard(None); wr.discard(None)
        self.ops.append((eng, method, args, kwargs, dma, rd, wr))

    def _const_read(self, reads, scalars):
        if self.const_tile is not None and any(
                isinstance(s, float) for s in scalars):
            reads = list(reads) + [self.const_tile]
        return reads

    def dma(self, eng, out, in_, **kw):
        self._rec(eng, "dma_start", (), {"out": out, "in_": in_, **kw},
                  True, [in_], [out])

    def mm(self, out, lhsT, rhs, **kw):
        self._rec("tensor", "matmul", (out, lhsT, rhs), kw, False,
                  [lhsT, rhs, out], [out])

    def transpose(self, out, in_, ident):
        self._rec("tensor", "transpose", (out, in_, ident), {}, False,
                  [in_, ident], [out])

    def act(self, out, in_, func, bias=None, scale=None, accum_out=None):
        kw = {}
        reads = [in_]
        writes = [out]
        if bias is not None:
            kw["bias"] = bias
            (reads if _is_ap(bias) else []).append(bias)
        if scale is not None:
            kw["scale"] = scale
            (reads if _is_ap(scale) else []).append(scale)
        if accum_out is not None:
            kw["accum_out"] = accum_out
            writes.append(accum_out)
        if self.const_tile is not None:
            reads = list(reads) + [self.const_tile]
        self._rec("scalar", "activation", (out, in_, func), kw, False,
                  reads, writes)

    def tt(self, out, a, b, op, eng="vector"):
        self._rec(eng, "tensor_tensor", (out, a, b), {"op": op}, False,
                  [a, b], [out])

    def ts(self, out, in0, s1, s2=None, op0=OP.mult, op1=None, eng="vector",
           accum_out=None):
        reads = [in0] + [s for s in (s1, s2) if _is_ap(s)]
        reads = self._const_read(reads, [s for s in (s1, s2)
                                         if not _is_ap(s) and s is not None])
        writes = [out] + ([accum_out] if accum_out is not None else [])
        if op1 is None:
            self._rec(eng, "tensor_scalar", (out, in0, s1, None, op0),
                      {"accum_out": accum_out}, False, reads, writes)
        else:
            self._rec(eng, "tensor_scalar", (out, in0, s1, s2, op0, op1),
                      {"accum_out": accum_out}, False, reads, writes)

    def stt(self, out, in0, scalar, in1, op0, op1, eng="vector"):
        reads = [in0, in1] + ([scalar] if _is_ap(scalar) else [])
        reads = self._const_read(reads, [scalar] if not _is_ap(scalar) else [])
        self._rec(eng, "scalar_tensor_tensor", (out, in0, scalar, in1, op0, op1),
                  {}, False, reads, writes=[out])

    def copy(self, out, in_, eng="vector"):
        self._rec(eng, "tensor_copy", (out, in_), {}, False, [in_], [out])

    def ttr(self, out, in0, in1, scalar, op0, op1, accum_out):
        reads = [in0, in1]
        reads = self._const_read(reads, [scalar] if not _is_ap(scalar) else [])
        self._rec("vector", "tensor_tensor_reduce", (out, in0, in1, 1.0, scalar,
                  op0, op1, accum_out), {}, False, reads, [out, accum_out])

    def red(self, out, in_, op, negate=None):
        self._rec("vector", "tensor_reduce", (out, in_, AX.X, op),
                  {"negate": negate}, False, [in_], [out])

    def recip(self, out, in_):
        self._rec("vector", "reciprocal", (out, in_), {}, False, [in_], [out])

    def scan(self, out, a, u, initial):
        reads = [a, u] + ([initial] if _is_ap(initial) else [])
        self._rec("vector", "tensor_tensor_scan",
                  (out, a, u, initial, OP.mult, OP.add), {}, False,
                  reads, [out])

    def memset(self, ap, val, eng="vector"):
        self._rec(eng, "memset", (ap, val), {}, False, [], [ap])

    TRUNC = None

    def emit(self, block, sems):
        if Prog.TRUNC is not None:
            self.ops = self.ops[:Prog.TRUNC]
        n = len(self.ops)

        last_writer = {}
        readers = {}
        deps = [set() for _ in range(n)]
        psum = self.psum_names
        for i, (eng, _m, _a, _k, dma, rd, wr) in enumerate(self.ops):
            for t in rd:
                if t in last_writer:
                    deps[i].add(last_writer[t])
            for t in wr:
                for r in readers.get(t, ()):
                    deps[i].add(r)
                if t in last_writer:
                    deps[i].add(last_writer[t])
            for t in wr:
                last_writer[t] = i
                readers[t] = []
            for t in rd:
                readers.setdefault(t, []).append(i)
                if t in psum:
                    last_writer[t] = i
                    readers[t] = []
        prev_dma = {}
        for i, (eng, _m, _a, _k, dma, _r, _w) in enumerate(self.ops):
            if dma:
                if eng in prev_dma:
                    deps[i].add(prev_dma[eng])
                prev_dma[eng] = i

        kept = [[] for _ in range(n)]
        need_inc = [False] * n
        for i, (eng, _m, _a, _k, dma, _r, _w) in enumerate(self.ops):
            if dma:
                need_inc[i] = True
            for j in deps[i]:
                jeng, jdma = self.ops[j][0], self.ops[j][4]
                if (not jdma) and jeng == eng and not (
                        self.same_engine_edges and eng != "tensor"):
                    continue
                kept[i].append(j)
                need_inc[j] = True

        sem_key = [None] * n
        cumval = [0] * n
        counters = {}
        for i, (eng, _m, _a, _k, dma, _r, _w) in enumerate(self.ops):
            key = ("dma_" + eng) if dma else eng
            sem_key[i] = key
            if need_inc[i]:
                counters[key] = counters.get(key, 0) + (16 if dma else 1)
                cumval[i] = counters[key]
        totals = dict(counters)

        reqs = [dict() for _ in range(n)]
        for i in range(n):
            for j in kept[i]:
                k, v = sem_key[j], cumval[j]
                if reqs[i].get(k, 0) < v:
                    reqs[i][k] = v

        ops = self.ops

        def engine_body(name):
            def body(eng_obj):
                guaranteed = {}
                for i, (ename, method, args, kwargs, dma, _r, _w) in enumerate(ops):
                    if ename != name:
                        continue
                    for k, v in sorted(reqs[i].items()):
                        if guaranteed.get(k, 0) >= v:
                            continue
                        eng_obj.wait_ge(sems[k], v)
                        guaranteed[k] = v
                    kwargs2 = {k: v for k, v in kwargs.items() if v is not None} \
                        if method in ("activation", "tensor_scalar") else kwargs
                    inst = getattr(eng_obj, method)(*args, **kwargs2)
                    if need_inc[i]:
                        inst.then_inc(sems[sem_key[i]], 16 if dma else 1)
                for k, v in totals.items():
                    eng_obj.wait_ge(sems[k], v)
            return body

        block.sync(engine_body("sync"))
        block.gpsimd(engine_body("gpsimd"))
        block.tensor(engine_body("tensor"))
        block.vector(engine_body("vector"))
        block.scalar(engine_body("scalar"))


def make_sems(nc, E):
    return {k: E(nc.semaphore(f"sem_{k}")) for k in
            ["tensor", "vector", "scalar", "gpsimd", "sync",
             "dma_sync", "dma_gpsimd", "dma_scalar"]}


def col_chunks(cols, nch=512):
    return [(c, min(nch, cols - c)) for c in range(0, cols, nch)]


class Consts:

    def __init__(self, nc, p, tile, dtype=F32):
        self.nc, self.p, self.tile, self.dtype = nc, p, tile, dtype
        self.n = 0
        self.seen = {}
        p.const_tile = tile[:].tensor.name

    def reg(self, *vals):
        for v in vals:
            v = float(v)
            if v in self.seen:
                continue
            ap = self.tile[:, self.n:self.n + 1]
            self.nc.const_aps.register(self.dtype, v, ap)
            self.p.memset(ap, v)
            self.seen[v] = ap
            self.n += 1


HALO1 = 128
XCOLS = HALO1 + TLOC


def build_mamba_pre():
    nc = bass.Bass("TRN2", target_bir_lowering=False, detect_race_conditions=False)
    xh = nc.dram_tensor("xh", [D, XCOLS], F32, kind="ExternalInput").ap()
    rms_w = nc.dram_tensor("rms_w", [D], F32, kind="ExternalInput").ap()
    in_proj = nc.dram_tensor("in_proj", [D, 2 * DIN], F32, kind="ExternalInput").ap()
    conv_w = nc.dram_tensor("conv_w", [DIN, 4], F32, kind="ExternalInput").ap()
    conv_b = nc.dram_tensor("conv_b", [DIN], F32, kind="ExternalInput").ap()
    x_proj = nc.dram_tensor("x_proj", [DIN, 96], F32, kind="ExternalInput").ap()
    dt_w = nc.dram_tensor("dt_w", [DTR, DIN], F32, kind="ExternalInput").ap()
    dt_b = nc.dram_tensor("dt_b", [DIN], F32, kind="ExternalInput").ap()

    xn_f = nc.dram_tensor("xn_f", [D, TLOC], F32, kind="ExternalOutput").ap()
    z_f = nc.dram_tensor("z_f", [DIN, TLOC], F32, kind="ExternalOutput").ap()
    xi_f = nc.dram_tensor("xi_f", [DIN, TLOC], F32, kind="ExternalOutput").ap()
    delta_f = nc.dram_tensor("delta_f", [DIN, TLOC], F32, kind="ExternalOutput").ap()
    bT = nc.dram_tensor("bT", [DST, TLOC], F32, kind="ExternalOutput").ap()
    cT = nc.dram_tensor("cT", [DST, TLOC], F32, kind="ExternalOutput").ap()

    with contextlib.ExitStack() as st:
        E = st.enter_context
        xc = [E(nc.sbuf_tensor(f"xc{i}", [128, XCOLS], F32)) for i in range(8)]
        xmb = [E(nc.sbuf_tensor(f"xmb{i}", [128, XCOLS], BF16)) for i in range(8)]
        sq = E(nc.sbuf_tensor("sq", [128, 512], F32))
        rows = E(nc.sbuf_tensor("rows", [1, XCOLS], F32))
        row2 = E(nc.sbuf_tensor("row2", [1, XCOLS], F32))
        crow = E(nc.sbuf_tensor("crow", [1, XCOLS], F32))
        ones1 = E(nc.sbuf_tensor("ones1", [1, 128], F32))
        ones1c = E(nc.sbuf_tensor("ones1c", [128, 1], F32))
        bc1 = E(nc.sbuf_tensor("bc1", [128, XCOLS], F32))
        bc2 = E(nc.sbuf_tensor("bc2", [128, XCOLS], F32))
        rmsw_sb = E(nc.sbuf_tensor("rmsw_sb", [128, 8], F32))
        wstage = [E(nc.sbuf_tensor(f"wstage{i}", [128, 1024], BF16)) for i in range(2)]
        cw_sb = E(nc.sbuf_tensor("cw_sb", [128, 64], F32))
        cb_sb = E(nc.sbuf_tensor("cb_sb", [128, 16], F32))
        dtb_sb = E(nc.sbuf_tensor("dtb_sb", [128, 16], F32))
        xp_sb = E(nc.sbuf_tensor("xp_sb", [128, 16 * 96], BF16))
        dtw_sb = E(nc.sbuf_tensor("dtw_sb", [64, DIN], BF16))
        xi_raw = E(nc.sbuf_tensor("xi_raw", [128, XCOLS], F32))
        xi_acc = E(nc.sbuf_tensor("xi_acc", [128, TLOC], F32))
        xib = E(nc.sbuf_tensor("xib", [128, TLOC], BF16))
        dbc_sb = E(nc.sbuf_tensor("dbc_sb", [96, TLOC], F32))
        dtlb = E(nc.sbuf_tensor("dtlb", [64, TLOC], BF16))
        sp1 = E(nc.sbuf_tensor("sp1", [128, TLOC], F32))
        sp2 = E(nc.sbuf_tensor("sp2", [128, TLOC], F32))
        ps_a_r = [E(nc.psum_tensor(f"ps_a{i}", [128, 512], F32)) for i in range(2)]
        ps_r1 = E(nc.psum_tensor("ps_r1", [1, 512], F32))
        ps_r2 = ps_r1
        ps_bc = E(nc.psum_tensor("ps_bc", [128, 512], F32))
        ps_dbc = E(nc.psum_tensor("ps_dbc", [96, TLOC], F32))
        cpool = E(nc.sbuf_tensor("cpool", [128, 8], F32))
        sems = make_sems(nc, E)
        block = E(nc.Block())

        p = Prog()
        cst = Consts(nc, p, cpool)
        cst.reg(1e-5, 1.0 / D, -0.5, -1.0, 1.0, 0.0)
        for i in range(8):
            p.dma("sync", xc[i][:], xh[128 * i:128 * (i + 1), :])
        p.dma("sync", rmsw_sb[:], rms_w.rearrange("(i p) -> p i", p=128),
              allow_slow_non_contiguous=True)
        p.dma("sync", cw_sb[:].rearrange("p (i k) -> p i k", k=4),
              conv_w.rearrange("(i p) k -> p i k", p=128))
        p.dma("sync", cb_sb[:], conv_b.rearrange("(i p) -> p i", p=128),
              allow_slow_non_contiguous=True)
        p.dma("sync", dtb_sb[:], dt_b.rearrange("(i p) -> p i", p=128),
              allow_slow_non_contiguous=True)
        p.dma("gpsimd", xp_sb[:].rearrange("p (i j) -> p i j", j=96),
              x_proj.rearrange("(i p) j -> p i j", p=128))
        p.dma("gpsimd", dtw_sb[:], dt_w)
        p.memset(ones1[:], 1.0)
        p.memset(ones1c[:], 1.0)

        for c0, cw in col_chunks(XCOLS):
            for i in range(8):
                p.mm(ps_r1[:, 0:cw], ones1c[:], xc[i][:, c0:c0 + cw],
                     start=(i == 0), stop=(i == 7))
            p.copy(rows[:, c0:c0 + cw], ps_r1[:, 0:cw])
            for i in range(8):
                p.act(sq[:, 0:cw], xc[i][:, c0:c0 + cw], AF.Square)
                p.mm(ps_r2[:, 0:cw], ones1c[:], sq[:, 0:cw],
                     start=(i == 0), stop=(i == 7))
            p.copy(row2[:, c0:c0 + cw], ps_r2[:, 0:cw])
        p.ts(rows[:], rows[:], 1.0 / D, op0=OP.mult)
        p.act(crow[:], rows[:], AF.Square)
        p.ts(row2[:], row2[:], 1.0 / D, op0=OP.mult)
        p.tt(crow[:], row2[:], crow[:], OP.subtract)
        p.act(crow[:], crow[:], AF.Ln, bias=1e-5)
        p.act(crow[:], crow[:], AF.Exp, scale=-0.5)
        for c0, cw in col_chunks(XCOLS):
            p.mm(ps_bc[:, 0:cw], ones1[:], crow[:, c0:c0 + cw], start=True, stop=True)
            p.copy(bc1[:, c0:c0 + cw], ps_bc[:, 0:cw])
        p.tt(crow[:], rows[:], crow[:], OP.mult)
        p.ts(crow[:], crow[:], -1.0, op0=OP.mult)
        for c0, cw in col_chunks(XCOLS):
            p.mm(ps_bc[:, 0:cw], ones1[:], crow[:, c0:c0 + cw], start=True, stop=True)
            p.copy(bc2[:, c0:c0 + cw], ps_bc[:, 0:cw])
        for i in range(8):
            p.tt(xc[i][:], xc[i][:], bc1[:], OP.mult)
            p.tt(xc[i][:], xc[i][:], bc2[:], OP.add)
            p.dma("sync", xn_f[128 * i:128 * (i + 1), :], xc[i][:, HALO1:XCOLS])

        for c0, cw in col_chunks(XCOLS):
            for i in range(8):
                p.act(sq[:, 0:cw], xc[i][:, c0:c0 + cw], AF.Square)
                p.mm(ps_r1[:, 0:cw], ones1c[:], sq[:, 0:cw],
                     start=(i == 0), stop=(i == 7))
            p.copy(rows[:, c0:c0 + cw], ps_r1[:, 0:cw])
        p.act(rows[:], rows[:], AF.Ln, scale=1.0 / D, bias=1e-5)
        p.act(rows[:], rows[:], AF.Exp, scale=-0.5)
        for c0, cw in col_chunks(XCOLS):
            p.mm(ps_bc[:, 0:cw], ones1[:], rows[:, c0:c0 + cw], start=True, stop=True)
            p.copy(bc1[:, c0:c0 + cw], ps_bc[:, 0:cw])
        for i in range(8):
            p.stt(xmb[i][:], xc[i][:], rmsw_sb[:, i:i + 1], bc1[:], OP.mult, OP.mult)

        for m in range(16):
            w = wstage[m % 2]
            p.dma("gpsimd", w[:],
                  in_proj[:, 128 * m:128 * (m + 1)].rearrange("(k p) m -> p k m", p=128))
            for ci, (c0, cw) in enumerate(col_chunks(XCOLS)):
                ps = ps_a_r[ci % 2]
                for k in range(8):
                    p.mm(ps[:, 0:cw], w[:, 128 * k:128 * (k + 1)],
                         xmb[k][:, c0:c0 + cw], start=(k == 0), stop=(k == 7))
                p.act(xi_raw[:, c0:c0 + cw], ps[:, 0:cw], AF.Copy)
            p.ts(xi_acc[:], xi_raw[:, HALO1 - 3:HALO1 - 3 + TLOC],
                 cw_sb[:, 4 * m:4 * m + 1], op0=OP.mult)
            for kk in range(1, 4):
                p.stt(xi_acc[:], xi_raw[:, HALO1 - 3 + kk:HALO1 - 3 + kk + TLOC],
                      cw_sb[:, 4 * m + kk:4 * m + kk + 1], xi_acc[:],
                      OP.mult, OP.add)
            p.act(xi_acc[:], xi_acc[:], AF.Silu, bias=cb_sb[:, m:m + 1])
            p.dma("sync", xi_f[128 * m:128 * (m + 1), :], xi_acc[:])
            p.copy(xib[:], xi_acc[:])
            for c0, cw in col_chunks(TLOC):
                p.mm(ps_dbc[:, c0:c0 + cw], xp_sb[:, 96 * m:96 * (m + 1)],
                     xib[:, c0:c0 + cw], start=(m == 0), stop=(m == 15))

        for m in range(16, 32):
            w = wstage[m % 2]
            p.dma("gpsimd", w[:],
                  in_proj[:, 128 * m:128 * (m + 1)].rearrange("(k p) m -> p k m", p=128))
            for ci, (c0, cw) in enumerate(col_chunks(TLOC)):
                ps = ps_a_r[ci % 2]
                for k in range(8):
                    p.mm(ps[:, 0:cw], w[:, 128 * k:128 * (k + 1)],
                         xmb[k][:, HALO1 + c0:HALO1 + c0 + cw],
                         start=(k == 0), stop=(k == 7))
                p.act(sp1[:, c0:c0 + cw], ps[:, 0:cw], AF.Copy)
            p.dma("sync", z_f[128 * (m - 16):128 * (m - 15), :], sp1[:])

        p.act(dbc_sb[:], ps_dbc[:], AF.Copy)
        p.dma("sync", bT, dbc_sb[64:80, :])
        p.dma("sync", cT, dbc_sb[80:96, :])
        p.copy(dtlb[:], dbc_sb[0:64, :])
        for dm in range(16):
            for ci, (c0, cw) in enumerate(col_chunks(TLOC)):
                ps = ps_a_r[ci % 2]
                p.mm(ps[:, 0:cw], dtw_sb[:, 128 * dm:128 * (dm + 1)],
                     dtlb[:, c0:c0 + cw], start=True, stop=True)
                p.act(sp1[:, c0:c0 + cw], ps[:, 0:cw], AF.Abs,
                      bias=dtb_sb[:, dm:dm + 1])
                p.act(sp1[:, c0:c0 + cw], sp1[:, c0:c0 + cw], AF.Exp, scale=-1.0)
                p.act(sp1[:, c0:c0 + cw], sp1[:, c0:c0 + cw], AF.Ln, bias=1.0)
                p.ts(sp2[:, c0:c0 + cw], ps[:, 0:cw], dtb_sb[:, dm:dm + 1], 0.0,
                     op0=OP.add, op1=OP.max)
                p.tt(sp2[:, c0:c0 + cw], sp2[:, c0:c0 + cw], sp1[:, c0:c0 + cw],
                     OP.add)
            p.dma("sync", delta_f[128 * dm:128 * (dm + 1), :], sp2[:])

        p.emit(block, sems)
    return nc


def shard_tokens(a_t):
    return [a_t[c * TLOC:(c + 1) * TLOC] for c in range(NCORE)]


def make_xh(xT):
    outs = []
    for c in range(NCORE):
        t0 = c * TLOC
        if c % 4 == 0:
            halo = np.zeros((D, HALO1), np.float32)
        else:
            halo = xT[:, t0 - HALO1:t0]
        outs.append(np.ascontiguousarray(
            np.concatenate([halo, xT[:, t0:t0 + TLOC]], axis=1)))
    return outs


def run_mamba_pre(x, w):
    xT = np.ascontiguousarray(x.reshape(16384, D).T)
    xhs = make_xh(xT)
    nc = build_mamba_pre()
    in_maps = []
    for c in range(NCORE):
        in_maps.append({
            "xh": xhs[c],
            "rms_w": w["rms_w"], "in_proj": w["in_proj"],
            "conv_w": w["conv_w"], "conv_b": w["conv_b"],
            "x_proj": w["x_proj"], "dt_w": w["dt_w"], "dt_b": w["dt_b"],
        })
    res = run_bass_kernel_spmd(nc, in_maps, core_ids=list(range(NCORE)))
    return res.results


TFULL = 16384
TCH = 4096


def build_scan():
    nc = bass.Bass("TRN2", target_bir_lowering=False, detect_race_conditions=False)
    deltaS = nc.dram_tensor("deltaS", [256, TFULL], F32, kind="ExternalInput").ap()
    xiS = nc.dram_tensor("xiS", [256, TFULL], F32, kind="ExternalInput").ap()
    bTf = nc.dram_tensor("bTf", [DST, TFULL], F32, kind="ExternalInput").ap()
    cTf = nc.dram_tensor("cTf", [DST, TFULL], F32, kind="ExternalInput").ap()
    a_log = nc.dram_tensor("a_log", [256, DST], F32, kind="ExternalInput").ap()
    d_vec = nc.dram_tensor("d_vec", [256], F32, kind="ExternalInput").ap()
    yS = nc.dram_tensor("yS", [256, TFULL], F32, kind="ExternalOutput").ap()

    with contextlib.ExitStack() as st:
        E = st.enter_context
        dl = E(nc.sbuf_tensor("dl", [128, TCH], F32))
        xi = E(nc.sbuf_tensor("xi", [128, TCH], F32))
        wv = E(nc.sbuf_tensor("wv", [128, TCH], F32))
        av = E(nc.sbuf_tensor("av", [128, TCH], F32))
        uv = E(nc.sbuf_tensor("uv", [128, TCH], F32))
        hv = E(nc.sbuf_tensor("hv", [128, TCH], F32))
        yv = E(nc.sbuf_tensor("yv", [128, TCH], F32))
        bbc = [E(nc.sbuf_tensor(f"bbc{i}", [128, TCH], F32)) for i in range(2)]
        cbc = [E(nc.sbuf_tensor(f"cbc{i}", [128, TCH], F32)) for i in range(2)]
        a_sb = E(nc.sbuf_tensor("a_sb", [128, DST], F32))
        d_sb = E(nc.sbuf_tensor("d_sb", [128, 2], F32))
        carry = E(nc.sbuf_tensor("carry", [128, DST], F32))
        cpool = E(nc.sbuf_tensor("cpool", [128, 4], F32))
        sems = make_sems(nc, E)
        block = E(nc.Block())

        p = Prog(same_engine_edges=True)
        cst = Consts(nc, p, cpool)
        cst.reg(-1.0, 1.0, 0.0)
        p.dma("sync", d_sb[:], d_vec.rearrange("(i p) -> p i", p=128),
              allow_slow_non_contiguous=True)
        for blk in range(2):
            p.dma("sync", a_sb[:], a_log[128 * blk:128 * (blk + 1), :])
            p.act(a_sb[:], a_sb[:], AF.Exp)
            p.ts(a_sb[:], a_sb[:], -1.0, op0=OP.mult)
            for tc in range(4):
                t0 = tc * TCH
                p.dma("sync", dl[:], deltaS[128 * blk:128 * (blk + 1), t0:t0 + TCH])
                p.dma("sync", xi[:], xiS[128 * blk:128 * (blk + 1), t0:t0 + TCH])
                p.tt(wv[:], dl[:], xi[:], OP.mult)
                p.ts(yv[:], xi[:], d_sb[:, blk:blk + 1], op0=OP.mult)
                for s in range(DST):
                    bb, cb = bbc[s % 2], cbc[s % 2]
                    p.dma("gpsimd", bb[:],
                          bTf[s:s + 1, t0:t0 + TCH].broadcast_to([128, TCH]))
                    p.dma("sync", cb[:],
                          cTf[s:s + 1, t0:t0 + TCH].broadcast_to([128, TCH]))
                    p.act(av[:], dl[:], AF.Exp, scale=a_sb[:, s:s + 1])
                    p.tt(uv[:], wv[:], bb[:], OP.mult)
                    init = 0.0 if tc % 2 == 0 else carry[:, s:s + 1]
                    p.scan(hv[:], av[:], uv[:], init)
                    p.copy(carry[:, s:s + 1], hv[:, TCH - 1:TCH])
                    p.tt(av[:], hv[:], cb[:], OP.mult)
                    p.tt(yv[:], yv[:], av[:], OP.add)
                p.dma("sync", yS[128 * blk:128 * (blk + 1), t0:t0 + TCH], yv[:])
        p.emit(block, sems)
    return nc


def run_scan(pre_results, a_log_full, d_full):
    delta_fm = np.concatenate([np.asarray(r["delta_f"]) for r in pre_results], axis=1)
    xi_fm = np.concatenate([np.asarray(r["xi_f"]) for r in pre_results], axis=1)
    bTf = np.concatenate([np.asarray(r["bT"]) for r in pre_results], axis=1)
    cTf = np.concatenate([np.asarray(r["cT"]) for r in pre_results], axis=1)
    nc = build_scan()
    in_maps = []
    for c in range(NCORE):
        r0 = c * 256
        in_maps.append({
            "deltaS": np.ascontiguousarray(delta_fm[r0:r0 + 256]),
            "xiS": np.ascontiguousarray(xi_fm[r0:r0 + 256]),
            "bTf": bTf, "cTf": cTf,
            "a_log": np.ascontiguousarray(a_log_full[r0:r0 + 256]),
            "d_vec": np.ascontiguousarray(d_full[r0:r0 + 256]),
        })
    res = run_bass_kernel_spmd(nc, in_maps, core_ids=list(range(NCORE)))
    return np.concatenate([np.asarray(r["yS"]) for r in res.results], axis=0)


def build_mgate():
    nc = bass.Bass("TRN2", target_bir_lowering=False, detect_race_conditions=False)
    yT = nc.dram_tensor("yT", [DIN, TLOC], F32, kind="ExternalInput").ap()
    z_f = nc.dram_tensor("z_f", [DIN, TLOC], F32, kind="ExternalInput").ap()
    xn_f = nc.dram_tensor("xn_f", [D, TLOC], F32, kind="ExternalInput").ap()
    out_proj = nc.dram_tensor("out_proj", [DIN, D], F32, kind="ExternalInput").ap()
    xT = nc.dram_tensor("xT", [D, TLOC], F32, kind="ExternalInput").ap()
    g_f = nc.dram_tensor("g_f", [D, TLOC], F32, kind="ExternalInput").ap()
    x1_f = nc.dram_tensor("x1_f", [D, TLOC], F32, kind="ExternalOutput").ap()

    with contextlib.ExitStack() as st:
        E = st.enter_context
        gb = [E(nc.sbuf_tensor(f"gb{i}", [128, TLOC], BF16)) for i in range(16)]
        zt = E(nc.sbuf_tensor("zt", [128, TLOC], F32))
        yt = E(nc.sbuf_tensor("yt", [128, TLOC], F32))
        xnt = E(nc.sbuf_tensor("xnt", [128, TLOC], F32))
        st_out = E(nc.sbuf_tensor("st_out", [128, TLOC], F32))
        gch = E(nc.sbuf_tensor("gch", [128, TLOC], F32))
        xch = E(nc.sbuf_tensor("xch", [128, TLOC], F32))
        op_sb = E(nc.sbuf_tensor("op_sb", [128, 16 * D], BF16))
        ps = [E(nc.psum_tensor(f"ps{i}", [128, 512], F32)) for i in range(2)]
        sems = make_sems(nc, E)
        block = E(nc.Block())

        p = Prog(same_engine_edges=True)
        p.dma("gpsimd", op_sb[:].rearrange("p (k m) -> p k m", m=D),
              out_proj.rearrange("(k p) m -> p k m", p=128))
        for i in range(16):
            p.dma("sync", zt[:], z_f[128 * i:128 * (i + 1), :])
            p.dma("sync", yt[:], yT[128 * i:128 * (i + 1), :])
            p.act(zt[:], zt[:], AF.Silu)
            p.tt(zt[:], zt[:], yt[:], OP.mult)
            p.copy(gb[i][:], zt[:])
        for m in range(8):
            p.dma("sync", xnt[:], xn_f[128 * m:128 * (m + 1), :])
            p.dma("sync", gch[:], g_f[128 * m:128 * (m + 1), :])
            p.dma("sync", xch[:], xT[128 * m:128 * (m + 1), :])
            for ci, (c0, cw) in enumerate(col_chunks(TLOC)):
                pp = ps[ci % 2]
                for k in range(16):
                    p.mm(pp[:, 0:cw], op_sb[:, k * D + 128 * m:k * D + 128 * (m + 1)],
                         gb[k][:, c0:c0 + cw], start=(k == 0), stop=(k == 15))
                p.tt(st_out[:, c0:c0 + cw], pp[:, 0:cw], xnt[:, c0:c0 + cw], OP.add)
            p.act(gch[:], gch[:], AF.Silu)
            p.tt(gch[:], gch[:], st_out[:], OP.mult)
            p.tt(gch[:], gch[:], xch[:], OP.add)
            p.dma("sync", x1_f[128 * m:128 * (m + 1), :], gch[:])
        p.emit(block, sems)
    return nc


def run_mgate(y_fm, pre_results, out_proj):
    nc = build_mgate()
    in_maps = []
    for c in range(NCORE):
        in_maps.append({
            "yT": np.ascontiguousarray(y_fm[:, c * TLOC:(c + 1) * TLOC]),
            "z_f": np.asarray(pre_results[c]["z_f"]),
            "xn_f": np.asarray(pre_results[c]["xn_f"]),
            "out_proj": out_proj,
        })
    res = run_bass_kernel_spmd(nc, in_maps, core_ids=list(range(NCORE)))
    return [np.asarray(r["s_f"]) for r in res.results]


HATT = 257
AXC = HATT + TLOC
QKC = 2304
NW = 8
WSZ = 256


def build_attn():
    nc = bass.Bass("TRN2", target_bir_lowering=False, detect_race_conditions=False)
    xh = nc.dram_tensor("xh", [D, AXC], F32, kind="ExternalInput").ap()
    wqkv = nc.dram_tensor("wqkv", [D, 3 * D], F32, kind="ExternalInput").ap()
    wo = nc.dram_tensor("wo", [D, D], F32, kind="ExternalInput").ap()
    qsE = nc.dram_tensor("qsE", [128, 1], F32, kind="ExternalInput").ap()
    ksE = nc.dram_tensor("ksE", [128, 1], F32, kind="ExternalInput").ap()
    csq = nc.dram_tensor("csq", [128, QKC], F32, kind="ExternalInput").ap()
    snq = nc.dram_tensor("snq", [128, QKC], F32, kind="ExternalInput").ap()
    csk = nc.dram_tensor("csk", [128, QKC], F32, kind="ExternalInput").ap()
    snk = nc.dram_tensor("snk", [128, QKC], F32, kind="ExternalInput").ap()
    maskQ = nc.dram_tensor("maskQ", [2, 128, 512], F32, kind="ExternalInput").ap()
    maskQ0 = nc.dram_tensor("maskQ0", [2, 128, 512], F32, kind="ExternalInput").ap()
    maskTd = nc.dram_tensor("maskTd", [4, 128, 512], F32, kind="ExternalInput").ap()
    maskT0d = nc.dram_tensor("maskT0d", [4, 128, 512], F32, kind="ExternalInput").ap()
    bcast2 = nc.dram_tensor("bcast2", [2, 128], F32, kind="ExternalInput").ap()
    perm = nc.dram_tensor("perm", [128, 128], F32, kind="ExternalInput").ap()
    ident = nc.dram_tensor("ident", [128, 128], F32, kind="ExternalInput").ap()
    x_out = nc.dram_tensor("x_out", [D, TLOC], F32, kind="ExternalOutput").ap()

    with contextlib.ExitStack() as st:
        E = st.enter_context
        xsb = [E(nc.sbuf_tensor(f"xsb{i}", [128, AXC], BF16)) for i in range(8)]
        vsb = [E(nc.sbuf_tensor(f"vsb{i}", [128, 16 * 65], BF16)) for i in range(18)]
        attn_sb = [E(nc.sbuf_tensor(f"attn{i}", [128, TLOC], BF16)) for i in range(8)]
        wv_all = E(nc.sbuf_tensor("wv_all", [128, 8 * D], BF16))
        csqb = E(nc.sbuf_tensor("csqb", [128, QKC], BF16))
        snqb = E(nc.sbuf_tensor("snqb", [128, QKC], BF16))
        cskb = E(nc.sbuf_tensor("cskb", [128, QKC], BF16))
        snkb = E(nc.sbuf_tensor("snkb", [128, QKC], BF16))
        mQ = E(nc.sbuf_tensor("mQ", [128, 2 * 512], F32))
        mQ0 = E(nc.sbuf_tensor("mQ0", [128, 2 * 512], F32))
        mT = E(nc.sbuf_tensor("mT", [128, 4 * 512], F32))
        mT0 = E(nc.sbuf_tensor("mT0", [128, 2 * 512], F32))
        bc2f = E(nc.sbuf_tensor("bc2f", [2, 128], F32))
        permb = E(nc.sbuf_tensor("permb", [128, 128], BF16))
        idb = E(nc.sbuf_tensor("idb", [128, 128], BF16))
        idf = E(nc.sbuf_tensor("idf", [128, 128], F32))
        ones2 = E(nc.sbuf_tensor("ones2", [128, 2], F32))
        ones1b = E(nc.sbuf_tensor("ones1b", [1, 128], BF16))
        onesKb = E(nc.sbuf_tensor("onesKb", [128, 1], BF16))
        qse_sb = E(nc.sbuf_tensor("qse_sb", [128, 1], F32))
        kse_sb = E(nc.sbuf_tensor("kse_sb", [128, 1], F32))
        wqk = [E(nc.sbuf_tensor("wqk0", [128, 2 * 1024], BF16))] * 2
        wost = E(nc.sbuf_tensor("wost", [128, 1024], BF16))
        qt = E(nc.sbuf_tensor("qt", [128, QKC], F32))
        qnb = E(nc.sbuf_tensor("qnb", [128, QKC], BF16))
        knb = E(nc.sbuf_tensor("knb", [128, QKC], BF16))
        ubb = [E(nc.sbuf_tensor(f"ubb{i}", [128, 512], BF16)) for i in range(2)]
        rbc = [E(nc.sbuf_tensor(f"rbc{i}", [128, 512], F32)) for i in range(2)]
        nrm = E(nc.sbuf_tensor("nrm", [2, 512], F32))
        nrmb = E(nc.sbuf_tensor("nrmb", [1, 512], BF16))
        scr = [E(nc.sbuf_tensor(f"scr{i}", [128, 512], F32)) for i in range(2)]
        rotb = E(nc.sbuf_tensor("rotb", [128, 512], BF16))
        t2b = E(nc.sbuf_tensor("t2b", [128, 512], BF16))
        scrT = [E(nc.sbuf_tensor(f"scrT{i}", [128, 512], F32)) for i in range(2)]
        ptb = [E(nc.sbuf_tensor(f"ptb{i}", [128, 512], BF16)) for i in range(2)]
        mstk = E(nc.sbuf_tensor("mstk", [128, 4], F32))
        negrow = [E(nc.sbuf_tensor(f"negrow{i}", [1, 512], BF16)) for i in range(2)]
        cpool = E(nc.sbuf_tensor("cpool", [128, 8], F32))
        psA = [E(nc.psum_tensor(f"psA{i}", [128, 512], F32)) for i in range(2)]
        psB = [E(nc.psum_tensor(f"psB{i}", [128, 512], F32)) for i in range(2)]
        psPV = [E(nc.psum_tensor(f"psPV{i}", [65, 512], F32)) for i in range(2)]
        psL = [E(nc.psum_tensor(f"psL{i}", [1, 512], F32)) for i in range(2)]
        sems = make_sems(nc, E)
        block = E(nc.Block())

        p = Prog()
        cst = Consts(nc, p, cpool)
        cst.reg(-1.0, 1.0, 0.0, 1e-12, 8.0)

        for i in range(8):
            p.dma("gpsimd", xsb[i][:], xh[128 * i:128 * (i + 1), :])
        p.dma("gpsimd", csqb[:], csq)
        p.dma("gpsimd", snqb[:], snq)
        p.dma("gpsimd", cskb[:], csk)
        p.dma("gpsimd", snkb[:], snk)
        p.dma("sync", mQ[:].rearrange("p (i c) -> p i c", c=512),
              maskQ.rearrange("i p c -> p i c"))
        p.dma("sync", mQ0[:].rearrange("p (i c) -> p i c", c=512),
              maskQ0.rearrange("i p c -> p i c"))
        p.dma("sync", mT[:].rearrange("p (i c) -> p i c", c=512),
              maskTd.rearrange("i p c -> p i c"))
        p.dma("sync", mT0[:].rearrange("p (i c) -> p i c", c=512),
              maskT0d[0:2].rearrange("i p c -> p i c"))
        p.dma("sync", bc2f[:], bcast2)
        p.dma("gpsimd", permb[:], perm)
        p.dma("gpsimd", idb[:], ident)
        p.dma("sync", idf[:], ident)
        p.dma("sync", qse_sb[:], qsE)
        p.dma("sync", kse_sb[:], ksE)
        for k in range(8):
            p.dma("gpsimd", wv_all[:, D * k:D * (k + 1)],
                  wqkv[128 * k:128 * (k + 1), 2 * D:3 * D])
        p.memset(ones2[:], 0.0)
        p.memset(ones2[0:64, 0:1], 1.0)
        p.memset(ones2[64:128, 1:2], 1.0)
        p.memset(ones1b[:], 1.0)
        p.memset(onesKb[:], 1.0)

        def xs_cols(i, c0, cwidth):
            off = 1 if i < 4 else 0
            return xsb[i][:, c0 + off:c0 + off + cwidth]

        for tc in range(18):
            p.memset(vsb[tc][:].rearrange("p (h c) -> p h c", c=65)[:, :, 64:65],
                     1.0)
            for half in range(2):
                ps = psA[(2 * tc + half) % 2]
                for k in range(8):
                    p.mm(ps[:, 0:512], xs_cols(k, 128 * tc, 128),
                         wv_all[:, D * k + 512 * half:D * k + 512 * (half + 1)],
                         start=(k == 0), stop=(k == 7))
                p.copy(vsb[tc][:].rearrange("p (h c) -> p h c", c=65)
                       [:, 8 * half:8 * (half + 1), 0:64],
                       ps[:, 0:512].rearrange("p (h c) -> p h c", c=64))

        def qk_make(dst_b, cs_b, sn_b, scale_col, lo=0):
            for ci, (c0, cw) in enumerate(col_chunks(QKC - lo)):
                c0 += lo
                s0 = scr[ci % 2]
                p.act(s0[:, 0:cw], qt[:, c0:c0 + cw], AF.Square)
                p.mm(psB[0][0:2, 0:cw], ones2[:], s0[:, 0:cw], start=True, stop=True)
                p.act(nrm[0:2, 0:cw], psB[0][0:2, 0:cw], AF.Sqrt)
                p.recip(nrm[0:2, 0:cw], nrm[0:2, 0:cw])
                p.ts(nrm[0:2, 0:cw], nrm[0:2, 0:cw], 1e12, op0=OP.min)
                p.mm(psB[0][:, 0:cw], bc2f[:], nrm[0:2, 0:cw],
                     start=True, stop=True)
                p.act(rbc[ci % 2][:, 0:cw], psB[0][:, 0:cw], AF.Copy)
                p.stt(qt[:, c0:c0 + cw], qt[:, c0:c0 + cw], scale_col,
                      rbc[ci % 2][:, 0:cw], OP.mult, OP.mult)
            for ci, (c0, cw) in enumerate(col_chunks(QKC - lo)):
                c0 += lo
                ub = ubb[ci % 2]
                ps = psB[ci % 2]
                p.copy(ub[:, 0:cw], qt[:, c0:c0 + cw])
                p.mm(ps[:, 0:cw], permb[:], ub[:, 0:cw],
                     start=True, stop=True)
                p.act(rotb[:, 0:cw], ps[:, 0:cw], AF.Copy)
                p.tt(rotb[:, 0:cw], rotb[:, 0:cw], sn_b[:, c0:c0 + cw], OP.mult)
                p.tt(t2b[:, 0:cw], ub[:, 0:cw], cs_b[:, c0:c0 + cw], OP.mult)
                p.tt(dst_b[:, c0:c0 + cw], t2b[:, 0:cw], rotb[:, 0:cw], OP.add)

        for hp in range(8):
            par = hp % 2
            wq = wqk[par]
            p.dma("gpsimd", wq[:, 0:1024].rearrange("p (k m) -> p k m", m=128),
                  wqkv[:, 128 * hp:128 * (hp + 1)].rearrange("(k p) m -> p k m", p=128))
            p.dma("gpsimd", wq[:, 1024:2048].rearrange("p (k m) -> p k m", m=128),
                  wqkv[:, D + 128 * hp:D + 128 * (hp + 1)].rearrange("(k p) m -> p k m", p=128))
            for c0, cw in col_chunks(TLOC):
                c0 += 256
                ps = psA[(c0 // 512) % 2]
                for k in range(8):
                    p.mm(ps[:, 0:cw], wq[:, 128 * k:128 * (k + 1)],
                         xs_cols(k, c0, cw), start=(k == 0), stop=(k == 7))
                p.act(qt[:, c0:c0 + cw], ps[:, 0:cw], AF.Copy)
            qk_make(qnb, csqb, snqb, qse_sb[:, 0:1], lo=256)
            for c0, cw in col_chunks(QKC):
                ps = psA[(c0 // 512) % 2]
                for k in range(8):
                    p.mm(ps[:, 0:cw], wq[:, 1024 + 128 * k:1024 + 128 * (k + 1)],
                         xs_cols(k, c0, cw), start=(k == 0), stop=(k == 7))
                p.act(qt[:, c0:c0 + cw], ps[:, 0:cw], AF.Copy)
            qk_make(knb, cskb, snkb, kse_sb[:, 0:1])
            qn, kn = qnb, knb

            for w in range(NW):
                wp = w % 2
                qcol0 = 256 + 256 * w
                kcol0 = 256 * w
                mq = mQ0 if w == 0 else mQ
                for hh in range(2):
                    base = 64 * hh
                    for qb in range(2):
                        ps = psA[(2 * hh + qb) % 2]
                        s0 = scr[(2 * hh + qb) % 2]
                        p.mm(ps[:, 0:512],
                             qn[base:base + 64, qcol0 + 128 * qb:qcol0 + 128 * (qb + 1)],
                             kn[base:base + 64, kcol0:kcol0 + 512],
                             start=True, stop=True)
                        p.tt(s0[:, 0:512], ps[:, 0:512],
                             mq[:, 512 * qb:512 * (qb + 1)], OP.add)
                        p.red(mstk[:, 2 * hh + qb:2 * hh + qb + 1], s0[:, 0:512],
                              OP.max)
                for col in range(4):
                    hh, qb = col // 2, col % 2
                    p.transpose(psL[0][0:1, 0:128], mstk[:, col:col + 1], idf[:])
                    p.ts(negrow[wp][0:1, 256 * hh + 128 * qb:256 * hh + 128 * (qb + 1)],
                         psL[0][0:1, 0:128], -1.0, op0=OP.mult)
                for kc in range(4):
                    ps = psB[kc % 2]
                    sT = scrT[kc % 2]
                    pt = ptb[kc % 2]
                    for hh in range(2):
                        base = 64 * hh
                        p.mm(ps[:, 256 * hh:256 * (hh + 1)],
                             kn[base:base + 64, kcol0 + 128 * kc:kcol0 + 128 * (kc + 1)],
                             qn[base:base + 64, qcol0:qcol0 + WSZ],
                             start=(hh == 0), stop=False,
                             tile_position=(base, 0))
                        p.mm(ps[:, 256 * hh:256 * (hh + 1)], ones1b[:],
                             negrow[wp][0:1, 256 * hh:256 * hh + WSZ],
                             start=False, stop=(hh == 1))
                    mt_ap = (mT0[:, 512 * kc:512 * (kc + 1)]
                             if (w == 0 and kc < 2)
                             else mT[:, 512 * kc:512 * (kc + 1)])
                    p.tt(sT[:, 0:512], ps[:, 0:512], mt_ap, OP.add)
                    p.act(pt[:, 0:512], sT[:, 0:512], AF.Exp, scale=8.0)
                    for hh in range(2):
                        h_glob = 2 * hp + hh
                        p.mm(psPV[wp][0:65, 256 * hh:256 * (hh + 1)],
                             vsb[2 * w + kc][:, 65 * h_glob:65 * (h_glob + 1)],
                             pt[:, 256 * hh:256 * (hh + 1)],
                             start=(kc == 0 and hh == 0),
                             stop=(kc == 3 and hh == 1))
                p.recip(nrm[0:1, 0:512], psPV[wp][64:65, 0:512])
                p.copy(nrmb[0:1, 0:512], nrm[0:1, 0:512])
                p.mm(psB[0][0:64, 0:512], ones1b[0:1, 0:64], nrmb[0:1, 0:512],
                     start=True, stop=True)
                p.act(scr[1][0:64, 0:512], psB[0][0:64, 0:512], AF.Copy)
                for hh in range(2):
                    p.tt(attn_sb[hp][64 * hh:64 * (hh + 1), 256 * w:256 * (w + 1)],
                         psPV[wp][0:64, 256 * hh:256 * (hh + 1)],
                         scr[1][0:64, 256 * hh:256 * (hh + 1)], OP.mult)

        out_sb = qt
        for m in range(8):
            p.dma("gpsimd", wost[:].rearrange("p (k m) -> p k m", m=128),
                  wo[:, 128 * m:128 * (m + 1)].rearrange("(k p) m -> p k m", p=128))
            for c0, cw in col_chunks(TLOC):
                par2 = (c0 // 512) % 2
                ps = psA[par2]
                xs_ = scr[par2]
                p.dma("sync", xs_[:, 0:cw],
                      xh[128 * m:128 * (m + 1), HATT + c0:HATT + c0 + cw])
                for k in range(8):
                    p.mm(ps[:, 0:cw], wost[:, 128 * k:128 * (k + 1)],
                         attn_sb[k][:, c0:c0 + cw], start=(k == 0), stop=(k == 7))
                p.tt(out_sb[:, c0:c0 + cw], ps[:, 0:cw], xs_[:, 0:cw], OP.add)
            p.dma("sync", x_out[128 * m:128 * (m + 1), :], out_sb[:, 0:TLOC])
        p.emit(block, sems)
    return nc


def attn_consts(core):
    DIM_HEAD, WINDOW = 64, 256
    n = 8192
    lt0 = (core % 4) * TLOC
    pos = np.arange(lt0 - 256, lt0 + TLOC, dtype=np.float64)
    half = np.arange(0, DIM_HEAD, 2, dtype=np.float64)
    inv_freq = 1.0 / 10000.0 ** (half / DIM_HEAD)
    freqs = pos[:, None] * inv_freq[None, :]
    freqs = np.concatenate([freqs, freqs], axis=1)
    base_scale = (half + 0.4 * DIM_HEAD) / (1.4 * DIM_HEAD)
    power = (pos - n // 2) / (WINDOW // 2)
    scale = base_scale[None, :] ** power[:, None]
    scale = np.concatenate([scale, scale], axis=1)
    cos, sin = np.cos(freqs), np.sin(freqs)
    csq = (cos * scale).T
    snq = (sin * scale).T
    csk = (cos / scale).T
    snk = (sin / scale).T
    two = lambda a: np.ascontiguousarray(
        np.concatenate([a, a], axis=0).astype(np.float32))
    csq, snq, csk, snk = map(two, (csq, snq, csk, snk))

    i_ = np.arange(256)[:, None]
    j_ = np.arange(512)[None, :]
    bad = ~((i_ <= j_) & (j_ <= i_ + 256))
    mq = np.where(bad, NEG, 0.0).astype(np.float32)
    maskQ = np.stack([mq[0:128], mq[128:256]])
    mt_full = np.where(bad.T, NEG, 0.0).astype(np.float32)
    dd = lambda m: np.concatenate([m, m], axis=1)
    maskT = np.stack([dd(mt_full[128 * kc:128 * (kc + 1)]) for kc in range(4)])
    if core % 4 == 0:
        mq0 = mq.copy()
        mq0[:, 0:256] = NEG
        maskQ0 = np.stack([mq0[0:128], mq0[128:256]])
        mt0 = mt_full.copy()
        mt0[0:256, :] = NEG
        maskT0 = np.stack([dd(mt0[128 * kc:128 * (kc + 1)]) for kc in range(4)])
    else:
        maskQ0, maskT0 = maskQ, maskT

    P = np.zeros((128, 128), np.float32)
    for blk in range(2):
        for dd in range(32):
            P[blk * 64 + dd + 32, blk * 64 + dd] = -1.0
            P[blk * 64 + dd, blk * 64 + dd + 32] = 1.0
    bcast2 = np.zeros((2, 128), np.float32)
    bcast2[0, 0:64] = 1.0
    bcast2[1, 64:128] = 1.0
    return {
        "csq": csq, "snq": snq, "csk": csk, "snk": snk,
        "maskQ": np.ascontiguousarray(maskQ), "maskQ0": np.ascontiguousarray(maskQ0),
        "maskTd": np.ascontiguousarray(maskT), "maskT0d": np.ascontiguousarray(maskT0),
        "bcast2": bcast2, "perm": P, "ident": np.eye(128, dtype=np.float32),
    }


def run_attn(nc_attn, x_fm, wqkv_i, qs_i, ks_i, wo_i, consts):
    in_maps = []
    for c in range(NCORE):
        t0 = c * TLOC
        if c % 4 == 0:
            halo = np.zeros((D, HATT), np.float32)
        else:
            halo = x_fm[:, t0 - HATT:t0]
        xh = np.ascontiguousarray(np.concatenate([halo, x_fm[:, t0:t0 + TLOC]], 1))
        m = {"xh": xh, "wqkv": wqkv_i, "wo": wo_i,
             "qsE": np.ascontiguousarray(np.tile(qs_i, 2)[:, None]),
             "ksE": np.ascontiguousarray(np.tile(ks_i, 2)[:, None])}
        m.update(consts[c])
        in_maps.append(m)
    res = run_bass_kernel_spmd(nc_attn, in_maps, core_ids=list(range(NCORE)))
    return np.concatenate([np.asarray(r["x_out"]) for r in res.results], axis=1)


DFF = 2730
NMC = 22


def build_ffn():
    nc = bass.Bass("TRN2", target_bir_lowering=False, detect_race_conditions=False)
    xh = nc.dram_tensor("xh", [D, 1 + TLOC], F32, kind="ExternalInput").ap()
    w1a = nc.dram_tensor("w1a", [D, DFF], F32, kind="ExternalInput").ap()
    w1g = nc.dram_tensor("w1g", [D, DFF], F32, kind="ExternalInput").ap()
    w2 = nc.dram_tensor("w2", [DFF, D], F32, kind="ExternalInput").ap()
    gE = nc.dram_tensor("gE", [128, 8], F32, kind="ExternalInput").ap()
    bE = nc.dram_tensor("bE", [128, 8], F32, kind="ExternalInput").ap()
    b1aE = nc.dram_tensor("b1aE", [128, NMC], F32, kind="ExternalInput").ap()
    b1gE = nc.dram_tensor("b1gE", [128, NMC], F32, kind="ExternalInput").ap()
    b2E = nc.dram_tensor("b2E", [128, 8], F32, kind="ExternalInput").ap()
    x_out = nc.dram_tensor("x_out", [D, TLOC], F32, kind="ExternalOutput").ap()

    with contextlib.ExitStack() as st:
        E = st.enter_context
        xnb = [E(nc.sbuf_tensor(f"xnb{i}", [128, TLOC], BF16)) for i in range(8)]
        xs32 = E(nc.sbuf_tensor("xs32", [128, 1 + TLOC], F32))
        sq = E(nc.sbuf_tensor("sq", [128, 512], F32))
        rows = E(nc.sbuf_tensor("rows", [1, TLOC], F32))
        row2 = E(nc.sbuf_tensor("row2", [1, TLOC], F32))
        crow = E(nc.sbuf_tensor("crow", [1, TLOC], F32))
        bc1b = E(nc.sbuf_tensor("bc1b", [128, TLOC], BF16))
        bc2b = E(nc.sbuf_tensor("bc2b", [128, TLOC], BF16))
        ones1 = E(nc.sbuf_tensor("ones1", [1, 128], F32))
        ones1c = E(nc.sbuf_tensor("ones1c", [128, 1], F32))
        ones1cb = E(nc.sbuf_tensor("ones1cb", [128, 1], BF16))
        gsb = E(nc.sbuf_tensor("gsb", [128, 8], F32))
        bsb = E(nc.sbuf_tensor("bsb", [128, 8], F32))
        b1as = E(nc.sbuf_tensor("b1as", [128, NMC], F32))
        b1gs = E(nc.sbuf_tensor("b1gs", [128, NMC], F32))
        b2s = E(nc.sbuf_tensor("b2s", [128, 8], F32))
        wst_r = [E(nc.sbuf_tensor(f"wst_{i}", [128, 8 * 128], BF16)) for i in range(2)]
        wst2_r = [E(nc.sbuf_tensor(f"wst2_{i}", [128, 8 * 128], BF16)) for i in range(2)]
        w2st = E(nc.sbuf_tensor("w2st", [128, NMC * 128], BF16))
        haf = E(nc.sbuf_tensor("haf", [128, TLOC], F32))
        gel = E(nc.sbuf_tensor("gel", [128, TLOC], F32))
        cpool = E(nc.sbuf_tensor("cpool", [128, 8], F32))
        psA = E(nc.psum_tensor("psA", [128, 512], F32))
        psB = E(nc.psum_tensor("psB", [128, 512], F32))
        psR1 = E(nc.psum_tensor("psR1", [1, 512], F32))
        psR2 = E(nc.psum_tensor("psR2", [1, 512], F32))
        psBC = E(nc.psum_tensor("psBC", [128, 512], F32))
        sems = make_sems(nc, E)
        block = E(nc.Block())
        st2 = contextlib.ExitStack()
        xbr = [st2.enter_context(nc.sbuf_tensor(f"xbr{i}", [128, 1 + TLOC], BF16))
               for i in range(8)]

        p = Prog()
        cst = Consts(nc, p, cpool)
        cst.reg(-1.0, 1.0, 0.0, 1e-5, 1.0 / D)
        p.dma("sync", gsb[:], gE)
        p.dma("sync", bsb[:], bE)
        p.dma("sync", b1as[:], b1aE)
        p.dma("sync", b1gs[:], b1gE)
        p.dma("sync", b2s[:], b2E)
        p.memset(ones1[:], 1.0)
        p.memset(ones1c[:], 1.0)
        p.memset(ones1cb[:], 1.0)

        def shx(i, c0, cw):
            off = 1 if i < 4 else 0
            return xbr[i][:, c0 + off:c0 + off + cw]

        for i in range(8):
            p.dma("gpsimd", xbr[i][:], xh[128 * i:128 * (i + 1), :])
        for c0, cw in col_chunks(TLOC):
            for i in range(8):
                p.mm(psR1[:, 0:cw], ones1cb[:], shx(i, c0, cw),
                     start=(i == 0), stop=(i == 7))
            p.copy(rows[:, c0:c0 + cw], psR1[:, 0:cw])
            for i in range(8):
                p.act(sq[:, 0:cw], shx(i, c0, cw), AF.Square)
                p.mm(psR2[:, 0:cw], ones1c[:], sq[:, 0:cw],
                     start=(i == 0), stop=(i == 7))
            p.copy(row2[:, c0:c0 + cw], psR2[:, 0:cw])
        p.ts(rows[:], rows[:], 1.0 / D, op0=OP.mult)
        p.act(crow[:], rows[:], AF.Square)
        p.ts(row2[:], row2[:], 1.0 / D, op0=OP.mult)
        p.tt(crow[:], row2[:], crow[:], OP.subtract)
        p.act(crow[:], crow[:], AF.Ln, bias=1e-5)
        p.act(crow[:], crow[:], AF.Exp, scale=-0.5)
        for c0, cw in col_chunks(TLOC):
            p.mm(psBC[:, 0:cw], ones1[:], crow[:, c0:c0 + cw], start=True, stop=True)
            p.copy(bc1b[:, c0:c0 + cw], psBC[:, 0:cw])
        p.tt(crow[:], rows[:], crow[:], OP.mult)
        p.ts(crow[:], crow[:], -1.0, op0=OP.mult)
        for c0, cw in col_chunks(TLOC):
            p.mm(psBC[:, 0:cw], ones1[:], crow[:, c0:c0 + cw], start=True, stop=True)
            p.copy(bc2b[:, c0:c0 + cw], psBC[:, 0:cw])
        for i in range(8):
            for c0, cw in col_chunks(TLOC):
                p.tt(xnb[i][:, c0:c0 + cw], shx(i, c0, cw), bc1b[:, c0:c0 + cw],
                     OP.mult)
            p.tt(xnb[i][:], xnb[i][:], bc2b[:], OP.add)
            p.ts(xnb[i][:], xnb[i][:], gsb[:, i:i + 1], bsb[:, i:i + 1],
                 op0=OP.mult, op1=OP.add)

        st2.close()
        tb = [E(nc.sbuf_tensor(f"tb{i}", [128, TLOC], BF16)) for i in range(NMC)]

        for mc in range(NMC):
            mw = min(128, DFF - 128 * mc)
            wst = wst_r[mc % 2]
            wst2 = wst2_r[mc % 2]
            p.dma("gpsimd", wst[:, 0:8 * mw].rearrange("p (k m) -> p k m", m=mw),
                  w1a[:, 128 * mc:128 * mc + mw].rearrange("(k p) m -> p k m", p=128))
            p.dma("gpsimd", wst2[:, 0:8 * mw].rearrange("p (k m) -> p k m", m=mw),
                  w1g[:, 128 * mc:128 * mc + mw].rearrange("(k p) m -> p k m", p=128))
            for c0, cw in col_chunks(TLOC):
                for k in range(8):
                    p.mm(psA[0:mw, 0:cw], wst[:, mw * k:mw * (k + 1)],
                         xnb[k][:, c0:c0 + cw], start=(k == 0), stop=(k == 7))
                for k in range(8):
                    p.mm(psB[0:mw, 0:cw], wst2[:, mw * k:mw * (k + 1)],
                         xnb[k][:, c0:c0 + cw], start=(k == 0), stop=(k == 7))
                p.ts(haf[0:mw, c0:c0 + cw], psA[0:mw, 0:cw],
                     b1as[0:mw, mc:mc + 1], op0=OP.add)
                p.act(gel[0:mw, c0:c0 + cw], psB[0:mw, 0:cw], AF.Gelu,
                      bias=b1gs[0:mw, mc:mc + 1])
            p.tt(tb[mc][0:mw, :], haf[0:mw, :], gel[0:mw, :], OP.mult)

        for m2 in range(8):
            p.dma("gpsimd",
                  w2st[:, 0:21 * 128].rearrange("p (k m) -> p k m", m=128),
                  w2[0:2688, 128 * m2:128 * (m2 + 1)].rearrange("(k p) m -> p k m", p=128))
            p.dma("gpsimd", w2st[0:42, 21 * 128:22 * 128],
                  w2[2688:2730, 128 * m2:128 * (m2 + 1)])
            p.dma("sync", xs32[:, 0:TLOC], xh[128 * m2:128 * (m2 + 1), 1:1 + TLOC])
            for ci, (c0, cw) in enumerate(col_chunks(TLOC)):
                ps = psA if ci % 2 == 0 else psB
                for mc in range(NMC):
                    mw = min(128, DFF - 128 * mc)
                    p.mm(ps[:, 0:cw], w2st[0:mw, 128 * mc:128 * mc + 128],
                         tb[mc][0:mw, c0:c0 + cw], start=(mc == 0),
                         stop=(mc == NMC - 1))
                p.stt(haf[:, c0:c0 + cw], ps[:, 0:cw], b2s[:, m2:m2 + 1],
                      xs32[:, c0:c0 + cw], OP.add, OP.add)
            p.dma("sync", x_out[128 * m2:128 * (m2 + 1), :], haf[:, 0:TLOC])
        p.emit(block, sems)
    return nc


def run_ffn(nc_ffn, x_fm, W1_i, b1_i, W2_i, b2_i, lng_i, lnb_i):
    rear = lambda v: np.ascontiguousarray(v.reshape(-1, 128).T)
    padE = lambda v, n: np.ascontiguousarray(
        np.pad(v, (0, n * 128 - len(v))).reshape(n, 128).T)
    in_maps = []
    for c in range(NCORE):
        t0 = c * TLOC
        if c % 4 == 0:
            halo = np.zeros((D, 1), np.float32)
        else:
            halo = x_fm[:, t0 - 1:t0]
        xh = np.ascontiguousarray(np.concatenate([halo, x_fm[:, t0:t0 + TLOC]], 1))
        in_maps.append({
            "xh": xh,
            "w1a": np.ascontiguousarray(W1_i[:, :DFF]),
            "w1g": np.ascontiguousarray(W1_i[:, DFF:]),
            "w2": W2_i,
            "gE": rear(lng_i), "bE": rear(lnb_i),
            "b1aE": padE(b1_i[:DFF], NMC), "b1gE": padE(b1_i[DFF:], NMC),
            "b2E": rear(b2_i),
        })
    res = run_bass_kernel_spmd(nc_ffn, in_maps, core_ids=list(range(NCORE)))
    return np.concatenate([np.asarray(r["x_out"]) for r in res.results], axis=1)


def build_combine1():
    nc = bass.Bass("TRN2", target_bir_lowering=False, detect_race_conditions=False)
    xT = nc.dram_tensor("xT", [D, TLOC], F32, kind="ExternalInput").ap()
    s_f = nc.dram_tensor("s_f", [D, TLOC], F32, kind="ExternalInput").ap()
    g_f = nc.dram_tensor("g_f", [D, TLOC], F32, kind="ExternalInput").ap()
    x1_f = nc.dram_tensor("x1_f", [D, TLOC], F32, kind="ExternalOutput").ap()

    with contextlib.ExitStack() as st:
        E = st.enter_context
        a = E(nc.sbuf_tensor("a", [128, TLOC], F32))
        b = E(nc.sbuf_tensor("b", [128, TLOC], F32))
        c = E(nc.sbuf_tensor("c", [128, TLOC], F32))
        sems = make_sems(nc, E)
        block = E(nc.Block())
        p = Prog()
        for i in range(8):
            p.dma("sync", a[:], g_f[128 * i:128 * (i + 1), :])
            p.dma("sync", b[:], s_f[128 * i:128 * (i + 1), :])
            p.dma("sync", c[:], xT[128 * i:128 * (i + 1), :])
            p.act(a[:], a[:], AF.Silu)
            p.tt(a[:], a[:], b[:], OP.mult)
            p.tt(a[:], a[:], c[:], OP.add)
            p.dma("sync", x1_f[128 * i:128 * (i + 1), :], a[:])
        p.emit(block, sems)
    return nc


QTERMS = [
    [(0, 0, 1), (1, 1, -1), (2, 2, -1), (3, 3, -1)],
    [(1, 0, 1), (0, 1, 1), (3, 2, 1), (2, 3, -1)],
    [(2, 0, 1), (3, 1, -1), (0, 2, 1), (1, 3, 1)],
    [(3, 0, 1), (2, 1, 1), (1, 2, -1), (0, 3, 1)],
]


def emit_quat_load(p, Wd, bd, wpos, wneg, bE_sb, scrcol):
    for w in range(4):
        for kc in range(2):
            p.dma("gpsimd", wpos[:, (w * 2 + kc) * 256:(w * 2 + kc + 1) * 256],
                  Wd[w, 128 * kc:128 * (kc + 1), :])
    p.ts(wneg[:], wpos[:], -1.0, op0=OP.mult)
    p.dma("sync", scrcol[:].rearrange("p (w h) -> p w h", h=2),
          bd.rearrange("w (h p) -> p w h", p=128),
          allow_slow_non_contiguous=True)
    for o in range(4):
        for h in range(2):
            terms = QTERMS[o]
            (c0_, w0_, s0_) = terms[0]
            p.ts(bE_sb[:, o * 2 + h:o * 2 + h + 1],
                 scrcol[:, w0_ * 2 + h:w0_ * 2 + h + 1], float(s0_), op0=OP.mult)
            for (c_, w_, s_) in terms[1:]:
                p.stt(bE_sb[:, o * 2 + h:o * 2 + h + 1],
                      scrcol[:, w_ * 2 + h:w_ * 2 + h + 1], float(s_),
                      bE_sb[:, o * 2 + h:o * 2 + h + 1], OP.mult, OP.add)


def emit_quat_mm_fm(p, xb, wpos, wneg, bE_sb, psA, out_cb, quatnorm, scrq, nsq,
                    cols=TLOC):
    for oc in range(8):
        o, half = oc // 2, oc % 2
        for c0, cw in col_chunks(cols):
            first = True
            for (c_, w_, s_) in QTERMS[o]:
                src = wpos if s_ > 0 else wneg
                for kc in range(2):
                    p.mm(psA[:, 0:cw],
                         src[:, (w_ * 2 + kc) * 256 + 128 * half:
                             (w_ * 2 + kc) * 256 + 128 * half + 128],
                         xb[c_ * 2 + kc][:, c0:c0 + cw],
                         start=first, stop=(kc == 1 and (c_, w_, s_) == QTERMS[o][3]))
                    first = False
            p.ts(scrq[oc][:, c0:c0 + cw], psA[:, 0:cw],
                 bE_sb[:, oc:oc + 1], op0=OP.add)


def build_quat():
    nc = bass.Bass("TRN2", target_bir_lowering=False, detect_race_conditions=False)
    x1 = nc.dram_tensor("x1", [D, TLOC], F32, kind="ExternalInput").ap()
    qW = nc.dram_tensor("qW", [4, 256, 256], F32, kind="ExternalInput").ap()
    qb = nc.dram_tensor("qb", [4, 256], F32, kind="ExternalInput").ap()
    kW = nc.dram_tensor("kW", [4, 256, 256], F32, kind="ExternalInput").ap()
    kb = nc.dram_tensor("kb", [4, 256], F32, kind="ExternalInput").ap()
    vW = nc.dram_tensor("vW", [4, 256, 256], F32, kind="ExternalInput").ap()
    vb = nc.dram_tensor("vb", [4, 256], F32, kind="ExternalInput").ap()
    xn2b_o = nc.dram_tensor("xn2b_o", [D, TLOC], BF16, kind="ExternalOutput").ap()
    q_f = nc.dram_tensor("q_f", [D, TLOC], BF16, kind="ExternalOutput").ap()
    k_f = nc.dram_tensor("k_f", [D, TLOC], BF16, kind="ExternalOutput").ap()
    v_t = nc.dram_tensor("v_t", [TLOC, D], BF16, kind="ExternalOutput").ap()
    bscr = nc.dram_tensor("bscr", [128, 8], F32, kind="Internal").ap()

    with contextlib.ExitStack() as st:
        E = st.enter_context
        xb = [E(nc.sbuf_tensor(f"xb{i}", [128, TLOC], BF16)) for i in range(8)]
        sq = E(nc.sbuf_tensor("sq", [128, 512], F32))
        rows = E(nc.sbuf_tensor("rows", [1, TLOC], F32))
        row2 = E(nc.sbuf_tensor("row2", [1, TLOC], F32))
        crow = E(nc.sbuf_tensor("crow", [1, TLOC], F32))
        bc1 = E(nc.sbuf_tensor("bc1", [128, TLOC], F32))
        bc2 = E(nc.sbuf_tensor("bc2", [128, TLOC], F32))
        ones1 = E(nc.sbuf_tensor("ones1", [1, 128], F32))
        ones1c = E(nc.sbuf_tensor("ones1c", [128, 1], F32))
        onesb = E(nc.sbuf_tensor("onesb", [1, 128], BF16))
        wpos = E(nc.sbuf_tensor("wpos", [128, 2048], BF16))
        wneg = E(nc.sbuf_tensor("wneg", [128, 2048], BF16))
        bE_sb = E(nc.sbuf_tensor("bE_sb", [128, 8], F32))
        scrcol = E(nc.sbuf_tensor("scrcol", [128, 8], F32))
        bErow = E(nc.sbuf_tensor("bErow", [1, 1024], BF16))
        vout = E(nc.sbuf_tensor("vout", [128, 512], BF16))
        cpool = E(nc.sbuf_tensor("cpool", [128, 8], F32))
        psA = E(nc.psum_tensor("psA", [128, 512], F32))
        psV = E(nc.psum_tensor("psV", [128, 512], F32))
        psR1 = E(nc.psum_tensor("psR1", [1, 512], F32))
        psR2 = E(nc.psum_tensor("psR2", [1, 512], F32))
        psBC = E(nc.psum_tensor("psBC", [128, 512], F32))
        sems = make_sems(nc, E)
        block = E(nc.Block())
        st2 = contextlib.ExitStack()
        xc = [st2.enter_context(nc.sbuf_tensor(f"xc{i}", [128, TLOC], F32))
              for i in range(8)]

        p = Prog(same_engine_edges=True)
        cst = Consts(nc, p, cpool)
        cst.reg(-1.0, 1.0, 0.0, 1e-5, 1e-6, 1.0 / D)
        p.memset(ones1[:], 1.0)
        p.memset(ones1c[:], 1.0)
        p.memset(onesb[:], 1.0)
        for i in range(8):
            p.dma("sync", xc[i][:], x1[128 * i:128 * (i + 1), :])
        for c0, cw in col_chunks(TLOC):
            for i in range(8):
                p.mm(psR1[:, 0:cw], ones1c[:], xc[i][:, c0:c0 + cw],
                     start=(i == 0), stop=(i == 7))
            p.copy(rows[:, c0:c0 + cw], psR1[:, 0:cw])
            for i in range(8):
                p.act(sq[:, 0:cw], xc[i][:, c0:c0 + cw], AF.Square)
                p.mm(psR2[:, 0:cw], ones1c[:], sq[:, 0:cw],
                     start=(i == 0), stop=(i == 7))
            p.copy(row2[:, c0:c0 + cw], psR2[:, 0:cw])
        p.ts(rows[:], rows[:], 1.0 / D, op0=OP.mult)
        p.act(crow[:], rows[:], AF.Square)
        p.ts(row2[:], row2[:], 1.0 / D, op0=OP.mult)
        p.tt(crow[:], row2[:], crow[:], OP.subtract)
        p.act(crow[:], crow[:], AF.Ln, bias=1e-5)
        p.act(crow[:], crow[:], AF.Exp, scale=-0.5)
        for c0, cw in col_chunks(TLOC):
            p.mm(psBC[:, 0:cw], ones1[:], crow[:, c0:c0 + cw], start=True, stop=True)
            p.copy(bc1[:, c0:c0 + cw], psBC[:, 0:cw])
        p.tt(crow[:], rows[:], crow[:], OP.mult)
        p.ts(crow[:], crow[:], -1.0, op0=OP.mult)
        for c0, cw in col_chunks(TLOC):
            p.mm(psBC[:, 0:cw], ones1[:], crow[:, c0:c0 + cw], start=True, stop=True)
            p.copy(bc2[:, c0:c0 + cw], psBC[:, 0:cw])
        for i in range(8):
            p.tt(xc[i][:], xc[i][:], bc1[:], OP.mult)
            p.tt(xc[i][:], xc[i][:], bc2[:], OP.add)
            p.copy(xb[i][:], xc[i][:])
            p.dma("sync", xn2b_o[128 * i:128 * (i + 1), :], xb[i][:])

        st2.close()
        xbo = [E(nc.sbuf_tensor(f"xbo{i}", [128, TLOC], BF16)) for i in range(8)]
        outq = [E(nc.sbuf_tensor(f"outq{i}", [128, TLOC], F32)) for i in range(8)]
        scr0 = E(nc.sbuf_tensor("scr0", [128, TLOC], F32))
        nsq = [E(nc.sbuf_tensor(f"nsq{i}", [128, TLOC], F32)) for i in range(2)]

        def qk_one(Wd, bd, out_dram):
            emit_quat_load(p, Wd, bd, wpos, wneg, bE_sb, scrcol)
            for oc in range(8):
                o, half = oc // 2, oc % 2
                for c0, cw in col_chunks(TLOC):
                    first = True
                    for ti, (c_, w_, s_) in enumerate(QTERMS[o]):
                        src = wpos if s_ > 0 else wneg
                        for kc in range(2):
                            p.mm(psA[:, 0:cw],
                                 src[:, (w_ * 2 + kc) * 256 + 128 * half:
                                     (w_ * 2 + kc) * 256 + 128 * half + 128],
                                 xb[c_ * 2 + kc][:, c0:c0 + cw],
                                 start=first, stop=(ti == 3 and kc == 1))
                            first = False
                    p.ts(outq[oc][:, c0:c0 + cw], psA[:, 0:cw],
                         bE_sb[:, oc:oc + 1], op0=OP.add)
            for half in range(2):
                p.act(nsq[half][:], outq[half][:], AF.Square)
                for o in range(1, 4):
                    p.act(scr0[:], outq[2 * o + half][:], AF.Square)
                    p.tt(nsq[half][:], nsq[half][:], scr0[:], OP.add)
                p.act(nsq[half][:], nsq[half][:], AF.Ln, bias=1e-6)
                p.act(nsq[half][:], nsq[half][:], AF.Exp, scale=-0.5)
            for oc in range(8):
                p.tt(outq[oc][:], outq[oc][:], nsq[oc % 2][:], OP.mult)
                p.copy(xbo[oc][:], outq[oc][:])
                p.dma("sync", out_dram[128 * oc:128 * (oc + 1), :], xbo[oc][:])

        qk_one(qW, qb, q_f)
        qk_one(kW, kb, k_f)

        emit_quat_load(p, vW, vb, wpos, wneg, bE_sb, scrcol)
        p.dma("sync", bscr, bE_sb[:])
        p.dma("gpsimd",
              bErow[:].rearrange("a (o h pp) -> a o h pp", o=4, h=2),
              bscr.rearrange("p (o h) -> o h p", o=4).unsqueeze(0),
              allow_slow_non_contiguous=True)
        for tc in range(16):
            for half2 in range(2):
                first = True
                for oo in range(2):
                    o = 2 * half2 + oo
                    for (c_, w_, s_) in QTERMS[o]:
                        src = wpos if s_ > 0 else wneg
                        for kc in range(2):
                            p.mm(psV[:, 256 * oo:256 * (oo + 1)],
                                 xb[c_ * 2 + kc][:, 128 * tc:128 * (tc + 1)],
                                 src[:, (w_ * 2 + kc) * 256:(w_ * 2 + kc + 1) * 256],
                                 start=first, stop=False)
                            first = False
                p.mm(psV[:, 0:512], onesb[:], bErow[0:1, 512 * half2:512 * (half2 + 1)],
                     start=False, stop=True)
                p.copy(vout[:], psV[:, 0:512])
                p.dma("sync", v_t[128 * tc:128 * (tc + 1), 512 * half2:512 * (half2 + 1)],
                      vout[:])
        p.emit(block, sems)
    return nc


def run_quat(nc_quat, x1_fm, inp):
    in_maps = []
    for c in range(NCORE):
        in_maps.append({
            "x1": np.ascontiguousarray(x1_fm[:, c * TLOC:(c + 1) * TLOC]),
            "qW": np.asarray(inp["L1_attn_qW"]), "qb": np.asarray(inp["L1_attn_qb"]),
            "kW": np.asarray(inp["L1_attn_kW"]), "kb": np.asarray(inp["L1_attn_kb"]),
            "vW": np.asarray(inp["L1_attn_vW"]), "vb": np.asarray(inp["L1_attn_vb"]),
        })
    res = run_bass_kernel_spmd(nc_quat, in_maps, core_ids=list(range(NCORE)))
    return res.results


TBAT = 8192


def build_attn2():
    nc = bass.Bass("TRN2", target_bir_lowering=False, detect_race_conditions=False)
    q_fd = nc.dram_tensor("q_fd", [D, TLOC], BF16, kind="ExternalInput").ap()
    kf = nc.dram_tensor("kf", [D, TBAT], BF16, kind="ExternalInput").ap()
    vt = nc.dram_tensor("vt", [TBAT, D], BF16, kind="ExternalInput").ap()
    jv = nc.dram_tensor("jv", [1, 1024], F32, kind="ExternalInput").ap()
    oW = nc.dram_tensor("oW", [4, 256, 256], F32, kind="ExternalInput").ap()
    ob = nc.dram_tensor("ob", [4, 256], F32, kind="ExternalInput").ap()
    s2_f = nc.dram_tensor("s2_f", [D, TLOC], F32, kind="ExternalOutput").ap()

    with contextlib.ExitStack() as st:
        E = st.enter_context
        qsb = [E(nc.sbuf_tensor(f"qsb{i}", [128, TLOC], BF16)) for i in range(8)]
        osb = [E(nc.sbuf_tensor(f"osb{i}", [128, TLOC], BF16)) for i in range(8)]
        ksb = [E(nc.sbuf_tensor(f"ksb{i}", [64, TBAT], BF16)) for i in range(2)]
        vsb = [E(nc.sbuf_tensor(f"vsb{i}", [128, 64 * 65], BF16)) for i in range(2)]
        qh = [E(nc.sbuf_tensor(f"qh{i}", [64, TLOC], BF16)) for i in range(2)]
        ptb = [E(nc.sbuf_tensor(f"ptb{i}", [128, 1024], BF16)) for i in range(2)]
        jrow = E(nc.sbuf_tensor("jrow", [1, 1024], F32))
        jcol = E(nc.sbuf_tensor("jcol", [1, 1], F32))
        jb = E(nc.sbuf_tensor("jb", [128, 1], F32))
        rl = E(nc.sbuf_tensor("rl", [1, 512], F32))
        rlb = E(nc.sbuf_tensor("rlb", [1, 512], BF16))
        rb_sb = E(nc.sbuf_tensor("rb_sb", [64, 512], F32))
        ones1 = E(nc.sbuf_tensor("ones1", [1, 128], F32))
        onesb = E(nc.sbuf_tensor("onesb", [1, 128], BF16))
        wpos = E(nc.sbuf_tensor("wpos", [128, 2048], BF16))
        wneg = E(nc.sbuf_tensor("wneg", [128, 2048], BF16))
        bE_sb = E(nc.sbuf_tensor("bE_sb", [128, 8], F32))
        scrcol = E(nc.sbuf_tensor("scrcol", [128, 8], F32))
        scrF = E(nc.sbuf_tensor("scrF", [128, TLOC], F32))
        cpool = E(nc.sbuf_tensor("cpool", [128, 8], F32))
        psS = [E(nc.psum_tensor(f"psS{i}", [128, 1024], F32)) for i in range(2)]
        psO = E(nc.psum_tensor("psO", [65, 512], F32))
        psB = E(nc.psum_tensor("psB", [128, 512], F32))
        sems = make_sems(nc, E)
        block = E(nc.Block())

        p = Prog(same_engine_edges=True)
        cst = Consts(nc, p, cpool)
        cst.reg(-1.0, 1.0, 0.0, 0.125, 0.1 / 1024.0)
        p.memset(ones1[:], 1.0)
        p.memset(onesb[:], 1.0)
        for i in range(8):
            p.dma("sync", qsb[i][:], q_fd[128 * i:128 * (i + 1), :])
        p.dma("sync", jrow[:], jv)
        p.act(jrow[:], jrow[:], AF.Sin)
        p.red(jcol[:], jrow[:], OP.add)
        p.ts(jcol[:], jcol[:], 0.1 / 1024.0, op0=OP.mult)
        p.mm(psB[:, 0:1], ones1[:], jcol[:], start=True, stop=True)
        p.copy(jb[:], psB[:, 0:1])

        for h in range(16):
            par = h % 2
            ks_, vs_, qh_ = ksb[par], vsb[par], qh[par]
            p.dma("sync", ks_[:], kf[64 * h:64 * (h + 1), :])
            p.memset(vs_[:].rearrange("p (kc c) -> p kc c", c=65)[:, :, 64:65], 1.0)
            p.dma("sync", vs_[:].rearrange("p (kc c) -> p kc c", c=65)[:, :, 0:64],
                  vt[:, 64 * h:64 * (h + 1)].rearrange("(kc p) m -> p kc m", p=128))
            hi, hr = h // 2, 64 * (h % 2)
            p.copy(qh_[:], qsb[hi][hr:hr + 64, :])
            for qc in range(4):
                for kk in range(32):
                    ps = psS[kk % 2]
                    pt = ptb[kk % 2]
                    for j in range(2):
                        kc = 2 * kk + j
                        p.mm(ps[:, 512 * j:512 * (j + 1)],
                             ks_[0:64, 128 * kc:128 * (kc + 1)],
                             qh_[:, 512 * qc:512 * (qc + 1)],
                             start=True, stop=True)
                    p.act(pt[:], ps[:], AF.Exp, scale=0.125, bias=jb[:])
                    for j in range(2):
                        kc = 2 * kk + j
                        p.mm(psO[:, 0:512],
                             vs_[:, 65 * kc:65 * (kc + 1)],
                             pt[:, 512 * j:512 * (j + 1)],
                             start=(kk == 0 and j == 0),
                             stop=(kk == 31 and j == 1))
                p.recip(rl[:], psO[64:65, 0:512])
                p.copy(rlb[:], rl[:])
                p.mm(psB[0:64, 0:512], onesb[0:1, 0:64], rlb[0:1, :],
                     start=True, stop=True)
                p.act(rb_sb[:], psB[0:64, 0:512], AF.Copy)
                p.tt(osb[hi][hr:hr + 64, 512 * qc:512 * (qc + 1)],
                     psO[0:64, 0:512], rb_sb[:], OP.mult)

        emit_quat_load(p, oW, ob, wpos, wneg, bE_sb, scrcol)
        for oc in range(8):
            o, half = oc // 2, oc % 2
            for c0, cw in col_chunks(TLOC):
                ps = psS[(c0 // 512) % 2]
                first = True
                for ti, (c_, w_, s_) in enumerate(QTERMS[o]):
                    src = wpos if s_ > 0 else wneg
                    for kc in range(2):
                        p.mm(ps[:, 0:cw],
                             src[:, (w_ * 2 + kc) * 256 + 128 * half:
                                 (w_ * 2 + kc) * 256 + 128 * half + 128],
                             osb[c_ * 2 + kc][:, c0:c0 + cw],
                             start=first, stop=(ti == 3 and kc == 1))
                        first = False
                p.ts(scrF[:, c0:c0 + cw], ps[:, 0:cw], bE_sb[:, oc:oc + 1],
                     op0=OP.add)
            p.dma("sync", s2_f[128 * oc:128 * (oc + 1), :], scrF[:])
        p.emit(block, sems)
    return nc


def run_attn2(nc_a2, quat_results, inp):
    kf_b, vt_b = [], []
    for b in range(2):
        kf_b.append(np.concatenate(
            [np.asarray(quat_results[4 * b + i]["k_f"]) for i in range(4)], axis=1))
        vt_b.append(np.concatenate(
            [np.asarray(quat_results[4 * b + i]["v_t"]) for i in range(4)], axis=0))
    in_maps = []
    for c in range(NCORE):
        b = c // 4
        in_maps.append({
            "q_fd": np.asarray(quat_results[c]["q_f"]),
            "kf": np.ascontiguousarray(kf_b[b]),
            "vt": np.ascontiguousarray(vt_b[b]),
            "jv": np.asarray(inp["L1_attn_J"])[None, :],
            "oW": np.asarray(inp["L1_attn_oW"]), "ob": np.asarray(inp["L1_attn_ob"]),
        })
    res = run_bass_kernel_spmd(nc_a2, in_maps, core_ids=list(range(NCORE)))
    return [np.asarray(r["s2_f"]) for r in res.results]


NEXP = 8


def build_moe():
    nc = bass.Bass("TRN2", target_bir_lowering=False, detect_race_conditions=False)
    xn2b = nc.dram_tensor("xn2b", [D, TLOC], BF16, kind="ExternalInput").ap()
    x1T = nc.dram_tensor("x1T", [TLOC, D], F32, kind="ExternalInput").ap()
    s2T = nc.dram_tensor("s2T", [TLOC, D], F32, kind="ExternalInput").ap()
    wr = nc.dram_tensor("wr", [D, NEXP], F32, kind="ExternalInput").ap()
    br = nc.dram_tensor("br", [NEXP, 1], F32, kind="ExternalInput").ap()
    w1 = nc.dram_tensor("w1", [NEXP, D, 2048], F32, kind="ExternalInput").ap()
    b1 = nc.dram_tensor("b1", [NEXP, 2048], F32, kind="ExternalInput").ap()
    w2 = nc.dram_tensor("w2", [NEXP, 2048, D], F32, kind="ExternalInput").ap()
    b2 = nc.dram_tensor("b2", [NEXP, D], F32, kind="ExternalInput").ap()
    ident = nc.dram_tensor("ident", [128, 128], F32, kind="ExternalInput").ap()
    out_t = nc.dram_tensor("out_t", [TLOC, D], F32, kind="ExternalOutput").ap()

    with contextlib.ExitStack() as st:
        E = st.enter_context
        xn = [E(nc.sbuf_tensor(f"xn{i}", [128, TLOC], BF16)) for i in range(8)]
        hb = [E(nc.sbuf_tensor(f"hb{i}", [128, TLOC], BF16)) for i in range(16)]
        acc = [E(nc.sbuf_tensor(f"acc{i}", [128, D], F32)) for i in range(16)]
        w2all = E(nc.sbuf_tensor("w2all", [128, 16 * D], BF16))
        w1st = [E(nc.sbuf_tensor(f"w1st{i}", [128, 8 * 128], BF16)) for i in range(2)]
        wr_sb = E(nc.sbuf_tensor("wr_sb", [128, 64], BF16))
        br_sb = E(nc.sbuf_tensor("br_sb", [8, 1], F32))
        b1E = E(nc.sbuf_tensor("b1E", [128, 16], F32))
        b2row = E(nc.sbuf_tensor("b2row", [1, D], BF16))
        lg_sb = E(nc.sbuf_tensor("lg_sb", [8, TLOC], BF16))
        idbB = E(nc.sbuf_tensor("idbB", [128, 128], BF16))
        lgT = E(nc.sbuf_tensor("lgT", [128, 8], F32))
        gate = E(nc.sbuf_tensor("gate", [128, 128], F32))
        m1c = E(nc.sbuf_tensor("m1c", [128, 1], F32))
        m2c = E(nc.sbuf_tensor("m2c", [128, 1], F32))
        w1c = E(nc.sbuf_tensor("w1c", [128, 1], F32))
        w2c = E(nc.sbuf_tensor("w2c", [128, 1], F32))
        eq1 = E(nc.sbuf_tensor("eq1", [128, 8], F32))
        eq2 = E(nc.sbuf_tensor("eq2", [128, 8], F32))
        mskd = E(nc.sbuf_tensor("mskd", [128, 8], F32))
        tmp = [E(nc.sbuf_tensor(f"tmp{i}", [128, 512], F32)) for i in range(2)]
        onesb = E(nc.sbuf_tensor("onesb", [1, 128], BF16))
        cpool = E(nc.sbuf_tensor("cpool", [128, 4], F32))
        psA = [E(nc.psum_tensor(f"psA{i}", [128, 512], F32)) for i in range(2)]
        psLG = E(nc.psum_tensor("psLG", [8, TLOC], F32))
        psTR = E(nc.psum_tensor("psTR", [128, 8], BF16))
        sems = make_sems(nc, E)
        block = E(nc.Block())

        p = Prog(same_engine_edges=True)
        cst = Consts(nc, p, cpool)
        cst.reg(-1.0, 1.0, 0.0, NEG)
        p.memset(onesb[:], 1.0)
        for i in range(8):
            p.dma("sync", xn[i][:], xn2b[128 * i:128 * (i + 1), :])
        p.dma("gpsimd", wr_sb[:].rearrange("p (k m) -> p k m", m=8),
              wr.rearrange("(k p) m -> p k m", p=128))
        p.dma("sync", br_sb[:], br)
        p.dma("gpsimd", idbB[:], ident)
        for c0, cw in col_chunks(TLOC):
            for k in range(8):
                p.mm(psLG[:, c0:c0 + cw], wr_sb[:, 8 * k:8 * (k + 1)],
                     xn[k][:, c0:c0 + cw], start=(k == 0), stop=(k == 7))
        p.ts(lg_sb[:], psLG[:], br_sb[:], op0=OP.add)
        for tc in range(16):
            p.transpose(psTR[:], lg_sb[0:8, 128 * tc:128 * (tc + 1)], idbB[0:8, 0:8])
            p.copy(lgT[:], psTR[:])
            p.red(m1c[:], lgT[:], OP.max)
            p.ts(eq1[:], lgT[:], m1c[:], op0=OP.is_equal)
            p.stt(mskd[:], eq1[:], NEG, lgT[:], OP.mult, OP.add)
            p.red(m2c[:], mskd[:], OP.max)
            p.tt(w1c[:], m1c[:], m2c[:], OP.subtract)
            p.act(w1c[:], w1c[:], AF.Sigmoid)
            p.ts(w2c[:], w1c[:], -1.0, 1.0, op0=OP.mult, op1=OP.add)
            p.ts(eq2[:], lgT[:], m2c[:], op0=OP.is_equal)
            p.ts(eq1[:], eq1[:], w1c[:], op0=OP.mult)
            p.ts(eq2[:], eq2[:], w2c[:], op0=OP.mult)
            p.tt(gate[:, 8 * tc:8 * (tc + 1)], eq1[:], eq2[:], OP.add)
        for e in range(NEXP):
            p.dma("sync", b1E[:].rearrange("p (i) -> p i"),
                  b1[e].rearrange("(i p) -> p i", p=128),
                  allow_slow_non_contiguous=True)
            p.dma("gpsimd", b2row[:], b2[e:e + 1, :])
            for mc in range(16):
                ws_ = w1st[mc % 2]
                p.dma("gpsimd", ws_[:].rearrange("p (k m) -> p k m", m=128),
                      w1[e, :, 128 * mc:128 * (mc + 1)].rearrange(
                          "(k p) m -> p k m", p=128))
                p.dma("gpsimd", w2all[:, D * mc:D * (mc + 1)],
                      w2[e, 128 * mc:128 * (mc + 1), :])
                for ci, (c0, cw) in enumerate(col_chunks(TLOC)):
                    ps = psA[ci % 2]
                    for k in range(8):
                        p.mm(ps[:, 0:cw], ws_[:, 128 * k:128 * (k + 1)],
                             xn[k][:, c0:c0 + cw], start=(k == 0), stop=(k == 7))
                    p.act(hb[mc][:, c0:c0 + cw], ps[:, 0:cw], AF.Relu,
                          bias=b1E[:, mc:mc + 1])
            for tc in range(16):
                gcol = gate[:, 8 * tc + e:8 * tc + e + 1]
                for half in range(2):
                    ps = psA[(2 * tc + half) % 2]
                    tm_ = tmp[(2 * tc + half) % 2]
                    for mc in range(16):
                        p.mm(ps[:, 0:512], hb[mc][:, 128 * tc:128 * (tc + 1)],
                             w2all[:, D * mc + 512 * half:D * mc + 512 * (half + 1)],
                             start=(mc == 0), stop=False)
                    p.mm(ps[:, 0:512], onesb[:],
                         b2row[0:1, 512 * half:512 * (half + 1)],
                         start=False, stop=True)
                    if e == 0:
                        p.ts(acc[tc][:, 512 * half:512 * (half + 1)], ps[:, 0:512],
                             gcol, op0=OP.mult)
                    else:
                        p.ts(tm_[:], ps[:, 0:512], gcol, op0=OP.mult)
                        p.tt(acc[tc][:, 512 * half:512 * (half + 1)],
                             acc[tc][:, 512 * half:512 * (half + 1)], tm_[:], OP.add)
        for tc in range(16):
            for half in range(2):
                hs = slice(512 * half, 512 * (half + 1))
                tm_ = tmp[half]
                sc_ = tmp[1 - half]
                p.act(sc_[:], acc[tc][:, hs], AF.Silu)
                p.dma("sync", tm_[:], s2T[128 * tc:128 * (tc + 1), hs])
                p.tt(sc_[:], sc_[:], tm_[:], OP.mult)
                p.dma("sync", tm_[:], x1T[128 * tc:128 * (tc + 1), hs])
                p.tt(sc_[:], sc_[:], tm_[:], OP.add)
                p.dma("sync", out_t[128 * tc:128 * (tc + 1), hs], sc_[:])
        p.emit(block, sems)
    return nc


def run_moe(nc_moe, quat_results, x1_fm, s2_list, inp):
    in_maps = []
    for c in range(NCORE):
        t0 = c * TLOC
        in_maps.append({
            "xn2b": np.asarray(quat_results[c]["xn2b_o"]),
            "x1T": np.ascontiguousarray(x1_fm[:, t0:t0 + TLOC].T),
            "s2T": np.ascontiguousarray(np.asarray(s2_list[c]).T),
            "wr": np.asarray(inp["L1_ffn_Wr"]),
            "br": np.asarray(inp["L1_ffn_br"])[:, None],
            "w1": np.asarray(inp["L1_ffn_W1"]), "b1": np.asarray(inp["L1_ffn_b1"]),
            "w2": np.asarray(inp["L1_ffn_W2"]), "b2": np.asarray(inp["L1_ffn_b2"]),
            "ident": np.eye(128, dtype=np.float32),
        })
    res = run_bass_kernel_spmd(nc_moe, in_maps, core_ids=list(range(NCORE)))
    return [np.asarray(r["out_t"]) for r in res.results]


_NC_CACHE = {}


def _get(name, builder):
    if name not in _NC_CACHE:
        _NC_CACHE[name] = builder()
    return _NC_CACHE[name]


def kernel(**inp):
    x = np.asarray(inp["L0_attn_x"], np.float32)
    xT = np.ascontiguousarray(x.reshape(16384, D).T)

    wm = {k: np.asarray(inp[f"L0_attn_{k}"], np.float32) for k in
          ["rms_w", "in_proj", "conv_w", "conv_b", "x_proj", "dt_w", "dt_b"]}
    nc_pre = _get("pre", build_mamba_pre)
    xhs = make_xh(xT)
    in_maps = [{"xh": xhs[c], **wm} for c in range(NCORE)]
    pre = run_bass_kernel_spmd(nc_pre, in_maps, core_ids=list(range(NCORE))).results

    y_fm = run_scan_cached(pre, np.asarray(inp["L0_attn_A_log"], np.float32),
                           np.asarray(inp["L0_attn_D"], np.float32))

    x_fm = np.concatenate([np.asarray(r["xn_f"]) for r in pre], axis=1)
    consts = [attn_consts(c) for c in range(NCORE)]
    nc_attn = _get("attn", build_attn)
    nc_ffn = _get("ffn", build_ffn)
    for i in range(6):
        x_fm = run_attn(nc_attn, x_fm,
                        np.asarray(inp["L0_ffn_Wqkv"][i], np.float32),
                        np.asarray(inp["L0_ffn_q_scale"][i], np.float32),
                        np.asarray(inp["L0_ffn_k_scale"][i], np.float32),
                        np.asarray(inp["L0_ffn_Wo"][i], np.float32), consts)
        x_fm = run_ffn(nc_ffn, x_fm,
                       np.asarray(inp["L0_ffn_W1"][i], np.float32),
                       np.asarray(inp["L0_ffn_b1"][i], np.float32),
                       np.asarray(inp["L0_ffn_W2"][i], np.float32),
                       np.asarray(inp["L0_ffn_b2"][i], np.float32),
                       np.asarray(inp["L0_ffn_ln_g"][i], np.float32),
                       np.asarray(inp["L0_ffn_ln_b"][i], np.float32))

    x1_fm = run_mgate_cached(y_fm, pre,
                             np.asarray(inp["L0_attn_out_proj"], np.float32),
                             xT, x_fm)

    nc_quat = _get("quat", build_quat)
    quat = run_quat(nc_quat, x1_fm, inp)
    nc_a2 = _get("a2", build_attn2)
    s2 = run_attn2(nc_a2, quat, inp)
    nc_moe = _get("moe", build_moe)
    outs = run_moe(nc_moe, quat, x1_fm, s2, inp)
    out = np.concatenate(outs, axis=0).reshape(2, 8192, D).astype(np.float32)
    return out


def run_scan_cached(pre_results, a_log_full, d_full):
    delta_fm = np.concatenate([np.asarray(r["delta_f"]) for r in pre_results], axis=1)
    xi_fm = np.concatenate([np.asarray(r["xi_f"]) for r in pre_results], axis=1)
    bTf = np.concatenate([np.asarray(r["bT"]) for r in pre_results], axis=1)
    cTf = np.concatenate([np.asarray(r["cT"]) for r in pre_results], axis=1)
    nc = _get("scan", build_scan)
    in_maps = []
    for c in range(NCORE):
        r0 = c * 256
        in_maps.append({
            "deltaS": np.ascontiguousarray(delta_fm[r0:r0 + 256]),
            "xiS": np.ascontiguousarray(xi_fm[r0:r0 + 256]),
            "bTf": bTf, "cTf": cTf,
            "a_log": np.ascontiguousarray(a_log_full[r0:r0 + 256]),
            "d_vec": np.ascontiguousarray(d_full[r0:r0 + 256]),
        })
    res = run_bass_kernel_spmd(nc, in_maps, core_ids=list(range(NCORE)))
    return np.concatenate([np.asarray(r["yS"]) for r in res.results], axis=0)


def run_mgate_cached(y_fm, pre_results, out_proj, xT, g_fm):
    nc = _get("mgate", build_mgate)
    in_maps = []
    for c in range(NCORE):
        t0 = c * TLOC
        in_maps.append({
            "yT": np.ascontiguousarray(y_fm[:, t0:t0 + TLOC]),
            "z_f": np.asarray(pre_results[c]["z_f"]),
            "xn_f": np.asarray(pre_results[c]["xn_f"]),
            "out_proj": out_proj,
            "xT": np.ascontiguousarray(xT[:, t0:t0 + TLOC]),
            "g_f": np.ascontiguousarray(g_fm[:, t0:t0 + TLOC]),
        })
    res = run_bass_kernel_spmd(nc, in_maps, core_ids=list(range(NCORE)))
    return np.concatenate([np.asarray(r["x1_f"]) for r in res.results], axis=1)


# version 12 (final)
# speedup vs baseline: 1.2052x; 1.0017x over previous
import contextlib
import numpy as np
import ml_dtypes
import concourse.bass as bass
import concourse.mybir as mybir
from concourse.bass_utils import run_bass_kernel_spmd
from concourse.alu_op_type import AluOpType as OP

F32 = mybir.dt.float32
BF16 = mybir.dt.bfloat16
AF = mybir.ActivationFunctionType
AX = mybir.AxisListType

NCORE = 8
TLOC = 2048
D = 1024
DIN = 2048
DST = 16
DTR = 64
NEG = -1.0e30

BF = ml_dtypes.bfloat16


def _ap_name(x):
    try:
        return x.tensor.name
    except AttributeError:
        return None


def _is_ap(x):
    return hasattr(x, "tensor")


class Prog:

    def __init__(self, same_engine_edges=False):
        self.ops = []
        self.const_tile = None
        self.psum_names = set()
        self.same_engine_edges = same_engine_edges

    def _rec(self, eng, method, args, kwargs, dma, reads, writes):
        for a in list(reads) + list(writes):
            if not isinstance(a, str) and hasattr(a, "space"):
                try:
                    if "PSUM" in str(a.space):
                        self.psum_names.add(a.tensor.name)
                except Exception:
                    pass
        rd = set(a if isinstance(a, str) else _ap_name(a) for a in reads)
        wr = set(a if isinstance(a, str) else _ap_name(a) for a in writes)
        rd.discard(None); wr.discard(None)
        self.ops.append((eng, method, args, kwargs, dma, rd, wr))

    def _const_read(self, reads, scalars):
        if self.const_tile is not None and any(
                isinstance(s, float) for s in scalars):
            reads = list(reads) + [self.const_tile]
        return reads

    def dma(self, eng, out, in_, **kw):
        self._rec(eng, "dma_start", (), {"out": out, "in_": in_, **kw},
                  True, [in_], [out])

    def mm(self, out, lhsT, rhs, **kw):
        self._rec("tensor", "matmul", (out, lhsT, rhs), kw, False,
                  [lhsT, rhs, out], [out])

    def transpose(self, out, in_, ident):
        self._rec("tensor", "transpose", (out, in_, ident), {}, False,
                  [in_, ident], [out])

    def act(self, out, in_, func, bias=None, scale=None, accum_out=None):
        kw = {}
        reads = [in_]
        writes = [out]
        if bias is not None:
            kw["bias"] = bias
            (reads if _is_ap(bias) else []).append(bias)
        if scale is not None:
            kw["scale"] = scale
            (reads if _is_ap(scale) else []).append(scale)
        if accum_out is not None:
            kw["accum_out"] = accum_out
            writes.append(accum_out)
        if self.const_tile is not None:
            reads = list(reads) + [self.const_tile]
        self._rec("scalar", "activation", (out, in_, func), kw, False,
                  reads, writes)

    def tt(self, out, a, b, op, eng="vector"):
        self._rec(eng, "tensor_tensor", (out, a, b), {"op": op}, False,
                  [a, b], [out])

    def ts(self, out, in0, s1, s2=None, op0=OP.mult, op1=None, eng="vector",
           accum_out=None):
        reads = [in0] + [s for s in (s1, s2) if _is_ap(s)]
        reads = self._const_read(reads, [s for s in (s1, s2)
                                         if not _is_ap(s) and s is not None])
        writes = [out] + ([accum_out] if accum_out is not None else [])
        if op1 is None:
            self._rec(eng, "tensor_scalar", (out, in0, s1, None, op0),
                      {"accum_out": accum_out}, False, reads, writes)
        else:
            self._rec(eng, "tensor_scalar", (out, in0, s1, s2, op0, op1),
                      {"accum_out": accum_out}, False, reads, writes)

    def stt(self, out, in0, scalar, in1, op0, op1, eng="vector"):
        reads = [in0, in1] + ([scalar] if _is_ap(scalar) else [])
        reads = self._const_read(reads, [scalar] if not _is_ap(scalar) else [])
        self._rec(eng, "scalar_tensor_tensor", (out, in0, scalar, in1, op0, op1),
                  {}, False, reads, writes=[out])

    def copy(self, out, in_, eng="vector"):
        self._rec(eng, "tensor_copy", (out, in_), {}, False, [in_], [out])

    def ttr(self, out, in0, in1, scalar, op0, op1, accum_out):
        reads = [in0, in1]
        reads = self._const_read(reads, [scalar] if not _is_ap(scalar) else [])
        self._rec("vector", "tensor_tensor_reduce", (out, in0, in1, 1.0, scalar,
                  op0, op1, accum_out), {}, False, reads, [out, accum_out])

    def red(self, out, in_, op, negate=None):
        self._rec("vector", "tensor_reduce", (out, in_, AX.X, op),
                  {"negate": negate}, False, [in_], [out])

    def recip(self, out, in_):
        self._rec("vector", "reciprocal", (out, in_), {}, False, [in_], [out])

    def scan(self, out, a, u, initial):
        reads = [a, u] + ([initial] if _is_ap(initial) else [])
        self._rec("vector", "tensor_tensor_scan",
                  (out, a, u, initial, OP.mult, OP.add), {}, False,
                  reads, [out])

    def memset(self, ap, val, eng="vector"):
        self._rec(eng, "memset", (ap, val), {}, False, [], [ap])

    TRUNC = None

    def emit(self, block, sems):
        if Prog.TRUNC is not None:
            self.ops = self.ops[:Prog.TRUNC]
        n = len(self.ops)

        last_writer = {}
        readers = {}
        deps = [set() for _ in range(n)]
        psum = self.psum_names
        for i, (eng, _m, _a, _k, dma, rd, wr) in enumerate(self.ops):
            for t in rd:
                if t in last_writer:
                    deps[i].add(last_writer[t])
            for t in wr:
                for r in readers.get(t, ()):
                    deps[i].add(r)
                if t in last_writer:
                    deps[i].add(last_writer[t])
            for t in wr:
                last_writer[t] = i
                readers[t] = []
            for t in rd:
                readers.setdefault(t, []).append(i)
                if t in psum:
                    last_writer[t] = i
                    readers[t] = []
        prev_dma = {}
        for i, (eng, _m, _a, _k, dma, _r, _w) in enumerate(self.ops):
            if dma:
                if eng in prev_dma:
                    deps[i].add(prev_dma[eng])
                prev_dma[eng] = i

        kept = [[] for _ in range(n)]
        need_inc = [False] * n
        for i, (eng, _m, _a, _k, dma, _r, _w) in enumerate(self.ops):
            if dma:
                need_inc[i] = True
            for j in deps[i]:
                jeng, jdma = self.ops[j][0], self.ops[j][4]
                if (not jdma) and jeng == eng and not (
                        self.same_engine_edges and eng != "tensor"):
                    continue
                kept[i].append(j)
                need_inc[j] = True

        sem_key = [None] * n
        cumval = [0] * n
        counters = {}
        for i, (eng, _m, _a, _k, dma, _r, _w) in enumerate(self.ops):
            key = ("dma_" + eng) if dma else eng
            sem_key[i] = key
            if need_inc[i]:
                counters[key] = counters.get(key, 0) + (16 if dma else 1)
                cumval[i] = counters[key]
        totals = dict(counters)

        reqs = [dict() for _ in range(n)]
        for i in range(n):
            for j in kept[i]:
                k, v = sem_key[j], cumval[j]
                if reqs[i].get(k, 0) < v:
                    reqs[i][k] = v

        ops = self.ops

        def engine_body(name):
            def body(eng_obj):
                guaranteed = {}
                for i, (ename, method, args, kwargs, dma, _r, _w) in enumerate(ops):
                    if ename != name:
                        continue
                    for k, v in sorted(reqs[i].items()):
                        if guaranteed.get(k, 0) >= v:
                            continue
                        eng_obj.wait_ge(sems[k], v)
                        guaranteed[k] = v
                    kwargs2 = {k: v for k, v in kwargs.items() if v is not None} \
                        if method in ("activation", "tensor_scalar") else kwargs
                    inst = getattr(eng_obj, method)(*args, **kwargs2)
                    if need_inc[i]:
                        inst.then_inc(sems[sem_key[i]], 16 if dma else 1)
                for k, v in totals.items():
                    eng_obj.wait_ge(sems[k], v)
            return body

        block.sync(engine_body("sync"))
        block.gpsimd(engine_body("gpsimd"))
        block.tensor(engine_body("tensor"))
        block.vector(engine_body("vector"))
        block.scalar(engine_body("scalar"))


def make_sems(nc, E):
    return {k: E(nc.semaphore(f"sem_{k}")) for k in
            ["tensor", "vector", "scalar", "gpsimd", "sync",
             "dma_sync", "dma_gpsimd", "dma_scalar"]}


def col_chunks(cols, nch=512):
    return [(c, min(nch, cols - c)) for c in range(0, cols, nch)]


class Consts:

    def __init__(self, nc, p, tile, dtype=F32):
        self.nc, self.p, self.tile, self.dtype = nc, p, tile, dtype
        self.n = 0
        self.seen = {}
        p.const_tile = tile[:].tensor.name

    def reg(self, *vals):
        for v in vals:
            v = float(v)
            if v in self.seen:
                continue
            ap = self.tile[:, self.n:self.n + 1]
            self.nc.const_aps.register(self.dtype, v, ap)
            self.p.memset(ap, v)
            self.seen[v] = ap
            self.n += 1


HALO1 = 128
XCOLS = HALO1 + TLOC


def build_mamba_pre():
    nc = bass.Bass("TRN2", target_bir_lowering=False, detect_race_conditions=False)
    xh = nc.dram_tensor("xh", [D, XCOLS], F32, kind="ExternalInput").ap()
    rms_w = nc.dram_tensor("rms_w", [D], F32, kind="ExternalInput").ap()
    in_proj = nc.dram_tensor("in_proj", [D, 2 * DIN], F32, kind="ExternalInput").ap()
    conv_w = nc.dram_tensor("conv_w", [DIN, 4], F32, kind="ExternalInput").ap()
    conv_b = nc.dram_tensor("conv_b", [DIN], F32, kind="ExternalInput").ap()
    x_proj = nc.dram_tensor("x_proj", [DIN, 96], F32, kind="ExternalInput").ap()
    dt_w = nc.dram_tensor("dt_w", [DTR, DIN], F32, kind="ExternalInput").ap()
    dt_b = nc.dram_tensor("dt_b", [DIN], F32, kind="ExternalInput").ap()

    xn_f = nc.dram_tensor("xn_f", [D, TLOC], F32, kind="ExternalOutput").ap()
    z_f = nc.dram_tensor("z_f", [DIN, TLOC], F32, kind="ExternalOutput").ap()
    xi_f = nc.dram_tensor("xi_f", [DIN, TLOC], F32, kind="ExternalOutput").ap()
    delta_f = nc.dram_tensor("delta_f", [DIN, TLOC], F32, kind="ExternalOutput").ap()
    bT = nc.dram_tensor("bT", [DST, TLOC], F32, kind="ExternalOutput").ap()
    cT = nc.dram_tensor("cT", [DST, TLOC], F32, kind="ExternalOutput").ap()

    with contextlib.ExitStack() as st:
        E = st.enter_context
        xc = [E(nc.sbuf_tensor(f"xc{i}", [128, XCOLS], F32)) for i in range(8)]
        xmb = [E(nc.sbuf_tensor(f"xmb{i}", [128, XCOLS], BF16)) for i in range(8)]
        sq = E(nc.sbuf_tensor("sq", [128, 512], F32))
        rows = E(nc.sbuf_tensor("rows", [1, XCOLS], F32))
        row2 = E(nc.sbuf_tensor("row2", [1, XCOLS], F32))
        crow = E(nc.sbuf_tensor("crow", [1, XCOLS], F32))
        ones1 = E(nc.sbuf_tensor("ones1", [1, 128], F32))
        ones1c = E(nc.sbuf_tensor("ones1c", [128, 1], F32))
        bc1 = E(nc.sbuf_tensor("bc1", [128, XCOLS], F32))
        bc2 = E(nc.sbuf_tensor("bc2", [128, XCOLS], F32))
        rmsw_sb = E(nc.sbuf_tensor("rmsw_sb", [128, 8], F32))
        wstage = [E(nc.sbuf_tensor(f"wstage{i}", [128, 1024], BF16)) for i in range(2)]
        cw_sb = E(nc.sbuf_tensor("cw_sb", [128, 64], F32))
        cb_sb = E(nc.sbuf_tensor("cb_sb", [128, 16], F32))
        dtb_sb = E(nc.sbuf_tensor("dtb_sb", [128, 16], F32))
        xp_sb = E(nc.sbuf_tensor("xp_sb", [128, 16 * 96], BF16))
        dtw_sb = E(nc.sbuf_tensor("dtw_sb", [64, DIN], BF16))
        xi_raw = E(nc.sbuf_tensor("xi_raw", [128, XCOLS], F32))
        xi_acc = E(nc.sbuf_tensor("xi_acc", [128, TLOC], F32))
        xib = E(nc.sbuf_tensor("xib", [128, TLOC], BF16))
        dbc_sb = E(nc.sbuf_tensor("dbc_sb", [96, TLOC], F32))
        dtlb = E(nc.sbuf_tensor("dtlb", [64, TLOC], BF16))
        sp1 = E(nc.sbuf_tensor("sp1", [128, TLOC], F32))
        sp2 = E(nc.sbuf_tensor("sp2", [128, TLOC], F32))
        ps_a_r = [E(nc.psum_tensor(f"ps_a{i}", [128, 512], F32)) for i in range(2)]
        ps_r1 = E(nc.psum_tensor("ps_r1", [1, 512], F32))
        ps_r2 = ps_r1
        ps_bc = E(nc.psum_tensor("ps_bc", [128, 512], F32))
        ps_dbc = E(nc.psum_tensor("ps_dbc", [96, TLOC], F32))
        cpool = E(nc.sbuf_tensor("cpool", [128, 8], F32))
        sems = make_sems(nc, E)
        block = E(nc.Block())

        p = Prog()
        cst = Consts(nc, p, cpool)
        cst.reg(1e-5, 1.0 / D, -0.5, -1.0, 1.0, 0.0)
        for i in range(8):
            p.dma("sync", xc[i][:], xh[128 * i:128 * (i + 1), :])
        p.dma("sync", rmsw_sb[:], rms_w.rearrange("(i p) -> p i", p=128),
              allow_slow_non_contiguous=True)
        p.dma("sync", cw_sb[:].rearrange("p (i k) -> p i k", k=4),
              conv_w.rearrange("(i p) k -> p i k", p=128))
        p.dma("sync", cb_sb[:], conv_b.rearrange("(i p) -> p i", p=128),
              allow_slow_non_contiguous=True)
        p.dma("sync", dtb_sb[:], dt_b.rearrange("(i p) -> p i", p=128),
              allow_slow_non_contiguous=True)
        p.dma("gpsimd", xp_sb[:].rearrange("p (i j) -> p i j", j=96),
              x_proj.rearrange("(i p) j -> p i j", p=128))
        p.dma("gpsimd", dtw_sb[:], dt_w)
        p.memset(ones1[:], 1.0)
        p.memset(ones1c[:], 1.0)

        for c0, cw in col_chunks(XCOLS):
            for i in range(8):
                p.mm(ps_r1[:, 0:cw], ones1c[:], xc[i][:, c0:c0 + cw],
                     start=(i == 0), stop=(i == 7))
            p.copy(rows[:, c0:c0 + cw], ps_r1[:, 0:cw])
            for i in range(8):
                p.act(sq[:, 0:cw], xc[i][:, c0:c0 + cw], AF.Square)
                p.mm(ps_r2[:, 0:cw], ones1c[:], sq[:, 0:cw],
                     start=(i == 0), stop=(i == 7))
            p.copy(row2[:, c0:c0 + cw], ps_r2[:, 0:cw])
        p.ts(rows[:], rows[:], 1.0 / D, op0=OP.mult)
        p.act(crow[:], rows[:], AF.Square)
        p.ts(row2[:], row2[:], 1.0 / D, op0=OP.mult)
        p.tt(crow[:], row2[:], crow[:], OP.subtract)
        p.act(crow[:], crow[:], AF.Ln, bias=1e-5)
        p.act(crow[:], crow[:], AF.Exp, scale=-0.5)
        for c0, cw in col_chunks(XCOLS):
            p.mm(ps_bc[:, 0:cw], ones1[:], crow[:, c0:c0 + cw], start=True, stop=True)
            p.copy(bc1[:, c0:c0 + cw], ps_bc[:, 0:cw])
        p.tt(crow[:], rows[:], crow[:], OP.mult)
        p.ts(crow[:], crow[:], -1.0, op0=OP.mult)
        for c0, cw in col_chunks(XCOLS):
            p.mm(ps_bc[:, 0:cw], ones1[:], crow[:, c0:c0 + cw], start=True, stop=True)
            p.copy(bc2[:, c0:c0 + cw], ps_bc[:, 0:cw])
        for i in range(8):
            p.tt(xc[i][:], xc[i][:], bc1[:], OP.mult)
            p.tt(xc[i][:], xc[i][:], bc2[:], OP.add)
            p.dma("sync", xn_f[128 * i:128 * (i + 1), :], xc[i][:, HALO1:XCOLS])

        for c0, cw in col_chunks(XCOLS):
            for i in range(8):
                p.act(sq[:, 0:cw], xc[i][:, c0:c0 + cw], AF.Square)
                p.mm(ps_r1[:, 0:cw], ones1c[:], sq[:, 0:cw],
                     start=(i == 0), stop=(i == 7))
            p.copy(rows[:, c0:c0 + cw], ps_r1[:, 0:cw])
        p.act(rows[:], rows[:], AF.Ln, scale=1.0 / D, bias=1e-5)
        p.act(rows[:], rows[:], AF.Exp, scale=-0.5)
        for c0, cw in col_chunks(XCOLS):
            p.mm(ps_bc[:, 0:cw], ones1[:], rows[:, c0:c0 + cw], start=True, stop=True)
            p.copy(bc1[:, c0:c0 + cw], ps_bc[:, 0:cw])
        for i in range(8):
            p.stt(xmb[i][:], xc[i][:], rmsw_sb[:, i:i + 1], bc1[:], OP.mult, OP.mult)

        for m in range(16):
            w = wstage[m % 2]
            p.dma("gpsimd", w[:],
                  in_proj[:, 128 * m:128 * (m + 1)].rearrange("(k p) m -> p k m", p=128))
            for ci, (c0, cw) in enumerate(col_chunks(XCOLS)):
                ps = ps_a_r[ci % 2]
                for k in range(8):
                    p.mm(ps[:, 0:cw], w[:, 128 * k:128 * (k + 1)],
                         xmb[k][:, c0:c0 + cw], start=(k == 0), stop=(k == 7))
                p.act(xi_raw[:, c0:c0 + cw], ps[:, 0:cw], AF.Copy)
            p.ts(xi_acc[:], xi_raw[:, HALO1 - 3:HALO1 - 3 + TLOC],
                 cw_sb[:, 4 * m:4 * m + 1], op0=OP.mult)
            for kk in range(1, 4):
                p.stt(xi_acc[:], xi_raw[:, HALO1 - 3 + kk:HALO1 - 3 + kk + TLOC],
                      cw_sb[:, 4 * m + kk:4 * m + kk + 1], xi_acc[:],
                      OP.mult, OP.add)
            p.act(xi_acc[:], xi_acc[:], AF.Silu, bias=cb_sb[:, m:m + 1])
            p.dma("sync", xi_f[128 * m:128 * (m + 1), :], xi_acc[:])
            p.copy(xib[:], xi_acc[:])
            for c0, cw in col_chunks(TLOC):
                p.mm(ps_dbc[:, c0:c0 + cw], xp_sb[:, 96 * m:96 * (m + 1)],
                     xib[:, c0:c0 + cw], start=(m == 0), stop=(m == 15))

        for m in range(16, 32):
            w = wstage[m % 2]
            p.dma("gpsimd", w[:],
                  in_proj[:, 128 * m:128 * (m + 1)].rearrange("(k p) m -> p k m", p=128))
            for ci, (c0, cw) in enumerate(col_chunks(TLOC)):
                ps = ps_a_r[ci % 2]
                for k in range(8):
                    p.mm(ps[:, 0:cw], w[:, 128 * k:128 * (k + 1)],
                         xmb[k][:, HALO1 + c0:HALO1 + c0 + cw],
                         start=(k == 0), stop=(k == 7))
                p.act(sp1[:, c0:c0 + cw], ps[:, 0:cw], AF.Copy)
            p.dma("sync", z_f[128 * (m - 16):128 * (m - 15), :], sp1[:])

        p.act(dbc_sb[:], ps_dbc[:], AF.Copy)
        p.dma("sync", bT, dbc_sb[64:80, :])
        p.dma("sync", cT, dbc_sb[80:96, :])
        p.copy(dtlb[:], dbc_sb[0:64, :])
        for dm in range(16):
            for ci, (c0, cw) in enumerate(col_chunks(TLOC)):
                ps = ps_a_r[ci % 2]
                p.mm(ps[:, 0:cw], dtw_sb[:, 128 * dm:128 * (dm + 1)],
                     dtlb[:, c0:c0 + cw], start=True, stop=True)
                p.act(sp1[:, c0:c0 + cw], ps[:, 0:cw], AF.Abs,
                      bias=dtb_sb[:, dm:dm + 1])
                p.act(sp1[:, c0:c0 + cw], sp1[:, c0:c0 + cw], AF.Exp, scale=-1.0)
                p.act(sp1[:, c0:c0 + cw], sp1[:, c0:c0 + cw], AF.Ln, bias=1.0)
                p.ts(sp2[:, c0:c0 + cw], ps[:, 0:cw], dtb_sb[:, dm:dm + 1], 0.0,
                     op0=OP.add, op1=OP.max)
                p.tt(sp2[:, c0:c0 + cw], sp2[:, c0:c0 + cw], sp1[:, c0:c0 + cw],
                     OP.add)
            p.dma("sync", delta_f[128 * dm:128 * (dm + 1), :], sp2[:])

        p.emit(block, sems)
    return nc


def shard_tokens(a_t):
    return [a_t[c * TLOC:(c + 1) * TLOC] for c in range(NCORE)]


def make_xh(xT):
    outs = []
    for c in range(NCORE):
        t0 = c * TLOC
        if c % 4 == 0:
            halo = np.zeros((D, HALO1), np.float32)
        else:
            halo = xT[:, t0 - HALO1:t0]
        outs.append(np.ascontiguousarray(
            np.concatenate([halo, xT[:, t0:t0 + TLOC]], axis=1)))
    return outs


def run_mamba_pre(x, w):
    xT = np.ascontiguousarray(x.reshape(16384, D).T)
    xhs = make_xh(xT)
    nc = build_mamba_pre()
    in_maps = []
    for c in range(NCORE):
        in_maps.append({
            "xh": xhs[c],
            "rms_w": w["rms_w"], "in_proj": w["in_proj"],
            "conv_w": w["conv_w"], "conv_b": w["conv_b"],
            "x_proj": w["x_proj"], "dt_w": w["dt_w"], "dt_b": w["dt_b"],
        })
    res = run_bass_kernel_spmd(nc, in_maps, core_ids=list(range(NCORE)))
    return res.results


TFULL = 16384
TCH = 4096


def build_scan():
    nc = bass.Bass("TRN2", target_bir_lowering=False, detect_race_conditions=False)
    deltaS = nc.dram_tensor("deltaS", [256, TFULL], F32, kind="ExternalInput").ap()
    xiS = nc.dram_tensor("xiS", [256, TFULL], F32, kind="ExternalInput").ap()
    bTf = nc.dram_tensor("bTf", [DST, TFULL], F32, kind="ExternalInput").ap()
    cTf = nc.dram_tensor("cTf", [DST, TFULL], F32, kind="ExternalInput").ap()
    a_log = nc.dram_tensor("a_log", [256, DST], F32, kind="ExternalInput").ap()
    d_vec = nc.dram_tensor("d_vec", [256], F32, kind="ExternalInput").ap()
    yS = nc.dram_tensor("yS", [256, TFULL], F32, kind="ExternalOutput").ap()

    with contextlib.ExitStack() as st:
        E = st.enter_context
        dl = E(nc.sbuf_tensor("dl", [128, TCH], F32))
        xi = E(nc.sbuf_tensor("xi", [128, TCH], F32))
        wv = E(nc.sbuf_tensor("wv", [128, TCH], F32))
        av = E(nc.sbuf_tensor("av", [128, TCH], F32))
        uv = E(nc.sbuf_tensor("uv", [128, TCH], F32))
        hv = E(nc.sbuf_tensor("hv", [128, TCH], F32))
        yv = E(nc.sbuf_tensor("yv", [128, TCH], F32))
        bbc = [E(nc.sbuf_tensor(f"bbc{i}", [128, TCH], F32)) for i in range(2)]
        cbc = [E(nc.sbuf_tensor(f"cbc{i}", [128, TCH], F32)) for i in range(2)]
        a_sb = E(nc.sbuf_tensor("a_sb", [128, DST], F32))
        d_sb = E(nc.sbuf_tensor("d_sb", [128, 2], F32))
        carry = E(nc.sbuf_tensor("carry", [128, DST], F32))
        cpool = E(nc.sbuf_tensor("cpool", [128, 4], F32))
        sems = make_sems(nc, E)
        block = E(nc.Block())

        p = Prog(same_engine_edges=True)
        cst = Consts(nc, p, cpool)
        cst.reg(-1.0, 1.0, 0.0)
        p.dma("sync", d_sb[:], d_vec.rearrange("(i p) -> p i", p=128),
              allow_slow_non_contiguous=True)
        for blk in range(2):
            p.dma("sync", a_sb[:], a_log[128 * blk:128 * (blk + 1), :])
            p.act(a_sb[:], a_sb[:], AF.Exp)
            p.ts(a_sb[:], a_sb[:], -1.0, op0=OP.mult)
            for tc in range(4):
                t0 = tc * TCH
                p.dma("sync", dl[:], deltaS[128 * blk:128 * (blk + 1), t0:t0 + TCH])
                p.dma("sync", xi[:], xiS[128 * blk:128 * (blk + 1), t0:t0 + TCH])
                p.tt(wv[:], dl[:], xi[:], OP.mult)
                p.ts(yv[:], xi[:], d_sb[:, blk:blk + 1], op0=OP.mult)
                for s in range(DST):
                    bb, cb = bbc[s % 2], cbc[s % 2]
                    p.dma("gpsimd", bb[:],
                          bTf[s:s + 1, t0:t0 + TCH].broadcast_to([128, TCH]))
                    p.dma("sync", cb[:],
                          cTf[s:s + 1, t0:t0 + TCH].broadcast_to([128, TCH]))
                    p.act(av[:], dl[:], AF.Exp, scale=a_sb[:, s:s + 1])
                    p.tt(uv[:], wv[:], bb[:], OP.mult)
                    init = 0.0 if tc % 2 == 0 else carry[:, s:s + 1]
                    p.scan(hv[:], av[:], uv[:], init)
                    p.copy(carry[:, s:s + 1], hv[:, TCH - 1:TCH])
                    p.tt(av[:], hv[:], cb[:], OP.mult)
                    p.tt(yv[:], yv[:], av[:], OP.add)
                p.dma("sync", yS[128 * blk:128 * (blk + 1), t0:t0 + TCH], yv[:])
        p.emit(block, sems)
    return nc


def run_scan(pre_results, a_log_full, d_full):
    delta_fm = np.concatenate([np.asarray(r["delta_f"]) for r in pre_results], axis=1)
    xi_fm = np.concatenate([np.asarray(r["xi_f"]) for r in pre_results], axis=1)
    bTf = np.concatenate([np.asarray(r["bT"]) for r in pre_results], axis=1)
    cTf = np.concatenate([np.asarray(r["cT"]) for r in pre_results], axis=1)
    nc = build_scan()
    in_maps = []
    for c in range(NCORE):
        r0 = c * 256
        in_maps.append({
            "deltaS": np.ascontiguousarray(delta_fm[r0:r0 + 256]),
            "xiS": np.ascontiguousarray(xi_fm[r0:r0 + 256]),
            "bTf": bTf, "cTf": cTf,
            "a_log": np.ascontiguousarray(a_log_full[r0:r0 + 256]),
            "d_vec": np.ascontiguousarray(d_full[r0:r0 + 256]),
        })
    res = run_bass_kernel_spmd(nc, in_maps, core_ids=list(range(NCORE)))
    return np.concatenate([np.asarray(r["yS"]) for r in res.results], axis=0)


def build_mgate():
    nc = bass.Bass("TRN2", target_bir_lowering=False, detect_race_conditions=False)
    yT = nc.dram_tensor("yT", [DIN, TLOC], F32, kind="ExternalInput").ap()
    z_f = nc.dram_tensor("z_f", [DIN, TLOC], F32, kind="ExternalInput").ap()
    xn_f = nc.dram_tensor("xn_f", [D, TLOC], F32, kind="ExternalInput").ap()
    out_proj = nc.dram_tensor("out_proj", [DIN, D], F32, kind="ExternalInput").ap()
    xT = nc.dram_tensor("xT", [D, TLOC], F32, kind="ExternalInput").ap()
    g_f = nc.dram_tensor("g_f", [D, TLOC], F32, kind="ExternalInput").ap()
    x1_f = nc.dram_tensor("x1_f", [D, TLOC], F32, kind="ExternalOutput").ap()

    with contextlib.ExitStack() as st:
        E = st.enter_context
        gb = [E(nc.sbuf_tensor(f"gb{i}", [128, TLOC], BF16)) for i in range(16)]
        zt = E(nc.sbuf_tensor("zt", [128, TLOC], F32))
        yt = E(nc.sbuf_tensor("yt", [128, TLOC], F32))
        xnt = E(nc.sbuf_tensor("xnt", [128, TLOC], F32))
        st_out = E(nc.sbuf_tensor("st_out", [128, TLOC], F32))
        gch = E(nc.sbuf_tensor("gch", [128, TLOC], F32))
        xch = E(nc.sbuf_tensor("xch", [128, TLOC], F32))
        op_sb = E(nc.sbuf_tensor("op_sb", [128, 16 * D], BF16))
        ps = [E(nc.psum_tensor(f"ps{i}", [128, 512], F32)) for i in range(2)]
        sems = make_sems(nc, E)
        block = E(nc.Block())

        p = Prog(same_engine_edges=True)
        p.dma("gpsimd", op_sb[:].rearrange("p (k m) -> p k m", m=D),
              out_proj.rearrange("(k p) m -> p k m", p=128))
        for i in range(16):
            p.dma("sync", zt[:], z_f[128 * i:128 * (i + 1), :])
            p.dma("sync", yt[:], yT[128 * i:128 * (i + 1), :])
            p.act(zt[:], zt[:], AF.Silu)
            p.tt(zt[:], zt[:], yt[:], OP.mult)
            p.copy(gb[i][:], zt[:])
        for m in range(8):
            p.dma("sync", xnt[:], xn_f[128 * m:128 * (m + 1), :])
            p.dma("sync", gch[:], g_f[128 * m:128 * (m + 1), :])
            p.dma("sync", xch[:], xT[128 * m:128 * (m + 1), :])
            for ci, (c0, cw) in enumerate(col_chunks(TLOC)):
                pp = ps[ci % 2]
                for k in range(16):
                    p.mm(pp[:, 0:cw], op_sb[:, k * D + 128 * m:k * D + 128 * (m + 1)],
                         gb[k][:, c0:c0 + cw], start=(k == 0), stop=(k == 15))
                p.tt(st_out[:, c0:c0 + cw], pp[:, 0:cw], xnt[:, c0:c0 + cw], OP.add)
            p.act(gch[:], gch[:], AF.Silu)
            p.tt(gch[:], gch[:], st_out[:], OP.mult)
            p.tt(gch[:], gch[:], xch[:], OP.add)
            p.dma("sync", x1_f[128 * m:128 * (m + 1), :], gch[:])
        p.emit(block, sems)
    return nc


def run_mgate(y_fm, pre_results, out_proj):
    nc = build_mgate()
    in_maps = []
    for c in range(NCORE):
        in_maps.append({
            "yT": np.ascontiguousarray(y_fm[:, c * TLOC:(c + 1) * TLOC]),
            "z_f": np.asarray(pre_results[c]["z_f"]),
            "xn_f": np.asarray(pre_results[c]["xn_f"]),
            "out_proj": out_proj,
        })
    res = run_bass_kernel_spmd(nc, in_maps, core_ids=list(range(NCORE)))
    return [np.asarray(r["s_f"]) for r in res.results]


HATT = 257
AXC = HATT + TLOC
QKC = 2304
NW = 8
WSZ = 256


def build_attn():
    nc = bass.Bass("TRN2", target_bir_lowering=False, detect_race_conditions=False)
    xh = nc.dram_tensor("xh", [D, AXC], F32, kind="ExternalInput").ap()
    wqkv = nc.dram_tensor("wqkv", [D, 3 * D], F32, kind="ExternalInput").ap()
    wo = nc.dram_tensor("wo", [D, D], F32, kind="ExternalInput").ap()
    qsE = nc.dram_tensor("qsE", [128, 1], F32, kind="ExternalInput").ap()
    ksE = nc.dram_tensor("ksE", [128, 1], F32, kind="ExternalInput").ap()
    csq = nc.dram_tensor("csq", [128, QKC], F32, kind="ExternalInput").ap()
    snq = nc.dram_tensor("snq", [128, QKC], F32, kind="ExternalInput").ap()
    csk = nc.dram_tensor("csk", [128, QKC], F32, kind="ExternalInput").ap()
    snk = nc.dram_tensor("snk", [128, QKC], F32, kind="ExternalInput").ap()
    maskQ = nc.dram_tensor("maskQ", [2, 128, 512], F32, kind="ExternalInput").ap()
    maskQ0 = nc.dram_tensor("maskQ0", [2, 128, 512], F32, kind="ExternalInput").ap()
    maskTd = nc.dram_tensor("maskTd", [4, 128, 512], F32, kind="ExternalInput").ap()
    maskT0d = nc.dram_tensor("maskT0d", [4, 128, 512], F32, kind="ExternalInput").ap()
    bcast2 = nc.dram_tensor("bcast2", [2, 128], F32, kind="ExternalInput").ap()
    perm = nc.dram_tensor("perm", [128, 128], F32, kind="ExternalInput").ap()
    ident = nc.dram_tensor("ident", [128, 128], F32, kind="ExternalInput").ap()
    x_out = nc.dram_tensor("x_out", [D, TLOC], F32, kind="ExternalOutput").ap()

    with contextlib.ExitStack() as st:
        E = st.enter_context
        xsb = [E(nc.sbuf_tensor(f"xsb{i}", [128, AXC], BF16)) for i in range(8)]
        vsb = [E(nc.sbuf_tensor(f"vsb{i}", [128, 16 * 65], BF16)) for i in range(18)]
        attn_sb = [E(nc.sbuf_tensor(f"attn{i}", [128, TLOC], BF16)) for i in range(8)]
        wv_all = E(nc.sbuf_tensor("wv_all", [128, 8 * D], BF16))
        csqb = E(nc.sbuf_tensor("csqb", [128, QKC], BF16))
        snqb = E(nc.sbuf_tensor("snqb", [128, QKC], BF16))
        cskb = E(nc.sbuf_tensor("cskb", [128, QKC], BF16))
        snkb = E(nc.sbuf_tensor("snkb", [128, QKC], BF16))
        mQ = E(nc.sbuf_tensor("mQ", [128, 2 * 512], F32))
        mQ0 = E(nc.sbuf_tensor("mQ0", [128, 2 * 512], F32))
        mT = E(nc.sbuf_tensor("mT", [128, 4 * 512], F32))
        mT0 = E(nc.sbuf_tensor("mT0", [128, 2 * 512], F32))
        bc2f = E(nc.sbuf_tensor("bc2f", [2, 128], F32))
        permb = E(nc.sbuf_tensor("permb", [128, 128], BF16))
        idb = E(nc.sbuf_tensor("idb", [128, 128], BF16))
        idf = E(nc.sbuf_tensor("idf", [128, 128], F32))
        ones2 = E(nc.sbuf_tensor("ones2", [128, 2], F32))
        ones1b = E(nc.sbuf_tensor("ones1b", [1, 128], BF16))
        onesKb = E(nc.sbuf_tensor("onesKb", [128, 1], BF16))
        qse_sb = E(nc.sbuf_tensor("qse_sb", [128, 1], F32))
        kse_sb = E(nc.sbuf_tensor("kse_sb", [128, 1], F32))
        wqk = [E(nc.sbuf_tensor("wqk0", [128, 2 * 1024], BF16))] * 2
        wost = E(nc.sbuf_tensor("wost", [128, 1024], BF16))
        qt = E(nc.sbuf_tensor("qt", [128, QKC], F32))
        qnb = E(nc.sbuf_tensor("qnb", [128, QKC], BF16))
        knb = E(nc.sbuf_tensor("knb", [128, QKC], BF16))
        ubb = [E(nc.sbuf_tensor(f"ubb{i}", [128, 512], BF16)) for i in range(2)]
        rbc = [E(nc.sbuf_tensor(f"rbc{i}", [128, 512], F32)) for i in range(2)]
        nrm = E(nc.sbuf_tensor("nrm", [2, 512], F32))
        nrmb = E(nc.sbuf_tensor("nrmb", [1, 512], BF16))
        scr = [E(nc.sbuf_tensor(f"scr{i}", [128, 512], F32)) for i in range(2)]
        rotb = E(nc.sbuf_tensor("rotb", [128, 512], BF16))
        t2b = E(nc.sbuf_tensor("t2b", [128, 512], BF16))
        scrT = [E(nc.sbuf_tensor(f"scrT{i}", [128, 512], F32)) for i in range(2)]
        ptb = [E(nc.sbuf_tensor(f"ptb{i}", [128, 512], BF16)) for i in range(2)]
        mstk = E(nc.sbuf_tensor("mstk", [128, 4], F32))
        negrow = [E(nc.sbuf_tensor(f"negrow{i}", [1, 512], BF16)) for i in range(2)]
        cpool = E(nc.sbuf_tensor("cpool", [128, 8], F32))
        psA = [E(nc.psum_tensor(f"psA{i}", [128, 512], F32)) for i in range(2)]
        psB = [E(nc.psum_tensor(f"psB{i}", [128, 512], F32)) for i in range(2)]
        psPV = [E(nc.psum_tensor(f"psPV{i}", [65, 512], F32)) for i in range(2)]
        psL = [E(nc.psum_tensor(f"psL{i}", [1, 512], F32)) for i in range(2)]
        sems = make_sems(nc, E)
        block = E(nc.Block())

        p = Prog()
        cst = Consts(nc, p, cpool)
        cst.reg(-1.0, 1.0, 0.0, 1e-12, 8.0)

        for i in range(8):
            p.dma("gpsimd", xsb[i][:], xh[128 * i:128 * (i + 1), :])
        p.dma("gpsimd", csqb[:], csq)
        p.dma("gpsimd", snqb[:], snq)
        p.dma("gpsimd", cskb[:], csk)
        p.dma("gpsimd", snkb[:], snk)
        p.dma("sync", mQ[:].rearrange("p (i c) -> p i c", c=512),
              maskQ.rearrange("i p c -> p i c"))
        p.dma("sync", mQ0[:].rearrange("p (i c) -> p i c", c=512),
              maskQ0.rearrange("i p c -> p i c"))
        p.dma("sync", mT[:].rearrange("p (i c) -> p i c", c=512),
              maskTd.rearrange("i p c -> p i c"))
        p.dma("sync", mT0[:].rearrange("p (i c) -> p i c", c=512),
              maskT0d[0:2].rearrange("i p c -> p i c"))
        p.dma("sync", bc2f[:], bcast2)
        p.dma("gpsimd", permb[:], perm)
        p.dma("gpsimd", idb[:], ident)
        p.dma("sync", idf[:], ident)
        p.dma("sync", qse_sb[:], qsE)
        p.dma("sync", kse_sb[:], ksE)
        for k in range(8):
            p.dma("gpsimd", wv_all[:, D * k:D * (k + 1)],
                  wqkv[128 * k:128 * (k + 1), 2 * D:3 * D])
        p.memset(ones2[:], 0.0)
        p.memset(ones2[0:64, 0:1], 1.0)
        p.memset(ones2[64:128, 1:2], 1.0)
        p.memset(ones1b[:], 1.0)
        p.memset(onesKb[:], 1.0)

        def xs_cols(i, c0, cwidth):
            off = 1 if i < 4 else 0
            return xsb[i][:, c0 + off:c0 + off + cwidth]

        for tc in range(18):
            p.memset(vsb[tc][:].rearrange("p (h c) -> p h c", c=65)[:, :, 64:65],
                     1.0)
            for half in range(2):
                ps = psA[(2 * tc + half) % 2]
                for k in range(8):
                    p.mm(ps[:, 0:512], xs_cols(k, 128 * tc, 128),
                         wv_all[:, D * k + 512 * half:D * k + 512 * (half + 1)],
                         start=(k == 0), stop=(k == 7))
                p.copy(vsb[tc][:].rearrange("p (h c) -> p h c", c=65)
                       [:, 8 * half:8 * (half + 1), 0:64],
                       ps[:, 0:512].rearrange("p (h c) -> p h c", c=64))

        def qk_make(dst_b, cs_b, sn_b, scale_col, lo=0):
            for ci, (c0, cw) in enumerate(col_chunks(QKC - lo)):
                c0 += lo
                s0 = scr[ci % 2]
                p.act(s0[:, 0:cw], qt[:, c0:c0 + cw], AF.Square)
                p.mm(psB[0][0:2, 0:cw], ones2[:], s0[:, 0:cw], start=True, stop=True)
                p.act(nrm[0:2, 0:cw], psB[0][0:2, 0:cw], AF.Sqrt)
                p.recip(nrm[0:2, 0:cw], nrm[0:2, 0:cw])
                p.ts(nrm[0:2, 0:cw], nrm[0:2, 0:cw], 1e12, op0=OP.min)
                p.mm(psB[0][:, 0:cw], bc2f[:], nrm[0:2, 0:cw],
                     start=True, stop=True)
                p.act(rbc[ci % 2][:, 0:cw], psB[0][:, 0:cw], AF.Copy)
                p.stt(qt[:, c0:c0 + cw], qt[:, c0:c0 + cw], scale_col,
                      rbc[ci % 2][:, 0:cw], OP.mult, OP.mult)
            for ci, (c0, cw) in enumerate(col_chunks(QKC - lo)):
                c0 += lo
                ub = ubb[ci % 2]
                ps = psB[ci % 2]
                p.copy(ub[:, 0:cw], qt[:, c0:c0 + cw])
                p.mm(ps[:, 0:cw], permb[:], ub[:, 0:cw],
                     start=True, stop=True)
                p.act(rotb[:, 0:cw], ps[:, 0:cw], AF.Copy)
                p.tt(rotb[:, 0:cw], rotb[:, 0:cw], sn_b[:, c0:c0 + cw], OP.mult)
                p.tt(t2b[:, 0:cw], ub[:, 0:cw], cs_b[:, c0:c0 + cw], OP.mult)
                p.tt(dst_b[:, c0:c0 + cw], t2b[:, 0:cw], rotb[:, 0:cw], OP.add)

        for hp in range(8):
            par = hp % 2
            wq = wqk[par]
            p.dma("gpsimd", wq[:, 0:1024].rearrange("p (k m) -> p k m", m=128),
                  wqkv[:, 128 * hp:128 * (hp + 1)].rearrange("(k p) m -> p k m", p=128))
            p.dma("gpsimd", wq[:, 1024:2048].rearrange("p (k m) -> p k m", m=128),
                  wqkv[:, D + 128 * hp:D + 128 * (hp + 1)].rearrange("(k p) m -> p k m", p=128))
            for c0, cw in col_chunks(TLOC):
                c0 += 256
                ps = psA[(c0 // 512) % 2]
                for k in range(8):
                    p.mm(ps[:, 0:cw], wq[:, 128 * k:128 * (k + 1)],
                         xs_cols(k, c0, cw), start=(k == 0), stop=(k == 7))
                p.act(qt[:, c0:c0 + cw], ps[:, 0:cw], AF.Copy)
            qk_make(qnb, csqb, snqb, qse_sb[:, 0:1], lo=256)
            for c0, cw in col_chunks(QKC):
                ps = psA[(c0 // 512) % 2]
                for k in range(8):
                    p.mm(ps[:, 0:cw], wq[:, 1024 + 128 * k:1024 + 128 * (k + 1)],
                         xs_cols(k, c0, cw), start=(k == 0), stop=(k == 7))
                p.act(qt[:, c0:c0 + cw], ps[:, 0:cw], AF.Copy)
            qk_make(knb, cskb, snkb, kse_sb[:, 0:1])
            qn, kn = qnb, knb

            for w in range(NW):
                wp = w % 2
                qcol0 = 256 + 256 * w
                kcol0 = 256 * w
                mq = mQ0 if w == 0 else mQ
                for hh in range(2):
                    base = 64 * hh
                    for qb in range(2):
                        ps = psA[(2 * hh + qb) % 2]
                        s0 = scr[(2 * hh + qb) % 2]
                        p.mm(ps[:, 0:512],
                             qn[base:base + 64, qcol0 + 128 * qb:qcol0 + 128 * (qb + 1)],
                             kn[base:base + 64, kcol0:kcol0 + 512],
                             start=True, stop=True)
                        p.tt(s0[:, 0:512], ps[:, 0:512],
                             mq[:, 512 * qb:512 * (qb + 1)], OP.add)
                        p.red(mstk[:, 2 * hh + qb:2 * hh + qb + 1], s0[:, 0:512],
                              OP.max)
                for col in range(4):
                    hh, qb = col // 2, col % 2
                    p.transpose(psL[0][0:1, 0:128], mstk[:, col:col + 1], idf[:])
                    p.ts(negrow[wp][0:1, 256 * hh + 128 * qb:256 * hh + 128 * (qb + 1)],
                         psL[0][0:1, 0:128], -1.0, op0=OP.mult)
                for kc in range(4):
                    ps = psB[kc % 2]
                    sT = scrT[kc % 2]
                    pt = ptb[kc % 2]
                    for hh in range(2):
                        base = 64 * hh
                        p.mm(ps[:, 256 * hh:256 * (hh + 1)],
                             kn[base:base + 64, kcol0 + 128 * kc:kcol0 + 128 * (kc + 1)],
                             qn[base:base + 64, qcol0:qcol0 + WSZ],
                             start=(hh == 0), stop=False,
                             tile_position=(base, 0))
                        p.mm(ps[:, 256 * hh:256 * (hh + 1)], ones1b[:],
                             negrow[wp][0:1, 256 * hh:256 * hh + WSZ],
                             start=False, stop=(hh == 1))
                    mt_ap = (mT0[:, 512 * kc:512 * (kc + 1)]
                             if (w == 0 and kc < 2)
                             else mT[:, 512 * kc:512 * (kc + 1)])
                    p.tt(sT[:, 0:512], ps[:, 0:512], mt_ap, OP.add)
                    p.act(pt[:, 0:512], sT[:, 0:512], AF.Exp, scale=8.0)
                    for hh in range(2):
                        h_glob = 2 * hp + hh
                        p.mm(psPV[wp][0:65, 256 * hh:256 * (hh + 1)],
                             vsb[2 * w + kc][:, 65 * h_glob:65 * (h_glob + 1)],
                             pt[:, 256 * hh:256 * (hh + 1)],
                             start=(kc == 0 and hh == 0),
                             stop=(kc == 3 and hh == 1))
                p.recip(nrm[0:1, 0:512], psPV[wp][64:65, 0:512])
                p.copy(nrmb[0:1, 0:512], nrm[0:1, 0:512])
                p.mm(psB[0][0:64, 0:512], ones1b[0:1, 0:64], nrmb[0:1, 0:512],
                     start=True, stop=True)
                p.act(scr[1][0:64, 0:512], psB[0][0:64, 0:512], AF.Copy)
                for hh in range(2):
                    p.tt(attn_sb[hp][64 * hh:64 * (hh + 1), 256 * w:256 * (w + 1)],
                         psPV[wp][0:64, 256 * hh:256 * (hh + 1)],
                         scr[1][0:64, 256 * hh:256 * (hh + 1)], OP.mult)

        out_sb = qt
        for m in range(8):
            p.dma("gpsimd", wost[:].rearrange("p (k m) -> p k m", m=128),
                  wo[:, 128 * m:128 * (m + 1)].rearrange("(k p) m -> p k m", p=128))
            for c0, cw in col_chunks(TLOC):
                par2 = (c0 // 512) % 2
                ps = psA[par2]
                xs_ = scr[par2]
                p.dma("sync", xs_[:, 0:cw],
                      xh[128 * m:128 * (m + 1), HATT + c0:HATT + c0 + cw])
                for k in range(8):
                    p.mm(ps[:, 0:cw], wost[:, 128 * k:128 * (k + 1)],
                         attn_sb[k][:, c0:c0 + cw], start=(k == 0), stop=(k == 7))
                p.tt(out_sb[:, c0:c0 + cw], ps[:, 0:cw], xs_[:, 0:cw], OP.add)
            p.dma("sync", x_out[128 * m:128 * (m + 1), :], out_sb[:, 0:TLOC])
        p.emit(block, sems)
    return nc


def attn_consts(core):
    DIM_HEAD, WINDOW = 64, 256
    n = 8192
    lt0 = (core % 4) * TLOC
    pos = np.arange(lt0 - 256, lt0 + TLOC, dtype=np.float64)
    half = np.arange(0, DIM_HEAD, 2, dtype=np.float64)
    inv_freq = 1.0 / 10000.0 ** (half / DIM_HEAD)
    freqs = pos[:, None] * inv_freq[None, :]
    freqs = np.concatenate([freqs, freqs], axis=1)
    base_scale = (half + 0.4 * DIM_HEAD) / (1.4 * DIM_HEAD)
    power = (pos - n // 2) / (WINDOW // 2)
    scale = base_scale[None, :] ** power[:, None]
    scale = np.concatenate([scale, scale], axis=1)
    cos, sin = np.cos(freqs), np.sin(freqs)
    csq = (cos * scale).T
    snq = (sin * scale).T
    csk = (cos / scale).T
    snk = (sin / scale).T
    two = lambda a: np.ascontiguousarray(
        np.concatenate([a, a], axis=0).astype(np.float32))
    csq, snq, csk, snk = map(two, (csq, snq, csk, snk))

    i_ = np.arange(256)[:, None]
    j_ = np.arange(512)[None, :]
    bad = ~((i_ <= j_) & (j_ <= i_ + 256))
    mq = np.where(bad, NEG, 0.0).astype(np.float32)
    maskQ = np.stack([mq[0:128], mq[128:256]])
    mt_full = np.where(bad.T, NEG, 0.0).astype(np.float32)
    dd = lambda m: np.concatenate([m, m], axis=1)
    maskT = np.stack([dd(mt_full[128 * kc:128 * (kc + 1)]) for kc in range(4)])
    if core % 4 == 0:
        mq0 = mq.copy()
        mq0[:, 0:256] = NEG
        maskQ0 = np.stack([mq0[0:128], mq0[128:256]])
        mt0 = mt_full.copy()
        mt0[0:256, :] = NEG
        maskT0 = np.stack([dd(mt0[128 * kc:128 * (kc + 1)]) for kc in range(4)])
    else:
        maskQ0, maskT0 = maskQ, maskT

    P = np.zeros((128, 128), np.float32)
    for blk in range(2):
        for dd in range(32):
            P[blk * 64 + dd + 32, blk * 64 + dd] = -1.0
            P[blk * 64 + dd, blk * 64 + dd + 32] = 1.0
    bcast2 = np.zeros((2, 128), np.float32)
    bcast2[0, 0:64] = 1.0
    bcast2[1, 64:128] = 1.0
    return {
        "csq": csq, "snq": snq, "csk": csk, "snk": snk,
        "maskQ": np.ascontiguousarray(maskQ), "maskQ0": np.ascontiguousarray(maskQ0),
        "maskTd": np.ascontiguousarray(maskT), "maskT0d": np.ascontiguousarray(maskT0),
        "bcast2": bcast2, "perm": P, "ident": np.eye(128, dtype=np.float32),
    }


def run_attn(nc_attn, x_fm, wqkv_i, qs_i, ks_i, wo_i, consts):
    in_maps = []
    for c in range(NCORE):
        t0 = c * TLOC
        if c % 4 == 0:
            halo = np.zeros((D, HATT), np.float32)
        else:
            halo = x_fm[:, t0 - HATT:t0]
        xh = np.ascontiguousarray(np.concatenate([halo, x_fm[:, t0:t0 + TLOC]], 1))
        m = {"xh": xh, "wqkv": wqkv_i, "wo": wo_i,
             "qsE": np.ascontiguousarray(np.tile(qs_i, 2)[:, None]),
             "ksE": np.ascontiguousarray(np.tile(ks_i, 2)[:, None])}
        m.update(consts[c])
        in_maps.append(m)
    res = run_bass_kernel_spmd(nc_attn, in_maps, core_ids=list(range(NCORE)))
    return np.concatenate([np.asarray(r["x_out"]) for r in res.results], axis=1)


DFF = 2730
NMC = 22


def build_ffn():
    nc = bass.Bass("TRN2", target_bir_lowering=False, detect_race_conditions=False)
    xh = nc.dram_tensor("xh", [D, 1 + TLOC], F32, kind="ExternalInput").ap()
    w1a = nc.dram_tensor("w1a", [D, DFF], F32, kind="ExternalInput").ap()
    w1g = nc.dram_tensor("w1g", [D, DFF], F32, kind="ExternalInput").ap()
    w2 = nc.dram_tensor("w2", [DFF, D], F32, kind="ExternalInput").ap()
    gE = nc.dram_tensor("gE", [128, 8], F32, kind="ExternalInput").ap()
    bE = nc.dram_tensor("bE", [128, 8], F32, kind="ExternalInput").ap()
    b1aE = nc.dram_tensor("b1aE", [128, NMC], F32, kind="ExternalInput").ap()
    b1gE = nc.dram_tensor("b1gE", [128, NMC], F32, kind="ExternalInput").ap()
    b2E = nc.dram_tensor("b2E", [128, 8], F32, kind="ExternalInput").ap()
    x_out = nc.dram_tensor("x_out", [D, TLOC], F32, kind="ExternalOutput").ap()

    with contextlib.ExitStack() as st:
        E = st.enter_context
        xnb = [E(nc.sbuf_tensor(f"xnb{i}", [128, TLOC], BF16)) for i in range(8)]
        xs32 = E(nc.sbuf_tensor("xs32", [128, 1 + TLOC], F32))
        sq = E(nc.sbuf_tensor("sq", [128, 512], F32))
        rows = E(nc.sbuf_tensor("rows", [1, TLOC], F32))
        row2 = E(nc.sbuf_tensor("row2", [1, TLOC], F32))
        crow = E(nc.sbuf_tensor("crow", [1, TLOC], F32))
        bc1b = E(nc.sbuf_tensor("bc1b", [128, TLOC], BF16))
        bc2b = E(nc.sbuf_tensor("bc2b", [128, TLOC], BF16))
        ones1 = E(nc.sbuf_tensor("ones1", [1, 128], F32))
        ones1c = E(nc.sbuf_tensor("ones1c", [128, 1], F32))
        ones1cb = E(nc.sbuf_tensor("ones1cb", [128, 1], BF16))
        gsb = E(nc.sbuf_tensor("gsb", [128, 8], F32))
        bsb = E(nc.sbuf_tensor("bsb", [128, 8], F32))
        b1as = E(nc.sbuf_tensor("b1as", [128, NMC], F32))
        b1gs = E(nc.sbuf_tensor("b1gs", [128, NMC], F32))
        b2s = E(nc.sbuf_tensor("b2s", [128, 8], F32))
        wst_r = [E(nc.sbuf_tensor(f"wst_{i}", [128, 8 * 128], BF16)) for i in range(2)]
        wst2_r = [E(nc.sbuf_tensor(f"wst2_{i}", [128, 8 * 128], BF16)) for i in range(2)]
        w2st = E(nc.sbuf_tensor("w2st", [128, NMC * 128], BF16))
        haf = E(nc.sbuf_tensor("haf", [128, TLOC], F32))
        gel = E(nc.sbuf_tensor("gel", [128, TLOC], F32))
        cpool = E(nc.sbuf_tensor("cpool", [128, 8], F32))
        psA = E(nc.psum_tensor("psA", [128, 512], F32))
        psB = E(nc.psum_tensor("psB", [128, 512], F32))
        psR1 = E(nc.psum_tensor("psR1", [1, 512], F32))
        psR2 = E(nc.psum_tensor("psR2", [1, 512], F32))
        psBC = E(nc.psum_tensor("psBC", [128, 512], F32))
        sems = make_sems(nc, E)
        block = E(nc.Block())
        st2 = contextlib.ExitStack()
        xbr = [st2.enter_context(nc.sbuf_tensor(f"xbr{i}", [128, 1 + TLOC], BF16))
               for i in range(8)]

        p = Prog()
        cst = Consts(nc, p, cpool)
        cst.reg(-1.0, 1.0, 0.0, 1e-5, 1.0 / D)
        p.dma("sync", gsb[:], gE)
        p.dma("sync", bsb[:], bE)
        p.dma("sync", b1as[:], b1aE)
        p.dma("sync", b1gs[:], b1gE)
        p.dma("sync", b2s[:], b2E)
        p.memset(ones1[:], 1.0)
        p.memset(ones1c[:], 1.0)
        p.memset(ones1cb[:], 1.0)

        def shx(i, c0, cw):
            off = 1 if i < 4 else 0
            return xbr[i][:, c0 + off:c0 + off + cw]

        for i in range(8):
            p.dma("gpsimd", xbr[i][:], xh[128 * i:128 * (i + 1), :])
        for c0, cw in col_chunks(TLOC):
            for i in range(8):
                p.mm(psR1[:, 0:cw], ones1cb[:], shx(i, c0, cw),
                     start=(i == 0), stop=(i == 7))
            p.copy(rows[:, c0:c0 + cw], psR1[:, 0:cw])
            for i in range(8):
                p.act(sq[:, 0:cw], shx(i, c0, cw), AF.Square)
                p.mm(psR2[:, 0:cw], ones1c[:], sq[:, 0:cw],
                     start=(i == 0), stop=(i == 7))
            p.copy(row2[:, c0:c0 + cw], psR2[:, 0:cw])
        p.ts(rows[:], rows[:], 1.0 / D, op0=OP.mult)
        p.act(crow[:], rows[:], AF.Square)
        p.ts(row2[:], row2[:], 1.0 / D, op0=OP.mult)
        p.tt(crow[:], row2[:], crow[:], OP.subtract)
        p.act(crow[:], crow[:], AF.Ln, bias=1e-5)
        p.act(crow[:], crow[:], AF.Exp, scale=-0.5)
        for c0, cw in col_chunks(TLOC):
            p.mm(psBC[:, 0:cw], ones1[:], crow[:, c0:c0 + cw], start=True, stop=True)
            p.copy(bc1b[:, c0:c0 + cw], psBC[:, 0:cw])
        p.tt(crow[:], rows[:], crow[:], OP.mult)
        p.ts(crow[:], crow[:], -1.0, op0=OP.mult)
        for c0, cw in col_chunks(TLOC):
            p.mm(psBC[:, 0:cw], ones1[:], crow[:, c0:c0 + cw], start=True, stop=True)
            p.copy(bc2b[:, c0:c0 + cw], psBC[:, 0:cw])
        for i in range(8):
            for c0, cw in col_chunks(TLOC):
                p.tt(xnb[i][:, c0:c0 + cw], shx(i, c0, cw), bc1b[:, c0:c0 + cw],
                     OP.mult)
            p.tt(xnb[i][:], xnb[i][:], bc2b[:], OP.add)
            p.ts(xnb[i][:], xnb[i][:], gsb[:, i:i + 1], bsb[:, i:i + 1],
                 op0=OP.mult, op1=OP.add)

        st2.close()
        tb = [E(nc.sbuf_tensor(f"tb{i}", [128, TLOC], BF16)) for i in range(NMC)]

        for mc in range(NMC):
            mw = min(128, DFF - 128 * mc)
            wst = wst_r[mc % 2]
            wst2 = wst2_r[mc % 2]
            p.dma("gpsimd", wst[:, 0:8 * mw].rearrange("p (k m) -> p k m", m=mw),
                  w1a[:, 128 * mc:128 * mc + mw].rearrange("(k p) m -> p k m", p=128))
            p.dma("gpsimd", wst2[:, 0:8 * mw].rearrange("p (k m) -> p k m", m=mw),
                  w1g[:, 128 * mc:128 * mc + mw].rearrange("(k p) m -> p k m", p=128))
            for c0, cw in col_chunks(TLOC):
                for k in range(8):
                    p.mm(psA[0:mw, 0:cw], wst[:, mw * k:mw * (k + 1)],
                         xnb[k][:, c0:c0 + cw], start=(k == 0), stop=(k == 7))
                for k in range(8):
                    p.mm(psB[0:mw, 0:cw], wst2[:, mw * k:mw * (k + 1)],
                         xnb[k][:, c0:c0 + cw], start=(k == 0), stop=(k == 7))
                p.ts(haf[0:mw, c0:c0 + cw], psA[0:mw, 0:cw],
                     b1as[0:mw, mc:mc + 1], op0=OP.add)
                p.act(gel[0:mw, c0:c0 + cw], psB[0:mw, 0:cw], AF.Gelu,
                      bias=b1gs[0:mw, mc:mc + 1])
            p.tt(tb[mc][0:mw, :], haf[0:mw, :], gel[0:mw, :], OP.mult)

        for m2 in range(8):
            p.dma("gpsimd",
                  w2st[:, 0:21 * 128].rearrange("p (k m) -> p k m", m=128),
                  w2[0:2688, 128 * m2:128 * (m2 + 1)].rearrange("(k p) m -> p k m", p=128))
            p.dma("gpsimd", w2st[0:42, 21 * 128:22 * 128],
                  w2[2688:2730, 128 * m2:128 * (m2 + 1)])
            p.dma("sync", xs32[:, 0:TLOC], xh[128 * m2:128 * (m2 + 1), 1:1 + TLOC])
            for ci, (c0, cw) in enumerate(col_chunks(TLOC)):
                ps = psA if ci % 2 == 0 else psB
                for mc in range(NMC):
                    mw = min(128, DFF - 128 * mc)
                    p.mm(ps[:, 0:cw], w2st[0:mw, 128 * mc:128 * mc + 128],
                         tb[mc][0:mw, c0:c0 + cw], start=(mc == 0),
                         stop=(mc == NMC - 1))
                p.stt(haf[:, c0:c0 + cw], ps[:, 0:cw], b2s[:, m2:m2 + 1],
                      xs32[:, c0:c0 + cw], OP.add, OP.add)
            p.dma("sync", x_out[128 * m2:128 * (m2 + 1), :], haf[:, 0:TLOC])
        p.emit(block, sems)
    return nc


def run_ffn(nc_ffn, x_fm, W1_i, b1_i, W2_i, b2_i, lng_i, lnb_i):
    rear = lambda v: np.ascontiguousarray(v.reshape(-1, 128).T)
    padE = lambda v, n: np.ascontiguousarray(
        np.pad(v, (0, n * 128 - len(v))).reshape(n, 128).T)
    in_maps = []
    for c in range(NCORE):
        t0 = c * TLOC
        if c % 4 == 0:
            halo = np.zeros((D, 1), np.float32)
        else:
            halo = x_fm[:, t0 - 1:t0]
        xh = np.ascontiguousarray(np.concatenate([halo, x_fm[:, t0:t0 + TLOC]], 1))
        in_maps.append({
            "xh": xh,
            "w1a": np.ascontiguousarray(W1_i[:, :DFF]),
            "w1g": np.ascontiguousarray(W1_i[:, DFF:]),
            "w2": W2_i,
            "gE": rear(lng_i), "bE": rear(lnb_i),
            "b1aE": padE(b1_i[:DFF], NMC), "b1gE": padE(b1_i[DFF:], NMC),
            "b2E": rear(b2_i),
        })
    res = run_bass_kernel_spmd(nc_ffn, in_maps, core_ids=list(range(NCORE)))
    return np.concatenate([np.asarray(r["x_out"]) for r in res.results], axis=1)


def build_combine1():
    nc = bass.Bass("TRN2", target_bir_lowering=False, detect_race_conditions=False)
    xT = nc.dram_tensor("xT", [D, TLOC], F32, kind="ExternalInput").ap()
    s_f = nc.dram_tensor("s_f", [D, TLOC], F32, kind="ExternalInput").ap()
    g_f = nc.dram_tensor("g_f", [D, TLOC], F32, kind="ExternalInput").ap()
    x1_f = nc.dram_tensor("x1_f", [D, TLOC], F32, kind="ExternalOutput").ap()

    with contextlib.ExitStack() as st:
        E = st.enter_context
        a = E(nc.sbuf_tensor("a", [128, TLOC], F32))
        b = E(nc.sbuf_tensor("b", [128, TLOC], F32))
        c = E(nc.sbuf_tensor("c", [128, TLOC], F32))
        sems = make_sems(nc, E)
        block = E(nc.Block())
        p = Prog()
        for i in range(8):
            p.dma("sync", a[:], g_f[128 * i:128 * (i + 1), :])
            p.dma("sync", b[:], s_f[128 * i:128 * (i + 1), :])
            p.dma("sync", c[:], xT[128 * i:128 * (i + 1), :])
            p.act(a[:], a[:], AF.Silu)
            p.tt(a[:], a[:], b[:], OP.mult)
            p.tt(a[:], a[:], c[:], OP.add)
            p.dma("sync", x1_f[128 * i:128 * (i + 1), :], a[:])
        p.emit(block, sems)
    return nc


QTERMS = [
    [(0, 0, 1), (1, 1, -1), (2, 2, -1), (3, 3, -1)],
    [(1, 0, 1), (0, 1, 1), (3, 2, 1), (2, 3, -1)],
    [(2, 0, 1), (3, 1, -1), (0, 2, 1), (1, 3, 1)],
    [(3, 0, 1), (2, 1, 1), (1, 2, -1), (0, 3, 1)],
]


def emit_quat_load(p, Wd, bd, wpos, wneg, bE_sb, scrcol):
    for w in range(4):
        for kc in range(2):
            p.dma("gpsimd", wpos[:, (w * 2 + kc) * 256:(w * 2 + kc + 1) * 256],
                  Wd[w, 128 * kc:128 * (kc + 1), :])
    p.ts(wneg[:], wpos[:], -1.0, op0=OP.mult)
    p.dma("sync", scrcol[:].rearrange("p (w h) -> p w h", h=2),
          bd.rearrange("w (h p) -> p w h", p=128),
          allow_slow_non_contiguous=True)
    for o in range(4):
        for h in range(2):
            terms = QTERMS[o]
            (c0_, w0_, s0_) = terms[0]
            p.ts(bE_sb[:, o * 2 + h:o * 2 + h + 1],
                 scrcol[:, w0_ * 2 + h:w0_ * 2 + h + 1], float(s0_), op0=OP.mult)
            for (c_, w_, s_) in terms[1:]:
                p.stt(bE_sb[:, o * 2 + h:o * 2 + h + 1],
                      scrcol[:, w_ * 2 + h:w_ * 2 + h + 1], float(s_),
                      bE_sb[:, o * 2 + h:o * 2 + h + 1], OP.mult, OP.add)


def emit_quat_mm_fm(p, xb, wpos, wneg, bE_sb, psA, out_cb, quatnorm, scrq, nsq,
                    cols=TLOC):
    for oc in range(8):
        o, half = oc // 2, oc % 2
        for c0, cw in col_chunks(cols):
            first = True
            for (c_, w_, s_) in QTERMS[o]:
                src = wpos if s_ > 0 else wneg
                for kc in range(2):
                    p.mm(psA[:, 0:cw],
                         src[:, (w_ * 2 + kc) * 256 + 128 * half:
                             (w_ * 2 + kc) * 256 + 128 * half + 128],
                         xb[c_ * 2 + kc][:, c0:c0 + cw],
                         start=first, stop=(kc == 1 and (c_, w_, s_) == QTERMS[o][3]))
                    first = False
            p.ts(scrq[oc][:, c0:c0 + cw], psA[:, 0:cw],
                 bE_sb[:, oc:oc + 1], op0=OP.add)


def build_quat():
    nc = bass.Bass("TRN2", target_bir_lowering=False, detect_race_conditions=False)
    x1 = nc.dram_tensor("x1", [D, TLOC], F32, kind="ExternalInput").ap()
    qW = nc.dram_tensor("qW", [4, 256, 256], F32, kind="ExternalInput").ap()
    qb = nc.dram_tensor("qb", [4, 256], F32, kind="ExternalInput").ap()
    kW = nc.dram_tensor("kW", [4, 256, 256], F32, kind="ExternalInput").ap()
    kb = nc.dram_tensor("kb", [4, 256], F32, kind="ExternalInput").ap()
    vW = nc.dram_tensor("vW", [4, 256, 256], F32, kind="ExternalInput").ap()
    vb = nc.dram_tensor("vb", [4, 256], F32, kind="ExternalInput").ap()
    xn2b_o = nc.dram_tensor("xn2b_o", [D, TLOC], BF16, kind="ExternalOutput").ap()
    q_f = nc.dram_tensor("q_f", [D, TLOC], BF16, kind="ExternalOutput").ap()
    k_f = nc.dram_tensor("k_f", [D, TLOC], BF16, kind="ExternalOutput").ap()
    v_t = nc.dram_tensor("v_t", [TLOC, D], BF16, kind="ExternalOutput").ap()
    bscr = nc.dram_tensor("bscr", [128, 8], F32, kind="Internal").ap()

    with contextlib.ExitStack() as st:
        E = st.enter_context
        xb = [E(nc.sbuf_tensor(f"xb{i}", [128, TLOC], BF16)) for i in range(8)]
        sq = E(nc.sbuf_tensor("sq", [128, 512], F32))
        rows = E(nc.sbuf_tensor("rows", [1, TLOC], F32))
        row2 = E(nc.sbuf_tensor("row2", [1, TLOC], F32))
        crow = E(nc.sbuf_tensor("crow", [1, TLOC], F32))
        bc1 = E(nc.sbuf_tensor("bc1", [128, TLOC], F32))
        bc2 = E(nc.sbuf_tensor("bc2", [128, TLOC], F32))
        ones1 = E(nc.sbuf_tensor("ones1", [1, 128], F32))
        ones1c = E(nc.sbuf_tensor("ones1c", [128, 1], F32))
        onesb = E(nc.sbuf_tensor("onesb", [1, 128], BF16))
        wpos = E(nc.sbuf_tensor("wpos", [128, 2048], BF16))
        wneg = E(nc.sbuf_tensor("wneg", [128, 2048], BF16))
        bE_sb = E(nc.sbuf_tensor("bE_sb", [128, 8], F32))
        scrcol = E(nc.sbuf_tensor("scrcol", [128, 8], F32))
        bErow = E(nc.sbuf_tensor("bErow", [1, 1024], BF16))
        vout = E(nc.sbuf_tensor("vout", [128, 512], BF16))
        cpool = E(nc.sbuf_tensor("cpool", [128, 8], F32))
        psA = E(nc.psum_tensor("psA", [128, 512], F32))
        psV = E(nc.psum_tensor("psV", [128, 512], F32))
        psR1 = E(nc.psum_tensor("psR1", [1, 512], F32))
        psR2 = E(nc.psum_tensor("psR2", [1, 512], F32))
        psBC = E(nc.psum_tensor("psBC", [128, 512], F32))
        sems = make_sems(nc, E)
        block = E(nc.Block())
        st2 = contextlib.ExitStack()
        xc = [st2.enter_context(nc.sbuf_tensor(f"xc{i}", [128, TLOC], F32))
              for i in range(8)]

        p = Prog(same_engine_edges=True)
        cst = Consts(nc, p, cpool)
        cst.reg(-1.0, 1.0, 0.0, 1e-5, 1e-6, 1.0 / D)
        p.memset(ones1[:], 1.0)
        p.memset(ones1c[:], 1.0)
        p.memset(onesb[:], 1.0)
        for i in range(8):
            p.dma("sync", xc[i][:], x1[128 * i:128 * (i + 1), :])
        for c0, cw in col_chunks(TLOC):
            for i in range(8):
                p.mm(psR1[:, 0:cw], ones1c[:], xc[i][:, c0:c0 + cw],
                     start=(i == 0), stop=(i == 7))
            p.copy(rows[:, c0:c0 + cw], psR1[:, 0:cw])
            for i in range(8):
                p.act(sq[:, 0:cw], xc[i][:, c0:c0 + cw], AF.Square)
                p.mm(psR2[:, 0:cw], ones1c[:], sq[:, 0:cw],
                     start=(i == 0), stop=(i == 7))
            p.copy(row2[:, c0:c0 + cw], psR2[:, 0:cw])
        p.ts(rows[:], rows[:], 1.0 / D, op0=OP.mult)
        p.act(crow[:], rows[:], AF.Square)
        p.ts(row2[:], row2[:], 1.0 / D, op0=OP.mult)
        p.tt(crow[:], row2[:], crow[:], OP.subtract)
        p.act(crow[:], crow[:], AF.Ln, bias=1e-5)
        p.act(crow[:], crow[:], AF.Exp, scale=-0.5)
        for c0, cw in col_chunks(TLOC):
            p.mm(psBC[:, 0:cw], ones1[:], crow[:, c0:c0 + cw], start=True, stop=True)
            p.copy(bc1[:, c0:c0 + cw], psBC[:, 0:cw])
        p.tt(crow[:], rows[:], crow[:], OP.mult)
        p.ts(crow[:], crow[:], -1.0, op0=OP.mult)
        for c0, cw in col_chunks(TLOC):
            p.mm(psBC[:, 0:cw], ones1[:], crow[:, c0:c0 + cw], start=True, stop=True)
            p.copy(bc2[:, c0:c0 + cw], psBC[:, 0:cw])
        for i in range(8):
            p.tt(xc[i][:], xc[i][:], bc1[:], OP.mult)
            p.tt(xc[i][:], xc[i][:], bc2[:], OP.add)
            p.copy(xb[i][:], xc[i][:])
            p.dma("sync", xn2b_o[128 * i:128 * (i + 1), :], xb[i][:])

        st2.close()
        xbo = [E(nc.sbuf_tensor(f"xbo{i}", [128, TLOC], BF16)) for i in range(8)]
        outq = [E(nc.sbuf_tensor(f"outq{i}", [128, TLOC], F32)) for i in range(8)]
        scr0 = E(nc.sbuf_tensor("scr0", [128, TLOC], F32))
        nsq = [E(nc.sbuf_tensor(f"nsq{i}", [128, TLOC], F32)) for i in range(2)]

        def qk_one(Wd, bd, out_dram):
            emit_quat_load(p, Wd, bd, wpos, wneg, bE_sb, scrcol)
            for oc in range(8):
                o, half = oc // 2, oc % 2
                for c0, cw in col_chunks(TLOC):
                    first = True
                    for ti, (c_, w_, s_) in enumerate(QTERMS[o]):
                        src = wpos if s_ > 0 else wneg
                        for kc in range(2):
                            p.mm(psA[:, 0:cw],
                                 src[:, (w_ * 2 + kc) * 256 + 128 * half:
                                     (w_ * 2 + kc) * 256 + 128 * half + 128],
                                 xb[c_ * 2 + kc][:, c0:c0 + cw],
                                 start=first, stop=(ti == 3 and kc == 1))
                            first = False
                    p.ts(outq[oc][:, c0:c0 + cw], psA[:, 0:cw],
                         bE_sb[:, oc:oc + 1], op0=OP.add)
            for half in range(2):
                p.act(nsq[half][:], outq[half][:], AF.Square)
                for o in range(1, 4):
                    p.act(scr0[:], outq[2 * o + half][:], AF.Square)
                    p.tt(nsq[half][:], nsq[half][:], scr0[:], OP.add)
                p.act(nsq[half][:], nsq[half][:], AF.Ln, bias=1e-6)
                p.act(nsq[half][:], nsq[half][:], AF.Exp, scale=-0.5)
            for oc in range(8):
                p.tt(outq[oc][:], outq[oc][:], nsq[oc % 2][:], OP.mult)
                p.copy(xbo[oc][:], outq[oc][:])
                p.dma("sync", out_dram[128 * oc:128 * (oc + 1), :], xbo[oc][:])

        qk_one(qW, qb, q_f)
        qk_one(kW, kb, k_f)

        emit_quat_load(p, vW, vb, wpos, wneg, bE_sb, scrcol)
        p.dma("sync", bscr, bE_sb[:])
        p.dma("gpsimd",
              bErow[:].rearrange("a (o h pp) -> a o h pp", o=4, h=2),
              bscr.rearrange("p (o h) -> o h p", o=4).unsqueeze(0),
              allow_slow_non_contiguous=True)
        for tc in range(16):
            for half2 in range(2):
                first = True
                for oo in range(2):
                    o = 2 * half2 + oo
                    for (c_, w_, s_) in QTERMS[o]:
                        src = wpos if s_ > 0 else wneg
                        for kc in range(2):
                            p.mm(psV[:, 256 * oo:256 * (oo + 1)],
                                 xb[c_ * 2 + kc][:, 128 * tc:128 * (tc + 1)],
                                 src[:, (w_ * 2 + kc) * 256:(w_ * 2 + kc + 1) * 256],
                                 start=first, stop=False)
                            first = False
                p.mm(psV[:, 0:512], onesb[:], bErow[0:1, 512 * half2:512 * (half2 + 1)],
                     start=False, stop=True)
                p.copy(vout[:], psV[:, 0:512])
                p.dma("sync", v_t[128 * tc:128 * (tc + 1), 512 * half2:512 * (half2 + 1)],
                      vout[:])
        p.emit(block, sems)
    return nc


def run_quat(nc_quat, x1_fm, inp):
    in_maps = []
    for c in range(NCORE):
        in_maps.append({
            "x1": np.ascontiguousarray(x1_fm[:, c * TLOC:(c + 1) * TLOC]),
            "qW": np.asarray(inp["L1_attn_qW"]), "qb": np.asarray(inp["L1_attn_qb"]),
            "kW": np.asarray(inp["L1_attn_kW"]), "kb": np.asarray(inp["L1_attn_kb"]),
            "vW": np.asarray(inp["L1_attn_vW"]), "vb": np.asarray(inp["L1_attn_vb"]),
        })
    res = run_bass_kernel_spmd(nc_quat, in_maps, core_ids=list(range(NCORE)))
    return res.results


def build_mgq():
    nc = bass.Bass("TRN2", target_bir_lowering=False, detect_race_conditions=False)
    yT = nc.dram_tensor("yT", [DIN, TLOC], F32, kind="ExternalInput").ap()
    z_f = nc.dram_tensor("z_f", [DIN, TLOC], F32, kind="ExternalInput").ap()
    xn_f = nc.dram_tensor("xn_f", [D, TLOC], F32, kind="ExternalInput").ap()
    out_proj = nc.dram_tensor("out_proj", [DIN, D], F32, kind="ExternalInput").ap()
    xT = nc.dram_tensor("xT", [D, TLOC], F32, kind="ExternalInput").ap()
    g_f = nc.dram_tensor("g_f", [D, TLOC], F32, kind="ExternalInput").ap()
    qW = nc.dram_tensor("qW", [4, 256, 256], F32, kind="ExternalInput").ap()
    qb = nc.dram_tensor("qb", [4, 256], F32, kind="ExternalInput").ap()
    kW = nc.dram_tensor("kW", [4, 256, 256], F32, kind="ExternalInput").ap()
    kb = nc.dram_tensor("kb", [4, 256], F32, kind="ExternalInput").ap()
    vW = nc.dram_tensor("vW", [4, 256, 256], F32, kind="ExternalInput").ap()
    vb = nc.dram_tensor("vb", [4, 256], F32, kind="ExternalInput").ap()
    x1_f = nc.dram_tensor("x1_f", [D, TLOC], F32, kind="ExternalOutput").ap()
    xn2b_o = nc.dram_tensor("xn2b_o", [D, TLOC], BF16, kind="ExternalOutput").ap()
    q_f = nc.dram_tensor("q_f", [D, TLOC], BF16, kind="ExternalOutput").ap()
    k_f = nc.dram_tensor("k_f", [D, TLOC], BF16, kind="ExternalOutput").ap()
    v_t = nc.dram_tensor("v_t", [TLOC, D], BF16, kind="ExternalOutput").ap()
    bscr = nc.dram_tensor("bscr", [128, 8], F32, kind="Internal").ap()

    with contextlib.ExitStack() as st:
        E = st.enter_context
        cpool = E(nc.sbuf_tensor("cpool", [128, 8], F32))
        ones1 = E(nc.sbuf_tensor("ones1", [1, 128], F32))
        ones1c = E(nc.sbuf_tensor("ones1c", [128, 1], F32))
        ones1cb = E(nc.sbuf_tensor("ones1cb", [128, 1], BF16))
        onesb = E(nc.sbuf_tensor("onesb", [1, 128], BF16))
        ps = [E(nc.psum_tensor(f"ps{i}", [128, 512], F32)) for i in range(2)]
        psR1 = E(nc.psum_tensor("psR1", [1, 512], F32))
        psR2 = E(nc.psum_tensor("psR2", [1, 512], F32))
        psBC = E(nc.psum_tensor("psBC", [128, 512], F32))
        sems = make_sems(nc, E)
        block = E(nc.Block())
        xb = [E(nc.sbuf_tensor(f"xb{i}", [128, TLOC], BF16)) for i in range(8)]
        stX = contextlib.ExitStack()
        x1cb = [stX.enter_context(nc.sbuf_tensor(f"x1cb{i}", [128, TLOC], BF16))
                for i in range(8)]
        stA = contextlib.ExitStack()
        EA = stA.enter_context
        gb = [EA(nc.sbuf_tensor(f"gb{i}", [128, TLOC], BF16)) for i in range(16)]
        op_sb = EA(nc.sbuf_tensor("op_sb", [128, 16 * D], BF16))
        stA1 = contextlib.ExitStack()
        zt = stA1.enter_context(nc.sbuf_tensor("zt", [128, TLOC], F32))
        yt = stA1.enter_context(nc.sbuf_tensor("yt", [128, TLOC], F32))

        p = Prog(same_engine_edges=True)
        cst = Consts(nc, p, cpool)
        cst.reg(-1.0, 1.0, 0.0, 1e-5, 1e-6, 1.0 / D)
        p.memset(ones1[:], 1.0)
        p.memset(ones1c[:], 1.0)
        p.memset(ones1cb[:], 1.0)
        p.memset(onesb[:], 1.0)
        p.dma("gpsimd", op_sb[:].rearrange("p (k m) -> p k m", m=D),
              out_proj.rearrange("(k p) m -> p k m", p=128))
        for i in range(16):
            p.dma("sync", zt[:], z_f[128 * i:128 * (i + 1), :])
            p.dma("sync", yt[:], yT[128 * i:128 * (i + 1), :])
            p.act(zt[:], zt[:], AF.Silu)
            p.tt(zt[:], zt[:], yt[:], OP.mult)
            p.copy(gb[i][:], zt[:])
        for t in (zt, yt):
            p.memset(t[:, 0:1], 0.0)
        stA1.close()
        stA2 = contextlib.ExitStack()
        xnt = stA2.enter_context(nc.sbuf_tensor("xnt", [128, TLOC], F32))
        gch = stA2.enter_context(nc.sbuf_tensor("gch", [128, TLOC], F32))
        xch = stA2.enter_context(nc.sbuf_tensor("xch", [128, TLOC], F32))
        for t in (xnt, gch, xch):
            p.memset(t[:, 0:1], 0.0)
        for m in range(8):
            p.dma("sync", xnt[:], xn_f[128 * m:128 * (m + 1), :])
            p.dma("sync", gch[:], g_f[128 * m:128 * (m + 1), :])
            p.dma("sync", xch[:], xT[128 * m:128 * (m + 1), :])
            for ci, (c0, cw) in enumerate(col_chunks(TLOC)):
                pp = ps[ci % 2]
                for k in range(16):
                    p.mm(pp[:, 0:cw], op_sb[:, k * D + 128 * m:k * D + 128 * (m + 1)],
                         gb[k][:, c0:c0 + cw], start=(k == 0), stop=(k == 15))
                p.tt(xnt[:, c0:c0 + cw], pp[:, 0:cw], xnt[:, c0:c0 + cw], OP.add)
            p.act(gch[:], gch[:], AF.Silu)
            p.tt(gch[:], gch[:], xnt[:], OP.mult)
            p.tt(gch[:], gch[:], xch[:], OP.add)
            p.dma("sync", x1_f[128 * m:128 * (m + 1), :], gch[:])
            p.copy(x1cb[m][:], gch[:])
        for t in (xnt, gch, xch, op_sb, *gb):
            p.memset(t[:, 0:1], 0.0)
        stA2.close()
        stA.close()
        stB = contextlib.ExitStack()
        EB = stB.enter_context
        sq = EB(nc.sbuf_tensor("sq", [128, 512], F32))
        rows = EB(nc.sbuf_tensor("rows", [1, TLOC], F32))
        row2 = EB(nc.sbuf_tensor("row2", [1, TLOC], F32))
        crow = EB(nc.sbuf_tensor("crow", [1, TLOC], F32))
        bc1 = EB(nc.sbuf_tensor("bc1", [128, TLOC], F32))
        bc2 = EB(nc.sbuf_tensor("bc2", [128, TLOC], F32))
        scrL = EB(nc.sbuf_tensor("scrL", [128, TLOC], F32))
        for t in (sq, rows, row2, crow, bc1, bc2, scrL):
            p.memset(t[:, 0:1], 0.0)
        for c0, cw in col_chunks(TLOC):
            for i in range(8):
                p.mm(psR1[:, 0:cw], ones1cb[:], x1cb[i][:, c0:c0 + cw],
                     start=(i == 0), stop=(i == 7))
            p.copy(rows[:, c0:c0 + cw], psR1[:, 0:cw])
            for i in range(8):
                p.act(sq[:, 0:cw], x1cb[i][:, c0:c0 + cw], AF.Square)
                p.mm(psR2[:, 0:cw], ones1c[:], sq[:, 0:cw],
                     start=(i == 0), stop=(i == 7))
            p.copy(row2[:, c0:c0 + cw], psR2[:, 0:cw])
        p.ts(rows[:], rows[:], 1.0 / D, op0=OP.mult)
        p.act(crow[:], rows[:], AF.Square)
        p.ts(row2[:], row2[:], 1.0 / D, op0=OP.mult)
        p.tt(crow[:], row2[:], crow[:], OP.subtract)
        p.act(crow[:], crow[:], AF.Ln, bias=1e-5)
        p.act(crow[:], crow[:], AF.Exp, scale=-0.5)
        for c0, cw in col_chunks(TLOC):
            p.mm(psBC[:, 0:cw], ones1[:], crow[:, c0:c0 + cw], start=True, stop=True)
            p.copy(bc1[:, c0:c0 + cw], psBC[:, 0:cw])
        p.tt(crow[:], rows[:], crow[:], OP.mult)
        p.ts(crow[:], crow[:], -1.0, op0=OP.mult)
        for c0, cw in col_chunks(TLOC):
            p.mm(psBC[:, 0:cw], ones1[:], crow[:, c0:c0 + cw], start=True, stop=True)
            p.copy(bc2[:, c0:c0 + cw], psBC[:, 0:cw])
        for i in range(8):
            p.copy(scrL[:], x1cb[i][:])
            p.tt(scrL[:], scrL[:], bc1[:], OP.mult)
            p.tt(scrL[:], scrL[:], bc2[:], OP.add)
            p.copy(xb[i][:], scrL[:])
            p.dma("sync", xn2b_o[128 * i:128 * (i + 1), :], xb[i][:])
        for t in (sq, rows, row2, crow, bc1, bc2, scrL, *x1cb):
            p.memset(t[:, 0:1], 0.0)
        stB.close()
        stX.close()
        wpos = E(nc.sbuf_tensor("wpos", [128, 2048], BF16))
        wneg = E(nc.sbuf_tensor("wneg", [128, 2048], BF16))
        bE_sb = E(nc.sbuf_tensor("bE_sb", [128, 8], F32))
        scrcol = E(nc.sbuf_tensor("scrcol", [128, 8], F32))
        bErow = E(nc.sbuf_tensor("bErow", [1, 1024], BF16))
        vout = E(nc.sbuf_tensor("vout", [128, 512], BF16))
        xbo = [E(nc.sbuf_tensor(f"xbo{i}", [128, TLOC], BF16)) for i in range(8)]
        outq = [E(nc.sbuf_tensor(f"outq{i}", [128, TLOC], F32)) for i in range(8)]
        scr0 = E(nc.sbuf_tensor("scr0", [128, TLOC], F32))
        nsq = [E(nc.sbuf_tensor(f"nsq{i}", [128, TLOC], F32)) for i in range(2)]
        for t in (wpos, wneg, bE_sb, scrcol, bErow, vout, scr0, *xbo, *outq, *nsq):
            p.memset(t[:, 0:1], 0.0)
        psA, psV = ps[0], ps[1]

        def qk_one(Wd, bd, out_dram):
            emit_quat_load(p, Wd, bd, wpos, wneg, bE_sb, scrcol)
            for oc in range(8):
                o, half = oc // 2, oc % 2
                for c0, cw in col_chunks(TLOC):
                    first = True
                    for ti, (c_, w_, s_) in enumerate(QTERMS[o]):
                        src = wpos if s_ > 0 else wneg
                        for kc in range(2):
                            p.mm(psA[:, 0:cw],
                                 src[:, (w_ * 2 + kc) * 256 + 128 * half:
                                     (w_ * 2 + kc) * 256 + 128 * half + 128],
                                 xb[c_ * 2 + kc][:, c0:c0 + cw],
                                 start=first, stop=(ti == 3 and kc == 1))
                            first = False
                    p.ts(outq[oc][:, c0:c0 + cw], psA[:, 0:cw],
                         bE_sb[:, oc:oc + 1], op0=OP.add)
            for half in range(2):
                p.act(nsq[half][:], outq[half][:], AF.Square)
                for o in range(1, 4):
                    p.act(scr0[:], outq[2 * o + half][:], AF.Square)
                    p.tt(nsq[half][:], nsq[half][:], scr0[:], OP.add)
                p.act(nsq[half][:], nsq[half][:], AF.Ln, bias=1e-6)
                p.act(nsq[half][:], nsq[half][:], AF.Exp, scale=-0.5)
            for oc in range(8):
                p.tt(outq[oc][:], outq[oc][:], nsq[oc % 2][:], OP.mult)
                p.copy(xbo[oc][:], outq[oc][:])
                p.dma("sync", out_dram[128 * oc:128 * (oc + 1), :], xbo[oc][:])

        qk_one(qW, qb, q_f)
        qk_one(kW, kb, k_f)

        emit_quat_load(p, vW, vb, wpos, wneg, bE_sb, scrcol)
        p.dma("sync", bscr, bE_sb[:])
        p.dma("gpsimd",
              bErow[:].rearrange("a (o h pp) -> a o h pp", o=4, h=2),
              bscr.rearrange("p (o h) -> o h p", o=4).unsqueeze(0),
              allow_slow_non_contiguous=True)
        for tc in range(16):
            for half2 in range(2):
                first = True
                for oo in range(2):
                    o = 2 * half2 + oo
                    for (c_, w_, s_) in QTERMS[o]:
                        src = wpos if s_ > 0 else wneg
                        for kc in range(2):
                            p.mm(psV[:, 256 * oo:256 * (oo + 1)],
                                 xb[c_ * 2 + kc][:, 128 * tc:128 * (tc + 1)],
                                 src[:, (w_ * 2 + kc) * 256:(w_ * 2 + kc + 1) * 256],
                                 start=first, stop=False)
                            first = False
                p.mm(psV[:, 0:512], onesb[:], bErow[0:1, 512 * half2:512 * (half2 + 1)],
                     start=False, stop=True)
                p.copy(vout[:], psV[:, 0:512])
                p.dma("sync", v_t[128 * tc:128 * (tc + 1), 512 * half2:512 * (half2 + 1)],
                      vout[:])
        p.emit(block, sems)
    return nc


def run_mgq_cached(y_fm, pre_results, out_proj, xT, g_fm, inp):
    nc = _get("mgq", build_mgq)
    in_maps = []
    for c in range(NCORE):
        t0 = c * TLOC
        in_maps.append({
            "yT": np.ascontiguousarray(y_fm[:, t0:t0 + TLOC]),
            "z_f": np.asarray(pre_results[c]["z_f"]),
            "xn_f": np.asarray(pre_results[c]["xn_f"]),
            "out_proj": out_proj,
            "xT": np.ascontiguousarray(xT[:, t0:t0 + TLOC]),
            "g_f": np.ascontiguousarray(g_fm[:, t0:t0 + TLOC]),
            "qW": np.asarray(inp["L1_attn_qW"]), "qb": np.asarray(inp["L1_attn_qb"]),
            "kW": np.asarray(inp["L1_attn_kW"]), "kb": np.asarray(inp["L1_attn_kb"]),
            "vW": np.asarray(inp["L1_attn_vW"]), "vb": np.asarray(inp["L1_attn_vb"]),
        })
    res = run_bass_kernel_spmd(nc, in_maps, core_ids=list(range(NCORE)))
    x1_fm = np.concatenate([np.asarray(r["x1_f"]) for r in res.results], axis=1)
    return x1_fm, res.results


TBAT = 8192


def build_attn2():
    nc = bass.Bass("TRN2", target_bir_lowering=False, detect_race_conditions=False)
    q_fd = nc.dram_tensor("q_fd", [D, TLOC], BF16, kind="ExternalInput").ap()
    kf = nc.dram_tensor("kf", [D, TBAT], BF16, kind="ExternalInput").ap()
    vt = nc.dram_tensor("vt", [TBAT, D], BF16, kind="ExternalInput").ap()
    jv = nc.dram_tensor("jv", [1, 1024], F32, kind="ExternalInput").ap()
    oW = nc.dram_tensor("oW", [4, 256, 256], F32, kind="ExternalInput").ap()
    ob = nc.dram_tensor("ob", [4, 256], F32, kind="ExternalInput").ap()
    s2_f = nc.dram_tensor("s2_f", [D, TLOC], F32, kind="ExternalOutput").ap()

    with contextlib.ExitStack() as st:
        E = st.enter_context
        qsb = [E(nc.sbuf_tensor(f"qsb{i}", [128, TLOC], BF16)) for i in range(8)]
        osb = [E(nc.sbuf_tensor(f"osb{i}", [128, TLOC], BF16)) for i in range(8)]
        ksb = [E(nc.sbuf_tensor(f"ksb{i}", [64, TBAT], BF16)) for i in range(2)]
        vsb = [E(nc.sbuf_tensor(f"vsb{i}", [128, 64 * 65], BF16)) for i in range(2)]
        qh = [E(nc.sbuf_tensor(f"qh{i}", [64, TLOC], BF16)) for i in range(2)]
        ptb = [E(nc.sbuf_tensor(f"ptb{i}", [128, 1024], BF16)) for i in range(2)]
        jrow = E(nc.sbuf_tensor("jrow", [1, 1024], F32))
        jcol = E(nc.sbuf_tensor("jcol", [1, 1], F32))
        jb = E(nc.sbuf_tensor("jb", [128, 1], F32))
        rl = E(nc.sbuf_tensor("rl", [1, 512], F32))
        rlb = E(nc.sbuf_tensor("rlb", [1, 512], BF16))
        rb_sb = E(nc.sbuf_tensor("rb_sb", [64, 512], F32))
        ones1 = E(nc.sbuf_tensor("ones1", [1, 128], F32))
        onesb = E(nc.sbuf_tensor("onesb", [1, 128], BF16))
        wpos = E(nc.sbuf_tensor("wpos", [128, 2048], BF16))
        wneg = E(nc.sbuf_tensor("wneg", [128, 2048], BF16))
        bE_sb = E(nc.sbuf_tensor("bE_sb", [128, 8], F32))
        scrcol = E(nc.sbuf_tensor("scrcol", [128, 8], F32))
        scrF = E(nc.sbuf_tensor("scrF", [128, TLOC], F32))
        cpool = E(nc.sbuf_tensor("cpool", [128, 8], F32))
        psS = [E(nc.psum_tensor(f"psS{i}", [128, 1024], F32)) for i in range(2)]
        psO = E(nc.psum_tensor("psO", [65, 512], F32))
        psB = E(nc.psum_tensor("psB", [128, 512], F32))
        sems = make_sems(nc, E)
        block = E(nc.Block())

        p = Prog(same_engine_edges=True)
        cst = Consts(nc, p, cpool)
        cst.reg(-1.0, 1.0, 0.0, 0.125, 0.1 / 1024.0)
        p.memset(ones1[:], 1.0)
        p.memset(onesb[:], 1.0)
        for i in range(8):
            p.dma("sync", qsb[i][:], q_fd[128 * i:128 * (i + 1), :])
        p.dma("sync", jrow[:], jv)
        p.act(jrow[:], jrow[:], AF.Sin)
        p.red(jcol[:], jrow[:], OP.add)
        p.ts(jcol[:], jcol[:], 0.1 / 1024.0, op0=OP.mult)
        p.mm(psB[:, 0:1], ones1[:], jcol[:], start=True, stop=True)
        p.copy(jb[:], psB[:, 0:1])

        for h in range(16):
            par = h % 2
            ks_, vs_, qh_ = ksb[par], vsb[par], qh[par]
            p.dma("sync", ks_[:], kf[64 * h:64 * (h + 1), :])
            p.memset(vs_[:].rearrange("p (kc c) -> p kc c", c=65)[:, :, 64:65], 1.0)
            p.dma("sync", vs_[:].rearrange("p (kc c) -> p kc c", c=65)[:, :, 0:64],
                  vt[:, 64 * h:64 * (h + 1)].rearrange("(kc p) m -> p kc m", p=128))
            hi, hr = h // 2, 64 * (h % 2)
            p.copy(qh_[:], qsb[hi][hr:hr + 64, :])
            for qc in range(4):
                for kk in range(32):
                    ps = psS[kk % 2]
                    pt = ptb[kk % 2]
                    for j in range(2):
                        kc = 2 * kk + j
                        p.mm(ps[:, 512 * j:512 * (j + 1)],
                             ks_[0:64, 128 * kc:128 * (kc + 1)],
                             qh_[:, 512 * qc:512 * (qc + 1)],
                             start=True, stop=True)
                    p.act(pt[:], ps[:], AF.Exp, scale=0.125, bias=jb[:])
                    for j in range(2):
                        kc = 2 * kk + j
                        p.mm(psO[:, 0:512],
                             vs_[:, 65 * kc:65 * (kc + 1)],
                             pt[:, 512 * j:512 * (j + 1)],
                             start=(kk == 0 and j == 0),
                             stop=(kk == 31 and j == 1))
                p.recip(rl[:], psO[64:65, 0:512])
                p.copy(rlb[:], rl[:])
                p.mm(psB[0:64, 0:512], onesb[0:1, 0:64], rlb[0:1, :],
                     start=True, stop=True)
                p.act(rb_sb[:], psB[0:64, 0:512], AF.Copy)
                p.tt(osb[hi][hr:hr + 64, 512 * qc:512 * (qc + 1)],
                     psO[0:64, 0:512], rb_sb[:], OP.mult)

        emit_quat_load(p, oW, ob, wpos, wneg, bE_sb, scrcol)
        for oc in range(8):
            o, half = oc // 2, oc % 2
            for c0, cw in col_chunks(TLOC):
                ps = psS[(c0 // 512) % 2]
                first = True
                for ti, (c_, w_, s_) in enumerate(QTERMS[o]):
                    src = wpos if s_ > 0 else wneg
                    for kc in range(2):
                        p.mm(ps[:, 0:cw],
                             src[:, (w_ * 2 + kc) * 256 + 128 * half:
                                 (w_ * 2 + kc) * 256 + 128 * half + 128],
                             osb[c_ * 2 + kc][:, c0:c0 + cw],
                             start=first, stop=(ti == 3 and kc == 1))
                        first = False
                p.ts(scrF[:, c0:c0 + cw], ps[:, 0:cw], bE_sb[:, oc:oc + 1],
                     op0=OP.add)
            p.dma("sync", s2_f[128 * oc:128 * (oc + 1), :], scrF[:])
        p.emit(block, sems)
    return nc


def run_attn2(nc_a2, quat_results, inp):
    kf_b, vt_b = [], []
    for b in range(2):
        kf_b.append(np.concatenate(
            [np.asarray(quat_results[4 * b + i]["k_f"]) for i in range(4)], axis=1))
        vt_b.append(np.concatenate(
            [np.asarray(quat_results[4 * b + i]["v_t"]) for i in range(4)], axis=0))
    in_maps = []
    for c in range(NCORE):
        b = c // 4
        in_maps.append({
            "q_fd": np.asarray(quat_results[c]["q_f"]),
            "kf": np.ascontiguousarray(kf_b[b]),
            "vt": np.ascontiguousarray(vt_b[b]),
            "jv": np.asarray(inp["L1_attn_J"])[None, :],
            "oW": np.asarray(inp["L1_attn_oW"]), "ob": np.asarray(inp["L1_attn_ob"]),
        })
    res = run_bass_kernel_spmd(nc_a2, in_maps, core_ids=list(range(NCORE)))
    return [np.asarray(r["s2_f"]) for r in res.results]


NEXP = 8


def build_moe():
    nc = bass.Bass("TRN2", target_bir_lowering=False, detect_race_conditions=False)
    xn2b = nc.dram_tensor("xn2b", [D, TLOC], BF16, kind="ExternalInput").ap()
    x1T = nc.dram_tensor("x1T", [TLOC, D], F32, kind="ExternalInput").ap()
    s2T = nc.dram_tensor("s2T", [TLOC, D], F32, kind="ExternalInput").ap()
    wr = nc.dram_tensor("wr", [D, NEXP], F32, kind="ExternalInput").ap()
    br = nc.dram_tensor("br", [NEXP, 1], F32, kind="ExternalInput").ap()
    w1 = nc.dram_tensor("w1", [NEXP, D, 2048], F32, kind="ExternalInput").ap()
    b1 = nc.dram_tensor("b1", [NEXP, 2048], F32, kind="ExternalInput").ap()
    w2 = nc.dram_tensor("w2", [NEXP, 2048, D], F32, kind="ExternalInput").ap()
    b2 = nc.dram_tensor("b2", [NEXP, D], F32, kind="ExternalInput").ap()
    ident = nc.dram_tensor("ident", [128, 128], F32, kind="ExternalInput").ap()
    out_t = nc.dram_tensor("out_t", [TLOC, D], F32, kind="ExternalOutput").ap()

    with contextlib.ExitStack() as st:
        E = st.enter_context
        xn = [E(nc.sbuf_tensor(f"xn{i}", [128, TLOC], BF16)) for i in range(8)]
        hb = [E(nc.sbuf_tensor(f"hb{i}", [128, TLOC], BF16)) for i in range(16)]
        acc = [E(nc.sbuf_tensor(f"acc{i}", [128, D], F32)) for i in range(16)]
        w2all = E(nc.sbuf_tensor("w2all", [128, 16 * D], BF16))
        w1st = [E(nc.sbuf_tensor(f"w1st{i}", [128, 8 * 128], BF16)) for i in range(2)]
        wr_sb = E(nc.sbuf_tensor("wr_sb", [128, 64], BF16))
        br_sb = E(nc.sbuf_tensor("br_sb", [8, 1], F32))
        b1E = E(nc.sbuf_tensor("b1E", [128, 16], F32))
        b2row = E(nc.sbuf_tensor("b2row", [1, D], BF16))
        lg_sb = E(nc.sbuf_tensor("lg_sb", [8, TLOC], BF16))
        idbB = E(nc.sbuf_tensor("idbB", [128, 128], BF16))
        lgT = E(nc.sbuf_tensor("lgT", [128, 8], F32))
        gate = E(nc.sbuf_tensor("gate", [128, 128], F32))
        m1c = E(nc.sbuf_tensor("m1c", [128, 1], F32))
        m2c = E(nc.sbuf_tensor("m2c", [128, 1], F32))
        w1c = E(nc.sbuf_tensor("w1c", [128, 1], F32))
        w2c = E(nc.sbuf_tensor("w2c", [128, 1], F32))
        eq1 = E(nc.sbuf_tensor("eq1", [128, 8], F32))
        eq2 = E(nc.sbuf_tensor("eq2", [128, 8], F32))
        mskd = E(nc.sbuf_tensor("mskd", [128, 8], F32))
        tmp = [E(nc.sbuf_tensor(f"tmp{i}", [128, 512], F32)) for i in range(2)]
        onesb = E(nc.sbuf_tensor("onesb", [1, 128], BF16))
        cpool = E(nc.sbuf_tensor("cpool", [128, 4], F32))
        psA = [E(nc.psum_tensor(f"psA{i}", [128, 512], F32)) for i in range(2)]
        psLG = E(nc.psum_tensor("psLG", [8, TLOC], F32))
        psTR = E(nc.psum_tensor("psTR", [128, 8], BF16))
        sems = make_sems(nc, E)
        block = E(nc.Block())

        p = Prog(same_engine_edges=True)
        cst = Consts(nc, p, cpool)
        cst.reg(-1.0, 1.0, 0.0, NEG)
        p.memset(onesb[:], 1.0)
        for i in range(8):
            p.dma("sync", xn[i][:], xn2b[128 * i:128 * (i + 1), :])
        p.dma("gpsimd", wr_sb[:].rearrange("p (k m) -> p k m", m=8),
              wr.rearrange("(k p) m -> p k m", p=128))
        p.dma("sync", br_sb[:], br)
        p.dma("gpsimd", idbB[:], ident)
        for c0, cw in col_chunks(TLOC):
            for k in range(8):
                p.mm(psLG[:, c0:c0 + cw], wr_sb[:, 8 * k:8 * (k + 1)],
                     xn[k][:, c0:c0 + cw], start=(k == 0), stop=(k == 7))
        p.ts(lg_sb[:], psLG[:], br_sb[:], op0=OP.add)
        for tc in range(16):
            p.transpose(psTR[:], lg_sb[0:8, 128 * tc:128 * (tc + 1)], idbB[0:8, 0:8])
            p.copy(lgT[:], psTR[:])
            p.red(m1c[:], lgT[:], OP.max)
            p.ts(eq1[:], lgT[:], m1c[:], op0=OP.is_equal)
            p.stt(mskd[:], eq1[:], NEG, lgT[:], OP.mult, OP.add)
            p.red(m2c[:], mskd[:], OP.max)
            p.tt(w1c[:], m1c[:], m2c[:], OP.subtract)
            p.act(w1c[:], w1c[:], AF.Sigmoid)
            p.ts(w2c[:], w1c[:], -1.0, 1.0, op0=OP.mult, op1=OP.add)
            p.ts(eq2[:], lgT[:], m2c[:], op0=OP.is_equal)
            p.ts(eq1[:], eq1[:], w1c[:], op0=OP.mult)
            p.ts(eq2[:], eq2[:], w2c[:], op0=OP.mult)
            p.tt(gate[:, 8 * tc:8 * (tc + 1)], eq1[:], eq2[:], OP.add)
        for e in range(NEXP):
            p.dma("sync", b1E[:].rearrange("p (i) -> p i"),
                  b1[e].rearrange("(i p) -> p i", p=128),
                  allow_slow_non_contiguous=True)
            p.dma("gpsimd", b2row[:], b2[e:e + 1, :])
            for mc in range(16):
                ws_ = w1st[mc % 2]
                p.dma("gpsimd", ws_[:].rearrange("p (k m) -> p k m", m=128),
                      w1[e, :, 128 * mc:128 * (mc + 1)].rearrange(
                          "(k p) m -> p k m", p=128))
                p.dma("gpsimd", w2all[:, D * mc:D * (mc + 1)],
                      w2[e, 128 * mc:128 * (mc + 1), :])
                for ci, (c0, cw) in enumerate(col_chunks(TLOC)):
                    ps = psA[ci % 2]
                    for k in range(8):
                        p.mm(ps[:, 0:cw], ws_[:, 128 * k:128 * (k + 1)],
                             xn[k][:, c0:c0 + cw], start=(k == 0), stop=(k == 7))
                    p.act(hb[mc][:, c0:c0 + cw], ps[:, 0:cw], AF.Relu,
                          bias=b1E[:, mc:mc + 1])
            for tc in range(16):
                gcol = gate[:, 8 * tc + e:8 * tc + e + 1]
                for half in range(2):
                    ps = psA[(2 * tc + half) % 2]
                    tm_ = tmp[(2 * tc + half) % 2]
                    for mc in range(16):
                        p.mm(ps[:, 0:512], hb[mc][:, 128 * tc:128 * (tc + 1)],
                             w2all[:, D * mc + 512 * half:D * mc + 512 * (half + 1)],
                             start=(mc == 0), stop=False)
                    p.mm(ps[:, 0:512], onesb[:],
                         b2row[0:1, 512 * half:512 * (half + 1)],
                         start=False, stop=True)
                    if e == 0:
                        p.ts(acc[tc][:, 512 * half:512 * (half + 1)], ps[:, 0:512],
                             gcol, op0=OP.mult)
                    else:
                        p.ts(tm_[:], ps[:, 0:512], gcol, op0=OP.mult)
                        p.tt(acc[tc][:, 512 * half:512 * (half + 1)],
                             acc[tc][:, 512 * half:512 * (half + 1)], tm_[:], OP.add)
        for tc in range(16):
            for half in range(2):
                hs = slice(512 * half, 512 * (half + 1))
                tm_ = tmp[half]
                sc_ = tmp[1 - half]
                p.act(sc_[:], acc[tc][:, hs], AF.Silu)
                p.dma("sync", tm_[:], s2T[128 * tc:128 * (tc + 1), hs])
                p.tt(sc_[:], sc_[:], tm_[:], OP.mult)
                p.dma("sync", tm_[:], x1T[128 * tc:128 * (tc + 1), hs])
                p.tt(sc_[:], sc_[:], tm_[:], OP.add)
                p.dma("sync", out_t[128 * tc:128 * (tc + 1), hs], sc_[:])
        p.emit(block, sems)
    return nc


def run_moe(nc_moe, quat_results, x1_fm, s2_list, inp):
    in_maps = []
    for c in range(NCORE):
        t0 = c * TLOC
        in_maps.append({
            "xn2b": np.asarray(quat_results[c]["xn2b_o"]),
            "x1T": np.ascontiguousarray(x1_fm[:, t0:t0 + TLOC].T),
            "s2T": np.ascontiguousarray(np.asarray(s2_list[c]).T),
            "wr": np.asarray(inp["L1_ffn_Wr"]),
            "br": np.asarray(inp["L1_ffn_br"])[:, None],
            "w1": np.asarray(inp["L1_ffn_W1"]), "b1": np.asarray(inp["L1_ffn_b1"]),
            "w2": np.asarray(inp["L1_ffn_W2"]), "b2": np.asarray(inp["L1_ffn_b2"]),
            "ident": np.eye(128, dtype=np.float32),
        })
    res = run_bass_kernel_spmd(nc_moe, in_maps, core_ids=list(range(NCORE)))
    return [np.asarray(r["out_t"]) for r in res.results]


_NC_CACHE = {}


def _get(name, builder):
    if name not in _NC_CACHE:
        _NC_CACHE[name] = builder()
    return _NC_CACHE[name]


def kernel(**inp):
    x = np.asarray(inp["L0_attn_x"], np.float32)
    xT = np.ascontiguousarray(x.reshape(16384, D).T)

    wm = {k: np.asarray(inp[f"L0_attn_{k}"], np.float32) for k in
          ["rms_w", "in_proj", "conv_w", "conv_b", "x_proj", "dt_w", "dt_b"]}
    nc_pre = _get("pre", build_mamba_pre)
    xhs = make_xh(xT)
    in_maps = [{"xh": xhs[c], **wm} for c in range(NCORE)]
    pre = run_bass_kernel_spmd(nc_pre, in_maps, core_ids=list(range(NCORE))).results

    y_fm = run_scan_cached(pre, np.asarray(inp["L0_attn_A_log"], np.float32),
                           np.asarray(inp["L0_attn_D"], np.float32))

    x_fm = np.concatenate([np.asarray(r["xn_f"]) for r in pre], axis=1)
    consts = [attn_consts(c) for c in range(NCORE)]
    nc_attn = _get("attn", build_attn)
    nc_ffn = _get("ffn", build_ffn)
    for i in range(6):
        x_fm = run_attn(nc_attn, x_fm,
                        np.asarray(inp["L0_ffn_Wqkv"][i], np.float32),
                        np.asarray(inp["L0_ffn_q_scale"][i], np.float32),
                        np.asarray(inp["L0_ffn_k_scale"][i], np.float32),
                        np.asarray(inp["L0_ffn_Wo"][i], np.float32), consts)
        x_fm = run_ffn(nc_ffn, x_fm,
                       np.asarray(inp["L0_ffn_W1"][i], np.float32),
                       np.asarray(inp["L0_ffn_b1"][i], np.float32),
                       np.asarray(inp["L0_ffn_W2"][i], np.float32),
                       np.asarray(inp["L0_ffn_b2"][i], np.float32),
                       np.asarray(inp["L0_ffn_ln_g"][i], np.float32),
                       np.asarray(inp["L0_ffn_ln_b"][i], np.float32))

    x1_fm, quat = run_mgq_cached(y_fm, pre,
                                 np.asarray(inp["L0_attn_out_proj"], np.float32),
                                 xT, x_fm, inp)

    nc_a2 = _get("a2", build_attn2)
    s2 = run_attn2(nc_a2, quat, inp)
    nc_moe = _get("moe", build_moe)
    outs = run_moe(nc_moe, quat, x1_fm, s2, inp)
    out = np.concatenate(outs, axis=0).reshape(2, 8192, D).astype(np.float32)
    return out


def run_scan_cached(pre_results, a_log_full, d_full):
    delta_fm = np.concatenate([np.asarray(r["delta_f"]) for r in pre_results], axis=1)
    xi_fm = np.concatenate([np.asarray(r["xi_f"]) for r in pre_results], axis=1)
    bTf = np.concatenate([np.asarray(r["bT"]) for r in pre_results], axis=1)
    cTf = np.concatenate([np.asarray(r["cT"]) for r in pre_results], axis=1)
    nc = _get("scan", build_scan)
    in_maps = []
    for c in range(NCORE):
        r0 = c * 256
        in_maps.append({
            "deltaS": np.ascontiguousarray(delta_fm[r0:r0 + 256]),
            "xiS": np.ascontiguousarray(xi_fm[r0:r0 + 256]),
            "bTf": bTf, "cTf": cTf,
            "a_log": np.ascontiguousarray(a_log_full[r0:r0 + 256]),
            "d_vec": np.ascontiguousarray(d_full[r0:r0 + 256]),
        })
    res = run_bass_kernel_spmd(nc, in_maps, core_ids=list(range(NCORE)))
    return np.concatenate([np.asarray(r["yS"]) for r in res.results], axis=0)


def run_mgate_cached(y_fm, pre_results, out_proj, xT, g_fm):
    nc = _get("mgate", build_mgate)
    in_maps = []
    for c in range(NCORE):
        t0 = c * TLOC
        in_maps.append({
            "yT": np.ascontiguousarray(y_fm[:, t0:t0 + TLOC]),
            "z_f": np.asarray(pre_results[c]["z_f"]),
            "xn_f": np.asarray(pre_results[c]["xn_f"]),
            "out_proj": out_proj,
            "xT": np.ascontiguousarray(xT[:, t0:t0 + TLOC]),
            "g_f": np.ascontiguousarray(g_fm[:, t0:t0 + TLOC]),
        })
    res = run_bass_kernel_spmd(nc, in_maps, core_ids=list(range(NCORE)))
    return np.concatenate([np.asarray(r["x1_f"]) for r in res.results], axis=1)
